# version 05
# baseline (speedup 1.0000x reference)
import numpy as np
import ml_dtypes
import concourse.bass as bass
import concourse.mybir as mybir
from concourse.bass_utils import run_bass_kernel_spmd
from contextlib import ExitStack

F32 = mybir.dt.float32
BF16 = mybir.dt.bfloat16
AF = mybir.ActivationFunctionType
ALU = mybir.AluOpType
AX = mybir.AxisListType

B, L, D = 2, 8192, 2048
NC, Q = 8, 4
T = L // Q
NH, HD = 16, 128
NT = T // HD
MOE_F, NU = 1408, 12
NFC = MOE_F // HD
FFN_H = 8192
EPS = 1e-5
bf16 = ml_dtypes.bfloat16

ENGS = ("g", "s", "t", "v", "c")


class Prog:
    def __init__(self):
        self.ops = {e: [] for e in ENGS}

    def add(self, key, fn):
        self.ops[key].append(fn)


class Sem:
    def __init__(self, ctx, nc, name):
        self.h = ctx.enter_context(nc.semaphore(name))
        self.v = 0


class Pool:

    def __init__(self, tiles):
        self.tiles = tiles
        self.guards = [[] for _ in tiles]
        self.i = 0

    def next(self):
        idx = self.i
        self.i = (self.i + 1) % len(self.tiles)
        g = self.guards[idx]
        self.guards[idx] = []
        return idx, self.tiles[idx], g

    def guard(self, idx, sem, val):
        self.guards[idx].append((sem.h, val))


def build_launch1(phases="ABC"):
    nc = bass.Bass("TRN2", target_bir_lowering=False)
    dt = nc.dram_tensor

    xTpre = dt("xTpre", [D, 3 * T], F32, kind="ExternalInput")
    xTown = dt("xTown", [D, T], F32, kind="ExternalInput")
    rp = dt("rp", [HD, NH], F32, kind="ExternalInput")
    igw = dt("igw", [NH, HD, HD], F32, kind="ExternalInput")
    igb = dt("igb", [HD, NH], F32, kind="ExternalInput")
    rgw = dt("rgw", [NH, HD, HD], F32, kind="ExternalInput")
    rgb = dt("rgb", [HD, NH], F32, kind="ExternalInput")
    msel = dt("msel", [HD, 5], F32, kind="ExternalInput")
    wr9 = dt("wr9", [D, 9], F32, kind="ExternalInput")
    wg_all = dt("wg_all", [NU, D, MOE_F], F32, kind="ExternalInput")
    wu_all = dt("wu_all", [NU, D, MOE_F], F32, kind="ExternalInput")
    wd_all = dt("wd_all", [NU, MOE_F, D], F32, kind="ExternalInput")
    wqk = dt("wqk", [D, D], F32, kind="ExternalInput")
    bqk = dt("bqk", [HD, NH], F32, kind="ExternalInput")
    wv = dt("wv", [D, D], F32, kind="ExternalInput")
    bv_row = dt("bv_row", [1, D], F32, kind="ExternalInput")
    wu1 = dt("wu1", [D, D], F32, kind="ExternalInput")
    bu = dt("bu", [HD, NH], F32, kind="ExternalInput")
    ident = dt("ident", [HD, HD], F32, kind="ExternalInput")
    ones_in = dt("ones_in", [1, HD], F32, kind="ExternalInput")
    w2rr = dt("w2rr", [64, HD * D], F32, kind="ExternalInput")
    h64 = dt("h64", [64, 64], F32, kind="ExternalInput")
    h128 = dt("h128", [HD, HD], F32, kind="ExternalInput")

    x2_out = dt("x2_out", [T, D], F32, kind="ExternalOutput")
    w2h_out = dt("w2h_out", [FFN_H, D], BF16, kind="ExternalOutput")
    q_out = dt("q_out", [D, T], BF16, kind="ExternalOutput")
    u_out = dt("u_out", [D, T], BF16, kind="ExternalOutput")
    kv_out = dt("kv_out", [HD, NH * HD], F32, kind="ExternalOutput")

    x1f = dt("x1f", [T, D], BF16, kind="Internal")
    w2s = dt("w2s", [64, HD * D], BF16, kind="Internal")
    moe_d = dt("moe_d", [T, D], F32, kind="Internal")

    P = Prog()
    ctx = ExitStack()
    with ctx:
        mm = Sem(ctx, nc, "mm")
        ve = Sem(ctx, nc, "ve")
        es = Sem(ctx, nc, "es")
        gp = Sem(ctx, nc, "gp")
        Lconst = Sem(ctx, nc, "Lconst")
        Lxtb = [Sem(ctx, nc, "Lxtb0"), Sem(ctx, nc, "Lxtb1")]
        Lxtf = [Sem(ctx, nc, "Lxtf0"), Sem(ctx, nc, "Lxtf1")]
        Lwd = Sem(ctx, nc, "Lwd")
        Lwg = [Sem(ctx, nc, "Lwg0"), Sem(ctx, nc, "Lwg1")]
        Lacc = [Sem(ctx, nc, "Lacc0"), Sem(ctx, nc, "Lacc1")]
        Lwq = [Sem(ctx, nc, "Lwq0"), Sem(ctx, nc, "Lwq1")]
        Lwvh = Sem(ctx, nc, "Lwvh")
        Lsp = Sem(ctx, nc, "Lsp")
        Lx2o = Sem(ctx, nc, "Lx2o")
        Lrl = [Sem(ctx, nc, "Lrl0"), Sem(ctx, nc, "Lrl1")]
        Lqout = Sem(ctx, nc, "Lqout")
        Lkv = Sem(ctx, nc, "Lkv")
        Lw2 = [Sem(ctx, nc, "Lw2_0"), Sem(ctx, nc, "Lw2_1")]
        Lw2s = Sem(ctx, nc, "Lw2s")
        Lu1 = [Sem(ctx, nc, "Lu1_0"), Sem(ctx, nc, "Lu1_1")]
        Lw2h = [Sem(ctx, nc, "Lw2h0"), Sem(ctx, nc, "Lw2h1")]

        def sb(name, shape, dtp):
            return nc.alloc_sbuf_tensor(name, shape, dtp)

        nsp = sb("nsp", [HD, NH], F32)
        sp_t1 = sb("sp_t1", [HD, NH], F32)
        igb_s = sb("igb_s", [HD, NH], F32)
        rgb_s = sb("rgb_s", [HD, NH], F32)
        msel_s = sb("msel_s", [HD, 5], F32)
        bqk_s = sb("bqk_s", [HD, NH], F32)
        bu_s = sb("bu_s", [HD, NH], F32)
        igw_b = sb("igw_b", [HD, NH, HD], BF16)
        rgw_b = sb("rgw_b", [HD, NH, HD], BF16)
        idf = sb("idf", [HD, HD], F32)
        idb = sb("idb", [HD, HD], BF16)
        onesb = sb("onesb", [1, HD], BF16)
        wr9_b = sb("wr9_b", [HD, NH, 9], BF16)
        ej = sb("ej", [HD, NH * 3], F32)
        Aj = sb("Aj", [HD, NH * 3], F32)
        carry = sb("carry", [HD, NH], F32)
        cc1 = sb("cc1", [HD, 1], F32)
        cc2 = sb("cc2", [HD, 1], F32)
        cc3 = sb("cc3", [HD, 1], F32)
        m0fix = sb("m0fix", [HD, 1], F32)
        s1_all = sb("s1_all", [HD, NT], F32)
        s2_all = sb("s2_all", [HD, NT], F32)
        mu_all = sb("mu_all", [HD, NT], F32)
        rs_all = sb("rs_all", [HD, NT], F32)
        scl = sb("scl", [HD, NT * NU], F32)
        prob = sb("prob", [HD, 8], F32)
        h64b = sb("h64b", [64, 64], BF16)
        h128b = sb("h128b", [HD, HD], BF16)
        hsc_c = sb("hsc_c", [HD, 1], F32)
        pmsk = sb("pmsk", [HD, 8], F32)
        pstat = sb("pstat", [HD, 4], F32)
        eps_c = sb("eps_c", [HD, 1], F32)

        R0h = nc.alloc_sbuf_tensor("R0", [HD, 32768], F32)
        R1h = nc.alloc_sbuf_tensor("R1", [HD, 16384], F32)
        A0 = nc.lookup_mloc(R0h).addr
        A1 = nc.lookup_mloc(R1h).addr

        def carve(name, shape, dtp, base, off):
            return nc.alloc_sbuf_tensor_at(name, shape, dtp, offset=base + off)

        x1tm_h = carve("x1tm", [HD, NT, D], BF16, A0, 0)
        sq_scrA = carve("sq_scrA", [HD, D], F32, A0, 65536)
        xt_f = [carve("xtf0", [HD, T], F32, A1, 0),
                carve("xtf1", [HD, T], F32, A1, 8192)]
        xt_b = [carve("xtb0", [HD, T], BF16, A1, 16384),
                carve("xtb1", [HD, T], BF16, A1, 20480)]
        B1p = [carve("B1a", [HD, T], F32, A1, 24576),
               carve("B1b", [HD, T], F32, A1, 49152)]
        B2p = [carve("B2a", [HD, T], F32, A1, 32768),
               carve("B2b", [HD, T], F32, A1, 57344)]
        B3 = carve("B3", [HD, T], F32, A1, 40960)
        w2rr_b = carve("w2rr_b", [64, 2, 8192], BF16, A0, 73728)
        stg2 = carve("stg2", [64, 8192], BF16, A0, 106496)
        u1t = carve("u1t", [HD, 2, 1024], BF16, A0, 122880)
        w2h_st = carve("w2h_st", [HD, 2, 1024], BF16, A0, 126976)
        h_e = carve("h_e", [HD, NFC, D], BF16, A0, 0)
        wd_b = carve("wd_b", [HD, NFC, D], BF16, A0, 45056)
        wfc = carve("wfc", [HD, 2, 2 * NH, HD], BF16, A0, 90112)
        stage = carve("stage", [HD, 2, D], F32, A0, 106496)
        sq_scrB = carve("sq_scrB", [HD, D], F32, A0, 122880)
        x2t_st = carve("x2t_st", [HD, D], F32, A0, 0)
        x1rb = carve("x1rb", [HD, D], BF16, A0, 114688)
        x1T_h = carve("x1T", [HD, NH, T], BF16, A1, 0)
        kh_h = carve("kh", [HD, NT, D], BF16, A0, 0)
        wvh = carve("wvh", [HD, NH, 512], BF16, A0, 65536)
        wqk_t = carve("wqk_t", [HD, 2, NH, HD], BF16, A0, 81920)
        vtt = carve("vtt", [HD, 512], BF16, A0, 90112)
        elu_r = carve("elu_r", [HD, 512], F32, A0, 91136)
        elu_t = carve("elu_t", [HD, 512], F32, A0, 93184)
        qcm_t = carve("qcm_t", [HD, T], BF16, A0, 95232)
        bvr_b = carve("bvr_b", [1, D], BF16, A0, 99328)
        kv_sb = carve("kv_sb", [HD, NH * HD], F32, A0, 103424)

        pp_t = [nc.alloc_psum_tensor(f"pp{i}", [HD, 512], F32) for i in range(4)]
        pp = Pool(pp_t)
        ptf_t = [nc.alloc_psum_tensor(f"ptf{i}", [HD, HD], F32) for i in range(2)]
        ptb_t = [nc.alloc_psum_tensor(f"ptb{i}", [HD, 1024], BF16) for i in range(2)]
        ptf = Pool(ptf_t)
        ptb = Pool(ptb_t)

        def W(key, sem, val=None):
            v = sem.v if val is None else val
            P.add(key, lambda e, h=sem.h, v=v: e.wait_ge(h, v))

        def WH(key, hv_list):
            for (h, v) in hv_list:
                P.add(key, lambda e, h=h, v=v: e.wait_ge(h, v))

        def dma(lane, key, out, in_, accum=False):
            lane.v += 16
            def f(e, out=out, in_=in_, accum=accum, h=lane.h):
                if accum:
                    e.dma_start(out=out, in_=in_, accum_op=ALU.add).then_inc(h, 16)
                else:
                    e.dma_start(out=out, in_=in_).then_inc(h, 16)
            P.add(key, f)
            return lane.v

        def SC(fn):
            es.v += 1
            P.add("c", lambda e, fn=fn, h=es.h: fn(e).then_inc(h, 1))
            return es.v

        def VE(fn):
            ve.v += 1
            P.add("v", lambda e, fn=fn, h=ve.h: fn(e).then_inc(h, 1))
            return ve.v

        def TE(fn):
            mm.v += 1
            P.add("t", lambda e, fn=fn, h=mm.h: fn(e).then_inc(h, 1))
            return mm.v

        def TN(fn):
            P.add("t", fn)

        def GP(fn):
            gp.v += 1
            P.add("g", lambda e, fn=fn, h=gp.h: fn(e).then_inc(h, 1))
            return gp.v

        dma(Lconst, "g", nsp[:, :], rp[:, :])
        dma(Lconst, "g", igb_s[:, :], igb[:, :])
        dma(Lconst, "g", rgb_s[:, :], rgb[:, :])
        dma(Lconst, "g", msel_s[:, :], msel[:, :])
        dma(Lconst, "g", bqk_s[:, :], bqk[:, :])
        dma(Lconst, "g", bu_s[:, :], bu[:, :])
        dma(Lconst, "g", igw_b[:, :, :], igw.ap().transpose([1, 0, 2]))
        dma(Lconst, "g", rgw_b[:, :, :], rgw.ap().transpose([1, 0, 2]))
        dma(Lconst, "g", idf[:, :], ident[:, :])
        dma(Lconst, "g", idb[:, :], ident[:, :])
        dma(Lconst, "g", onesb[:, :], ones_in[:, :])
        dma(Lconst, "g", h64b[:, :], h64[:, :])
        dma(Lconst, "g", h128b[:, :], h128[:, :])
        c0 = dma(Lconst, "g", wr9_b[:, :, :], wr9.ap().rearrange("(t p) n -> p t n", p=HD))

        VE(lambda v: v.memset(eps_c[:, :], EPS))
        W("v", ve)
        VE(lambda v: v.memset(hsc_c[:, :], 1.0 / float(np.sqrt(FFN_H))))
        P.add("t", lambda e, h=Lconst.h, v=Lconst.v: e.wait_ge(h, v))
        W("c", Lconst, c0)
        SC(lambda s: s.activation(sp_t1[:, :], nsp[:, :], AF.Abs))
        W("c", es)
        SC(lambda s: s.activation(sp_t1[:, :], sp_t1[:, :], AF.Exp, scale=-1.0))
        W("c", es)
        SC(lambda s: s.activation(sp_t1[:, :], sp_t1[:, :], AF.Ln, bias=1.0))
        W("c", es)
        SC(lambda s: s.activation(nsp[:, :], nsp[:, :], AF.Relu))
        W("v", es)
        VE(lambda v: v.tensor_add(nsp[:, :], nsp[:, :], sp_t1[:, :]))
        W("v", ve)
        VE(lambda v: v.tensor_scalar_mul(nsp[:, :], nsp[:, :], -8.0))
        nsp_done = ve.v

        w2v = w2rr.ap()
        w2sv = w2s.ap().rearrange("a (b d) -> a b d", d=D)
        prep = {"wg2": [[], []], "stg_guard": [], "ug": [[], []],
                "first_r": True, "w2s_done": None}

        def emit_prep_load(ch):
            if ch >= 32:
                return
            wbi = ch % 2
            WH("g", prep["wg2"][wbi])
            prep.setdefault("lv", {})[ch] = dma(
                Lw2[wbi], "g", w2rr_b[:, wbi, :],
                w2v[:, ch * 8192:(ch + 1) * 8192])

        def emit_prep_s2(ch):
            wbi = ch % 2
            if ch == 0:
                emit_prep_load(0)
            emit_prep_load(ch + 1)
            lv = prep["lv"][ch]
            lastmm = None
            for jj in range(16):
                idx, ptile, gu = pp.next()
                WH("t", gu)
                W("t", Lw2[wbi], lv)
                mv = TE(lambda t, pt=ptile, wbi=wbi, jj=jj:
                        t.matmul(pt[0:64, :], h64b[:, :],
                                 w2rr_b[:, wbi, jj * 512:(jj + 1) * 512],
                                 start=True, stop=True))
                lastmm = mv
                if jj % 2 == 0:
                    P.add("v", lambda e, h2=mm.h, v=mv: e.wait_ge(h2, v))
                    if jj == 0:
                        WH("v", prep["stg_guard"])
                    evv = VE(lambda v, pt=ptile, jj=jj:
                             v.tensor_copy(stg2[:, jj * 512:(jj + 1) * 512], pt[0:64, :]))
                    pp.guard(idx, ve, evv)
                else:
                    P.add("c", lambda e, h2=mm.h, v=mv: e.wait_ge(h2, v))
                    if jj == 1:
                        WH("c", prep["stg_guard"])
                    evv = SC(lambda s, pt=ptile, jj=jj:
                             s.copy(stg2[:, jj * 512:(jj + 1) * 512], pt[0:64, :]))
                    pp.guard(idx, es, evv)
            prep["wg2"][wbi] = [(mm.h, lastmm)]
            P.add("s", lambda e, h2=ve.h, v=ve.v: e.wait_ge(h2, v))
            P.add("s", lambda e, h2=es.h, v=es.v: e.wait_ge(h2, v))
            sv = dma(Lw2s, "s", w2s.ap()[:, ch * 8192:(ch + 1) * 8192], stg2[:, :])
            prep["stg_guard"] = [(Lw2s.h, sv)]

        def emit_prep_s1(step):
            a, dc2 = divmod(step, 4)
            ubi = step % 2
            WH("g", prep["ug"][ubi])
            if prep["first_r"]:
                prep["w2s_done"] = Lw2s.v
                P.add("g", lambda e, h2=Lw2s.h, v=Lw2s.v: e.wait_ge(h2, v))
                prep["first_r"] = False
            lv = dma(Lu1[ubi], "g", u1t[:, ubi, 0:512],
                     w2sv[a, :, dc2 * 512:(dc2 + 1) * 512])
            idx, ptile, gu = pp.next()
            WH("t", gu)
            W("t", Lu1[ubi], lv)
            mv = TE(lambda t, pt=ptile, ubi=ubi:
                    t.matmul(pt[:, :], h128b[:, :], u1t[:, ubi, 0:512],
                             start=True, stop=True))
            prep["ug"][ubi] = [(mm.h, mv)]
            P.add("c", lambda e, h2=mm.h, v=mv: e.wait_ge(h2, v))
            P.add("c", lambda e, h2=Lw2h[ubi].h, v=Lw2h[ubi].v: e.wait_ge(h2, v))
            evv = SC(lambda s, pt=ptile, ubi=ubi:
                     s.activation(w2h_st[:, ubi, 0:512], pt[:, :], AF.Copy,
                                  scale=hsc_c[:, 0:1]))
            pp.guard(idx, es, evv)
            P.add("s", lambda e, h2=es.h, v=evv: e.wait_ge(h2, v))
            dma(Lw2h[ubi], "s",
                w2h_out.ap()[a * HD:(a + 1) * HD, dc2 * 512:(dc2 + 1) * 512],
                w2h_st[:, ubi, 0:512])

        def emit_prep_slice(it):
            if it < 32:
                emit_prep_s2(it)
            else:
                for step in range((it - 32) * 8, (it - 32) * 8 + 8):
                    emit_prep_s1(step)

        lastf = [[], []]
        lastb = [[], []]
        prep_scan_guard = []
        gates_free = [[(ve.h, nsp_done)], [(ve.h, nsp_done)]]
        x1tm = x1tm_h
        itrs = [(0, False), (1, False), (2, False), (3, True)]
        xt_lv = {}

        def emit_xt_load(it2):
            if it2 > 63:
                return
            j2, h2 = divmod(it2, NH)
            b2 = it2 % 2
            if j2 == 3:
                src2 = xTown[h2 * HD:(h2 + 1) * HD, :]
            else:
                src2 = xTpre[h2 * HD:(h2 + 1) * HD, j2 * T:(j2 + 1) * T]
            WH("s", lastf[b2])
            vf2 = dma(Lxtf[b2], "s", xt_f[b2][:, :], src2)
            WH("g", lastb[b2])
            vb2 = dma(Lxtb[b2], "g", xt_b[b2][:, :], src2)
            xt_lv[it2] = (vf2, vb2)

        for (j, own) in itrs:
            for h in range(NH):
                it = j * NH + h
                bi = it % 2
                B1 = B1p[bi]
                B2 = B2p[bi]
                if it == 0:
                    emit_xt_load(0)
                emit_xt_load(it + 1)
                vf = xt_lv[it][0]
                vb = xt_lv[it][1]

                first_evac = True
                last_gmm = None
                for lc in range(4):
                    sl = slice(lc * 512, (lc + 1) * 512)
                    for (gw, gb, gout) in ((igw_b, igb_s, B1), (rgw_b, rgb_s, B2)):
                        idx, ptile, gu = pp.next()
                        WH("t", gu)
                        W("t", Lxtb[bi], vb)
                        mv = TE(lambda t, pt=ptile, gw=gw, h=h, sl=sl, bi=bi:
                                t.matmul(pt[:, :], gw[:, h, :], xt_b[bi][:, sl],
                                         start=True, stop=True))
                        last_gmm = mv
                        P.add("c", lambda e, h2=mm.h, v=mv: e.wait_ge(h2, v))
                        if first_evac:
                            WH("c", gates_free[bi])
                            first_evac = False
                        evv = SC(lambda s, pt=ptile, go=gout, sl=sl, gb=gb, h=h:
                                 s.activation(go[:, sl], pt[:, :], AF.Sigmoid,
                                              bias=gb[:, h:h + 1]))
                        pp.guard(idx, es, evv)
                ev_last = es.v
                lastb[bi] = [(mm.h, last_gmm)]

                emit_prep_slice(it)
                P.add("g", lambda e, h2=Lxtf[bi].h, v=vf: e.wait_ge(h2, v))
                P.add("g", lambda e, h2=es.h, v=ev_last: e.wait_ge(h2, v))
                WH("g", prep_scan_guard)
                z1v = GP(lambda g, bi=bi, B1=B1:
                         g.tensor_mul(B3[:, :], xt_f[bi][:, :], B1[:, :]))
                if not own:
                    P.add("v", lambda e, h2=es.h, v=ev_last: e.wait_ge(h2, v))
                    sgav = VE(lambda v, B2=B2: v.reduce_sum(cc1[:, :], B2[:, :], axis=AX.X))
                    P.add("c", lambda e, h2=ve.h, v=sgav: e.wait_ge(h2, v))
                    SC(lambda s, h=h, j=j:
                       s.activation(Aj[:, h * 3 + j:h * 3 + j + 1],
                                    cc1[:, :], AF.Exp, scale=nsp[:, h:h + 1]))
                W("c", es)
                av = SC(lambda s, h=h, B2=B2:
                        s.activation(B2[:, :], B2[:, :], AF.Exp, scale=nsp[:, h:h + 1]))
                P.add("c", lambda e, h2=gp.h, v=z1v: e.wait_ge(h2, v))
                W("c", es)
                SC(lambda s, B1=B1, B2=B2: s.activation(B1[:, :], B2[:, :], AF.Square))
                W("c", es)
                m_v = SC(lambda s, B1=B1: s.activation(B1[:, :], B1[:, :], AF.Sqrt,
                                                       scale=-1.0, bias=1.0))
                mfix = (es.h, m_v)
                if j == 0:
                    P.add("v", lambda e, h2=es.h, v=m_v: e.wait_ge(h2, v))
                    fv = VE(lambda v, B1=B1: v.memset(B1[:, 0:1], 1.0))
                    mfix = (ve.h, fv)
                elif own:
                    P.add("v", lambda e, h2=es.h, v=m_v: e.wait_ge(h2, v))
                    VE(lambda v, B1=B1:
                       v.scalar_tensor_tensor(m0fix[:, :], B1[:, 0:1],
                                              msel_s[:, 4:5], msel_s[:, 3:4],
                                              ALU.mult, ALU.add))
                    W("v", ve)
                    fv = VE(lambda v, B1=B1: v.tensor_copy(B1[:, 0:1], m0fix[:, :]))
                    mfix = (ve.h, fv)
                P.add("g", lambda e, h2=mfix[0], v=mfix[1]: e.wait_ge(h2, v))
                z2v = GP(lambda g, B1=B1: g.tensor_mul(B3[:, :], B3[:, :], B1[:, :]))
                P.add("v", lambda e, h2=gp.h, v=z2v: e.wait_ge(h2, v))
                P.add("v", lambda e, h2=es.h, v=av: e.wait_ge(h2, v))

                if not own:
                    VE(lambda v, B1=B1, B2=B2:
                       v.tensor_tensor_scan(B1[:, :], B2[:, :], B3[:, :],
                                            0.0, ALU.mult, ALU.add))
                    W("v", ve)
                    scanv = VE(lambda v, h=h, j=j, B1=B1:
                               v.tensor_copy(ej[:, h * 3 + j:h * 3 + j + 1],
                                             B1[:, T - 1:T]))
                    gates_free[bi] = [(ve.h, scanv)]
                    prep_scan_guard = [(ve.h, ve.v - 1)]
                    lastf[bi] = [(gp.h, z1v)]
                else:
                    e0 = ej[:, h * 3 + 0:h * 3 + 1]
                    e1 = ej[:, h * 3 + 1:h * 3 + 2]
                    e2 = ej[:, h * 3 + 2:h * 3 + 3]
                    A1c = Aj[:, h * 3 + 1:h * 3 + 2]
                    A2c = Aj[:, h * 3 + 2:h * 3 + 3]
                    VE(lambda v, e0=e0, e1=e1, A1c=A1c:
                       v.scalar_tensor_tensor(cc2[:, :], e0, A1c, e1, ALU.mult, ALU.add))
                    W("v", ve)
                    VE(lambda v, e2=e2, A2c=A2c:
                       v.scalar_tensor_tensor(cc3[:, :], cc2[:, :], A2c, e2, ALU.mult, ALU.add))
                    W("v", ve)
                    VE(lambda v, e0=e0, h=h:
                       v.tensor_single_scalar(carry[:, h:h + 1], e0, msel_s[:, 0:1], ALU.mult))
                    W("v", ve)
                    VE(lambda v, h=h:
                       v.scalar_tensor_tensor(carry[:, h:h + 1], cc2[:, :], msel_s[:, 1:2],
                                              carry[:, h:h + 1], ALU.mult, ALU.add))
                    W("v", ve)
                    VE(lambda v, h=h:
                       v.scalar_tensor_tensor(carry[:, h:h + 1], cc3[:, :], msel_s[:, 2:3],
                                              carry[:, h:h + 1], ALU.mult, ALU.add))
                    W("v", ve)
                    VE(lambda v, h=h, B1=B1, B2=B2:
                       v.tensor_tensor_scan(B1[:, :], B2[:, :], B3[:, :],
                                            carry[:, h:h + 1], ALU.mult, ALU.add))
                    W("v", ve)
                    resv = VE(lambda v, bi=bi, B1=B1:
                              v.tensor_add(B3[:, :], B1[:, :], xt_f[bi][:, :]))
                    prep_scan_guard = [(mm.h, mm.v), (ve.h, resv)]
                    lastf[bi] = [(ve.h, resv)]
                    trl = None
                    for lt in range(NT):
                        idx, ptile, gu = ptf.next()
                        P.add("t", lambda e, h2=ve.h, v=resv: e.wait_ge(h2, v))
                        WH("t", gu)
                        tv = TE(lambda t, lt=lt, pt=ptile:
                                t.transpose(pt[:, :], B3[:, lt * HD:(lt + 1) * HD],
                                            idf[:, :]))
                        P.add("v", lambda e, h2=mm.h, v=tv: e.wait_ge(h2, v))
                        evv = VE(lambda v, lt=lt, h=h, pt=ptile:
                                 v.tensor_copy(x1tm[:, lt, h * HD:(h + 1) * HD],
                                               pt[:, :]))
                        ptf.guard(idx, ve, evv)
                        trl = tv
                    gates_free[bi] = [(ve.h, ve.v), (mm.h, trl)]
        x1_assembled = ve.v
        for lt in range(NT):
            P.add("v", lambda e, h=ve.h, v=x1_assembled: e.wait_ge(h, v))
            VE(lambda v, lt=lt:
               v.reduce_sum(s1_all[:, lt:lt + 1], x1tm[:, lt, :], axis=AX.X))
            P.add("c", lambda e, h=ve.h, v=x1_assembled: e.wait_ge(h, v))
            W("c", es)
            SC(lambda s, lt=lt:
               s.activation(sq_scrA[:, :], x1tm[:, lt, :], AF.Square,
                            accum_out=s2_all[:, lt:lt + 1]))
        W("v", ve)
        W("v", es)
        VE(lambda v: v.tensor_scalar_mul(mu_all[:, :], s1_all[:, :], 1.0 / D))
        W("v", ve)
        VE(lambda v: v.tensor_scalar_mul(rs_all[:, :], s2_all[:, :], 1.0 / D))
        W("v", ve)
        VE(lambda v: v.tensor_mul(s1_all[:, :], mu_all[:, :], mu_all[:, :]))
        W("v", ve)
        VE(lambda v: v.tensor_sub(rs_all[:, :], rs_all[:, :], s1_all[:, :]))
        W("c", ve)
        SC(lambda s: s.activation(rs_all[:, :], rs_all[:, :], AF.Sqrt, bias=eps_c[:, 0:1]))
        W("v", es)
        VE(lambda v: v.reciprocal(rs_all[:, :], rs_all[:, :]))
        rs_done = ve.v
        x1T = x1T_h
        spill_last = Lsp.v
        x1T_trl = None
        for lt in range(NT):
            P.add("v", lambda e, h=ve.h, v=rs_done: e.wait_ge(h, v))
            av = VE(lambda v, lt=lt:
                    v.tensor_scalar(x1tm[:, lt, :], x1tm[:, lt, :],
                                    mu_all[:, lt:lt + 1], rs_all[:, lt:lt + 1],
                                    ALU.subtract, ALU.mult))
            P.add("s", lambda e, h=ve.h, v=av: e.wait_ge(h, v))
            spill_last = dma(Lsp, "s", x1f[lt * HD:(lt + 1) * HD, :], x1tm[:, lt, :])
            for dtk in range(NH):
                idx, ptile, gu = ptb.next()
                P.add("t", lambda e, h=ve.h, v=av: e.wait_ge(h, v))
                WH("t", gu)
                tv = TE(lambda t, lt=lt, dtk=dtk, pt=ptile:
                        t.transpose(pt[:, 0:HD], x1tm[:, lt, dtk * HD:(dtk + 1) * HD],
                                    idb[:, :]))
                x1T_trl = tv
                P.add("v", lambda e, h=mm.h, v=tv: e.wait_ge(h, v))
                evv = VE(lambda v, lt=lt, dtk=dtk, pt=ptile:
                         v.tensor_copy(x1T[:, dtk, lt * HD:(lt + 1) * HD], pt[:, 0:HD]))
                ptb.guard(idx, ve, evv)
        x1T_ready = ve.v

        if "B" not in phases:
            phases_skip_b = True
        for lt in range(NT if "B" in phases else 0):
            idx, ptile, gu = pp.next()
            WH("t", gu)
            if lt == 0:
                W("t", ve, x1T_ready)
            for dtk in range(NH):
                if dtk < NH - 1:
                    TN(lambda t, pt=ptile, lt=lt, dtk=dtk:
                       t.matmul(pt[:, 0:9], x1T[:, dtk, lt * HD:(lt + 1) * HD],
                                wr9_b[:, dtk, :], start=(dtk == 0), stop=False))
                else:
                    mv = TE(lambda t, pt=ptile, lt=lt, dtk=dtk:
                            t.matmul(pt[:, 0:9], x1T[:, dtk, lt * HD:(lt + 1) * HD],
                                     wr9_b[:, dtk, :], start=False, stop=True))
            P.add("v", lambda e, h=mm.h, v=mv: e.wait_ge(h, v))
            VE(lambda v, pt=ptile:
               v.reduce_max(pstat[:, 0:1], pt[:, 0:8], axis=AX.X, negate=True))
            W("c", ve)
            SC(lambda s, pt=ptile:
               s.activation(prob[:, :], pt[:, 0:8], AF.Exp, bias=pstat[:, 0:1],
                            accum_out=pstat[:, 1:2]))
            evv = SC(lambda s, pt=ptile, lt=lt:
                     s.activation(scl[:, lt * NU + 8:lt * NU + 9], pt[:, 8:9], AF.Sigmoid))
            pp.guard(idx, es, evv)
            W("v", es)
            VE(lambda v: v.reciprocal(pstat[:, 2:3], pstat[:, 1:2]))
            W("v", ve)
            VE(lambda v: v.tensor_scalar_mul(prob[:, :], prob[:, :], pstat[:, 2:3]))
            W("v", ve)
            VE(lambda v: v.reduce_max(pstat[:, 0:1], prob[:, :], axis=AX.X))
            W("v", ve)
            VE(lambda v: v.tensor_single_scalar(pmsk[:, :], prob[:, :],
                                                pstat[:, 0:1], ALU.is_ge))
            W("v", ve)
            VE(lambda v: v.scalar_tensor_tensor(pmsk[:, :], pmsk[:, :], -2.0,
                                                prob[:, :], ALU.mult, ALU.add))
            W("v", ve)
            VE(lambda v: v.reduce_max(pstat[:, 3:4], pmsk[:, :], axis=AX.X))
            W("v", ve)
            VE(lambda v: v.tensor_single_scalar(pmsk[:, :], prob[:, :],
                                                pstat[:, 3:4], ALU.is_ge))
            W("v", ve)
            VE(lambda v, lt=lt:
               v.tensor_mul(scl[:, lt * NU:lt * NU + 8], prob[:, :], pmsk[:, :]))
            W("v", ve)
            for r in range(1, 4):
                VE(lambda v, lt=lt, r=r:
                   v.tensor_copy(scl[:, lt * NU + 8 + r:lt * NU + 9 + r],
                                 scl[:, lt * NU + 8:lt * NU + 9]))
        scl_ready = ve.v

        P.add("g", lambda e, h=mm.h, v=x1T_trl: e.wait_ge(h, v))
        P.add("g", lambda e, h=Lsp.h, v=spill_last: e.wait_ge(h, v))
        for _lane in (Lw2s, Lw2h[0], Lw2h[1], Lu1[0], Lu1[1], Lw2[0], Lw2[1]):
            P.add("g", lambda e, h=_lane.h, v=_lane.v: e.wait_ge(h, v))

        units_range = range(NU if "B" in phases else 0)
        wfc_guard = [[], []]
        stage_guard = [[], []]
        prev_wd_reader = None
        prev_accum = None
        prev_silu_reader = None
        for u in units_range:
            wdl = None
            for fc in range(NFC):
                wbi = fc % 2
                if fc == 2:
                    if prev_wd_reader is not None:
                        P.add("g", lambda e, h=mm.h, v=prev_wd_reader: e.wait_ge(h, v))
                    wdl = dma(Lwd, "g", wd_b[:, :, :],
                              wd_all.ap()[u].rearrange("(t p) d -> p t d", p=HD))
                WH("g", wfc_guard[wbi])
                dma(Lwg[wbi], "g", wfc[:, wbi, 0:NH, :],
                    wg_all.ap()[u].rearrange("(t p) f -> p t f", p=HD)
                    [:, :, fc * HD:(fc + 1) * HD])
                l2 = dma(Lwg[wbi], "g", wfc[:, wbi, NH:2 * NH, :],
                         wu_all.ap()[u].rearrange("(t p) f -> p t f", p=HD)
                         [:, :, fc * HD:(fc + 1) * HD])
                fcmm_last = None
                for tc in range(4):
                    tsl = slice(tc * 512, (tc + 1) * 512)
                    i1, p1, g1 = pp.next()
                    i2, p2, g2 = pp.next()
                    WH("t", g1)
                    WH("t", g2)
                    W("t", Lwg[wbi], l2)
                    if u == 0 and fc == 0 and tc == 0:
                        W("t", ve, scl_ready)
                    for dtk in range(NH):
                        if dtk < NH - 1:
                            TN(lambda t, p1=p1, wbi=wbi, dtk=dtk, tsl=tsl:
                               t.matmul(p1[:, :], wfc[:, wbi, dtk, :], x1T[:, dtk, tsl],
                                        start=(dtk == 0), stop=False))
                        else:
                            TE(lambda t, p1=p1, wbi=wbi, dtk=dtk, tsl=tsl:
                               t.matmul(p1[:, :], wfc[:, wbi, dtk, :], x1T[:, dtk, tsl],
                                        start=False, stop=True))
                    for dtk in range(NH):
                        if dtk < NH - 1:
                            TN(lambda t, p2=p2, wbi=wbi, dtk=dtk, tsl=tsl:
                               t.matmul(p2[:, :], wfc[:, wbi, NH + dtk, :],
                                        x1T[:, dtk, tsl], start=(dtk == 0), stop=False))
                        else:
                            mv = TE(lambda t, p2=p2, wbi=wbi, dtk=dtk, tsl=tsl:
                                    t.matmul(p2[:, :], wfc[:, wbi, NH + dtk, :],
                                             x1T[:, dtk, tsl], start=False, stop=True))
                    fcmm_last = mv
                    P.add("c", lambda e, h=mm.h, v=mv: e.wait_ge(h, v))
                    if prev_silu_reader is not None:
                        P.add("c", lambda e, h=ve.h, v=prev_silu_reader: e.wait_ge(h, v))
                    W("c", es)
                    s1v = SC(lambda s, p1=p1: s.activation(sq_scrB[:, 0:512], p1[:, :], AF.Silu))
                    pp.guard(i1, es, s1v)
                    P.add("v", lambda e, h=es.h, v=s1v: e.wait_ge(h, v))
                    hv = VE(lambda v, p2=p2, fc=fc, tsl=tsl:
                            v.tensor_mul(h_e[:, fc, tsl], sq_scrB[:, 0:512], p2[:, :]))
                    pp.guard(i2, ve, hv)
                    prev_silu_reader = hv
                wfc_guard[wbi] = [(mm.h, fcmm_last)]
            up_all = ve.v
            for tt in range(NT):
                sbi = tt % 2
                dn_last = None
                for dc in range(4):
                    dsl = slice(dc * 512, (dc + 1) * 512)
                    idx, ptile, gu = pp.next()
                    WH("t", gu)
                    if dc == 0 and tt == 0:
                        W("t", ve, up_all)
                        W("t", Lwd, wdl)
                    for fc in range(NFC):
                        if fc < NFC - 1:
                            TN(lambda t, pt=ptile, fc=fc, tt=tt, dsl=dsl:
                               t.matmul(pt[:, :], h_e[:, fc, tt * HD:(tt + 1) * HD],
                                        wd_b[:, fc, dsl], start=(fc == 0), stop=False))
                        else:
                            mv = TE(lambda t, pt=ptile, fc=fc, tt=tt, dsl=dsl:
                                    t.matmul(pt[:, :], h_e[:, fc, tt * HD:(tt + 1) * HD],
                                             wd_b[:, fc, dsl], start=False, stop=True))
                    dn_last = mv
                    P.add("v", lambda e, h=mm.h, v=mv: e.wait_ge(h, v))
                    WH("v", stage_guard[sbi] if dc == 0 else [])
                    evv = VE(lambda v, pt=ptile, sbi=sbi, dsl=dsl, tt=tt, u=u:
                             v.tensor_scalar(stage[:, sbi, dsl], pt[:, :],
                                             scl[:, tt * NU + u:tt * NU + u + 1], None,
                                             ALU.mult))
                    pp.guard(idx, ve, evv)
                P.add("g", lambda e, h=ve.h, v=ve.v: e.wait_ge(h, v))
                if prev_accum is not None:
                    WH("g", prev_accum)
                avd = dma(Lacc[sbi], "g", moe_d[tt * HD:(tt + 1) * HD, :],
                          stage[:, sbi, :], accum=(u > 0))
                stage_guard[sbi] = [(Lacc[sbi].h, avd)]
            prev_accum = [(Lacc[0].h, Lacc[0].v), (Lacc[1].h, Lacc[1].v)]
            prev_wd_reader = dn_last

        moe_done = [(Lacc[0].h, Lacc[0].v), (Lacc[1].h, Lacc[1].v)]
        x2T = x1T
        WH("s", moe_done)
        prev_add = None
        prev_x2_users = []
        for lt in range(NT if "B" in phases else 0):
            if prev_add is not None:
                P.add("s", lambda e, h=ve.h, v=prev_add: e.wait_ge(h, v))
            v1 = dma(Lrl[0], "s", stage[:, 0, :], moe_d[lt * HD:(lt + 1) * HD, :])
            v2 = dma(Lrl[1], "s", x1rb[:, :], x1f[lt * HD:(lt + 1) * HD, :])
            P.add("v", lambda e, h=Lrl[0].h, v=v1: e.wait_ge(h, v))
            P.add("v", lambda e, h=Lrl[1].h, v=v2: e.wait_ge(h, v))
            WH("v", prev_x2_users)
            VE(lambda v: v.tensor_add(x2t_st[:, :], stage[:, 0, :], x1rb[:, :]))
            W("v", ve)
            VE(lambda v: v.reduce_sum(pstat[:, 0:1], x2t_st[:, :], axis=AX.X))
            W("c", ve)
            W("c", es)
            SC(lambda s: s.activation(sq_scrB[:, :], x2t_st[:, :], AF.Square,
                                      accum_out=pstat[:, 1:2]))
            W("v", es)
            VE(lambda v: v.tensor_scalar_mul(pstat[:, 0:1], pstat[:, 0:1], 1.0 / D))
            W("v", ve)
            VE(lambda v: v.tensor_scalar_mul(pstat[:, 1:2], pstat[:, 1:2], 1.0 / D))
            W("v", ve)
            VE(lambda v: v.tensor_mul(pstat[:, 2:3], pstat[:, 0:1], pstat[:, 0:1]))
            W("v", ve)
            VE(lambda v: v.tensor_sub(pstat[:, 1:2], pstat[:, 1:2], pstat[:, 2:3]))
            W("c", ve)
            SC(lambda s: s.activation(pstat[:, 1:2], pstat[:, 1:2], AF.Sqrt, bias=eps_c[:, 0:1]))
            W("v", es)
            VE(lambda v: v.reciprocal(pstat[:, 1:2], pstat[:, 1:2]))
            W("v", ve)
            lnv = VE(lambda v: v.tensor_scalar(x2t_st[:, :], x2t_st[:, :],
                                               pstat[:, 0:1], pstat[:, 1:2],
                                               ALU.subtract, ALU.mult))
            prev_add = lnv
            P.add("s", lambda e, h=ve.h, v=lnv: e.wait_ge(h, v))
            x2dma = dma(Lx2o, "s", x2_out[lt * HD:(lt + 1) * HD, :], x2t_st[:, :])
            users = [(Lx2o.h, x2dma)]
            for dtk in range(NH):
                idx, ptile, gu = ptf.next()
                P.add("t", lambda e, h=ve.h, v=lnv: e.wait_ge(h, v))
                WH("t", gu)
                tv = TE(lambda t, dtk=dtk, pt=ptile:
                        t.transpose(pt[:, :], x2t_st[:, dtk * HD:(dtk + 1) * HD],
                                    idf[:, :]))
                P.add("v", lambda e, h=mm.h, v=tv: e.wait_ge(h, v))
                evv = VE(lambda v, lt=lt, dtk=dtk, pt=ptile:
                         v.tensor_copy(x2T[:, dtk, lt * HD:(lt + 1) * HD], pt[:, :]))
                ptf.guard(idx, ve, evv)
            users.append((mm.h, tv))
            prev_x2_users = users
        x2T_ready = ve.v
        x2_out_done = Lx2o.v

        kh_tm = kh_h
        wq_guard = [[], []]
        P.add("g", lambda e, h=mm.h, v=mm.v: e.wait_ge(h, v))
        bvl = dma(Lwvh, "g", bvr_b[:, :], bv_row[:, :])
        kv0 = VE(lambda v: v.memset(kv_sb[:, :], 0.0))
        qcm_guard = [(Lx2o.h, x2_out_done)]

        def proj_cm(ht, wsrc, bias_s, kind, out_dram, do_kh):
            nonlocal qcm_guard
            bi = ht % 2
            WH("g", wq_guard[bi])
            lv = dma(Lwq[bi], "g", wqk_t[:, bi, :, :],
                     wsrc.ap().rearrange("(t p) h -> p t h", p=HD)
                     [:, :, ht * HD:(ht + 1) * HD])
            lastmm = None
            done = None
            for tc in range(4):
                tsl = slice(tc * 512, (tc + 1) * 512)
                idx, ptile, gu = pp.next()
                WH("t", gu)
                W("t", Lwq[bi], lv)
                if ht == 0 and tc == 0:
                    W("t", ve, x2T_ready)
                for dtk in range(NH):
                    if dtk < NH - 1:
                        TN(lambda t, pt=ptile, bi=bi, dtk=dtk, tsl=tsl:
                           t.matmul(pt[:, :], wqk_t[:, bi, dtk, :], x2T[:, dtk, tsl],
                                    start=(dtk == 0), stop=False))
                    else:
                        mv = TE(lambda t, pt=ptile, bi=bi, dtk=dtk, tsl=tsl:
                                t.matmul(pt[:, :], wqk_t[:, bi, dtk, :], x2T[:, dtk, tsl],
                                         start=False, stop=True))
                lastmm = mv
                if kind == "silu":
                    P.add("c", lambda e, h=mm.h, v=mv: e.wait_ge(h, v))
                    if tc == 0:
                        WH("c", qcm_guard)
                    evv = SC(lambda s, pt=ptile, ht=ht, tsl=tsl:
                             s.activation(qcm_t[:, tsl], pt[:, :], AF.Silu,
                                          bias=bias_s[:, ht:ht + 1]))
                    pp.guard(idx, es, evv)
                    done = (es.h, evv)
                elif "4" in phases:
                    P.add("c", lambda e, h=mm.h, v=mv: e.wait_ge(h, v))
                    if tc == 0:
                        WH("c", qcm_guard)
                    evv = SC(lambda s, pt=ptile, ht=ht, tsl=tsl:
                             s.activation(qcm_t[:, tsl], pt[:, :], AF.Relu,
                                          bias=bias_s[:, ht:ht + 1]))
                    pp.guard(idx, es, evv)
                    done = (es.h, evv)
                else:
                    P.add("c", lambda e, h=mm.h, v=mv: e.wait_ge(h, v))
                    if tc > 0:
                        W("c", ve)
                    rv = SC(lambda s, pt=ptile, ht=ht:
                            s.activation(elu_r[:, :], pt[:, :], AF.Relu,
                                         bias=bias_s[:, ht:ht + 1]))
                    P.add("v", lambda e, h=mm.h, v=mv: e.wait_ge(h, v))
                    W("v", es)
                    if "1" in phases:
                        tv_ = VE(lambda v, pt=ptile: v.tensor_copy(elu_t[:, :], pt[:, :]))
                    else:
                        tv_ = VE(lambda v, pt=ptile, ht=ht:
                                 v.tensor_scalar(elu_t[:, :], pt[:, :], bias_s[:, ht:ht + 1],
                                                 0.0, ALU.add, ALU.min))
                    pp.guard(idx, ve, tv_)
                    pp.guard(idx, es, rv)
                    W("c", ve, tv_)
                    if "2" in phases:
                        ev_ = SC(lambda s: s.activation(elu_t[:, :], elu_t[:, :], AF.Copy))
                    else:
                        ev_ = SC(lambda s: s.activation(elu_t[:, :], elu_t[:, :], AF.Exp))
                    P.add("v", lambda e, h=es.h, v=ev_: e.wait_ge(h, v))
                    P.add("v", lambda e, h=es.h, v=rv: e.wait_ge(h, v))
                    if tc == 0:
                        WH("v", qcm_guard)
                    if "3" in phases:
                        fv = VE(lambda v, tsl=tsl: v.tensor_copy(qcm_t[:, tsl], elu_t[:, :]))
                    else:
                        fv = VE(lambda v, ht=ht, tsl=tsl:
                                v.scalar_tensor_tensor(qcm_t[:, tsl], elu_t[:, :], -1.0,
                                                       elu_r[:, :], ALU.add, ALU.add))
                    done = (ve.h, fv)
            wq_guard[bi] = [(mm.h, lastmm)]
            P.add("s", lambda e, h=done[0], v=done[1]: e.wait_ge(h, v))
            ov = dma(Lqout, "s", out_dram[ht * HD:(ht + 1) * HD, :], qcm_t[:, :])
            users = [(Lqout.h, ov)]
            if do_kh:
                for lt in range(NT):
                    idx, ptile, gu = ptb.next()
                    P.add("t", lambda e, h=done[0], v=done[1]: e.wait_ge(h, v))
                    WH("t", gu)
                    tv2 = TE(lambda t, lt=lt, pt=ptile:
                             t.transpose(pt[:, 0:HD], qcm_t[:, lt * HD:(lt + 1) * HD],
                                         idb[:, :]))
                    P.add("v", lambda e, h=mm.h, v=tv2: e.wait_ge(h, v))
                    if ht == 0 and lt == 0:
                        P.add("v", lambda e, h=Lx2o.h, v=x2_out_done: e.wait_ge(h, v))
                    evv = VE(lambda v, lt=lt, ht=ht, pt=ptile:
                             v.tensor_copy(kh_tm[:, lt, ht * HD:(ht + 1) * HD],
                                           pt[:, 0:HD]))
                    ptb.guard(idx, ve, evv)
                users.append((mm.h, tv2))
            qcm_guard = users

        for ht in range(NH if ("C" in phases or "q" in phases) else 0):
            proj_cm(ht, wqk, bqk_s, "silu" if "E" in phases else "elu", q_out,
                    do_kh=("K" not in phases))
        kh_ready = ve.v
        for ht in range(NH if ("C" in phases or "U" in phases) else 0):
            proj_cm(ht, wu1, bu_s, "silu", u_out, do_kh=False)

        wv_guard = []
        vtt_guard = []
        for hc in range(4 if "C" in phases else 0):
            WH("g", wv_guard)
            lv = dma(Lwvh, "g", wvh[:, :, :],
                     wv.ap().rearrange("(t p) h -> p t h", p=HD)
                     [:, :, hc * 512:(hc + 1) * 512])
            lastmm = None
            for tt in range(NT):
                idx, ptile, gu = pp.next()
                WH("t", gu)
                W("t", Lwvh, lv)
                for dtk in range(NH):
                    TN(lambda t, pt=ptile, dtk=dtk, tt=tt:
                       t.matmul(pt[:, :], x2T[:, dtk, tt * HD:(tt + 1) * HD],
                                wvh[:, dtk, :], start=(dtk == 0), stop=False))
                mv = TE(lambda t, pt=ptile, hc=hc:
                        t.matmul(pt[:, :], onesb[0:1, :],
                                 bvr_b[0:1, hc * 512:(hc + 1) * 512],
                                 start=False, stop=True))
                lastmm = mv
                P.add("c", lambda e, h=mm.h, v=mv: e.wait_ge(h, v))
                WH("c", vtt_guard)
                sv = SC(lambda s, pt=ptile: s.activation(vtt[:, :], pt[:, :], AF.Silu))
                pp.guard(idx, es, sv)
                kvmms = []
                for hh in range(4):
                    habs = hc * 4 + hh
                    i2, pt2, gu2 = pp.next()
                    WH("t", gu2)
                    P.add("t", lambda e, h=es.h, v=sv: e.wait_ge(h, v))
                    if hc == 0 and tt == 0 and hh == 0:
                        W("t", ve, kh_ready)
                    tv3 = TE(lambda t, pt2=pt2, tt=tt, habs=habs, hh=hh:
                             t.matmul(pt2[:, 0:HD],
                                      kh_tm[:, tt, habs * HD:(habs + 1) * HD],
                                      vtt[:, hh * HD:(hh + 1) * HD],
                                      start=True, stop=True))
                    kvmms.append(tv3)
                    P.add("v", lambda e, h=mm.h, v=tv3: e.wait_ge(h, v))
                    if hc == 0 and tt == 0 and hh == 0:
                        W("v", ve, kv0)
                    evv = VE(lambda v, pt2=pt2, habs=habs:
                             v.tensor_add(kv_sb[:, habs * HD:(habs + 1) * HD],
                                          kv_sb[:, habs * HD:(habs + 1) * HD],
                                          pt2[:, 0:HD]))
                    pp.guard(i2, ve, evv)
                vtt_guard = [(mm.h, kvmms[-1])]
            wv_guard = [(mm.h, lastmm)]
        P.add("s", lambda e, h=ve.h, v=ve.v: e.wait_ge(h, v))
        dma(Lkv, "s", kv_out[:, :], kv_sb[:, :])

        for lane in (Lkv, Lqout, Lx2o, Lsp, Lrl[0], Lrl[1], Lacc[0], Lacc[1]):
            W("s", lane)
        for lane in (Lwvh, Lwq[0], Lwq[1], Lwd, Lwg[0], Lwg[1],
                     Lw2[0], Lw2[1], Lu1[0], Lu1[1]):
            W("g", lane)
        for lane in (Lw2s, Lw2h[0], Lw2h[1]):
            W("s", lane)

        with nc.Block() as blk:
            @blk.gpsimd
            def _(g):
                for f in P.ops["g"]:
                    f(g)

            @blk.sync
            def _(s):
                for f in P.ops["s"]:
                    f(s)

            @blk.tensor
            def _(t):
                for f in P.ops["t"]:
                    f(t)

            @blk.vector
            def _(v):
                for f in P.ops["v"]:
                    f(v)

            @blk.scalar
            def _(c):
                for f in P.ops["c"]:
                    f(c)

    return nc


def prep_launch1_inputs(inp):
    f32 = np.float32
    x = np.asarray(inp["L0_attn_x"], f32)
    xT = [np.ascontiguousarray(x[b].T) for b in range(B)]
    rp = np.ascontiguousarray(np.asarray(inp["L0_attn_recurrent_param"], f32)
                              .reshape(NH, HD).T)
    igb_ = np.ascontiguousarray(np.asarray(inp["L0_attn_input_gate_bias"], f32).T)
    rgb_ = np.ascontiguousarray(np.asarray(inp["L0_attn_recurrent_gate_bias"], f32).T)
    igw_ = np.ascontiguousarray(np.asarray(inp["L0_attn_input_gate_weight"], f32))
    rgw_ = np.ascontiguousarray(np.asarray(inp["L0_attn_recurrent_gate_weight"], f32))
    wr9_ = np.ascontiguousarray(
        np.concatenate([np.asarray(inp["L0_ffn_Wr"], f32),
                        np.asarray(inp["L0_ffn_Wsg"], f32)], axis=1))
    sgq = np.asarray(inp["L0_ffn_Sg"], f32).reshape(D, 4, MOE_F).transpose(1, 0, 2)
    suq = np.asarray(inp["L0_ffn_Su"], f32).reshape(D, 4, MOE_F).transpose(1, 0, 2)
    sdq = np.asarray(inp["L0_ffn_Sd"], f32).reshape(4, MOE_F, D)
    wg_ = np.ascontiguousarray(np.concatenate([np.asarray(inp["L0_ffn_Wg"], f32), sgq], 0))
    wu_ = np.ascontiguousarray(np.concatenate([np.asarray(inp["L0_ffn_Wu"], f32), suq], 0))
    wd_ = np.ascontiguousarray(np.concatenate([np.asarray(inp["L0_ffn_Wd"], f32), sdq], 0))
    wqk_ = np.ascontiguousarray(np.asarray(inp["L1_attn_Wqk"], f32))
    bqk_ = np.ascontiguousarray(np.asarray(inp["L1_attn_bqk"], f32).reshape(NH, HD).T)
    wv_ = np.ascontiguousarray(np.asarray(inp["L1_attn_Wv"], f32))
    bvr_ = np.asarray(inp["L1_attn_bv"], f32).reshape(1, D)
    wu1_ = np.ascontiguousarray(np.asarray(inp["L1_attn_Wu"], f32))
    bu_ = np.ascontiguousarray(np.asarray(inp["L1_attn_bu"], f32).reshape(NH, HD).T)
    ident = np.eye(HD, dtype=f32)
    ones = np.ones((1, HD), f32)
    w2rr_ = np.ascontiguousarray(np.asarray(inp["L1_ffn_W2"], f32).T).reshape(64, HD * D)
    h64_ = _sylvester(64)
    h128_ = _sylvester(128)

    maps = []
    for c in range(NC):
        b, q = c // 4, c % 4
        msel_ = np.zeros((HD, 5), f32)
        if q >= 1:
            msel_[:, q - 1] = 1.0
        msel_[:, 3] = 1.0 if q == 0 else 0.0
        msel_[:, 4] = 0.0 if q == 0 else 1.0
        maps.append({
            "xTpre": xT[b][:, 0:3 * T],
            "xTown": np.ascontiguousarray(xT[b][:, q * T:(q + 1) * T]),
            "rp": rp, "igw": igw_, "igb": igb_, "rgw": rgw_, "rgb": rgb_,
            "msel": msel_, "wr9": wr9_,
            "wg_all": wg_, "wu_all": wu_, "wd_all": wd_,
            "wqk": wqk_, "bqk": bqk_, "wv": wv_, "bv_row": bvr_,
            "wu1": wu1_, "bu": bu_, "ident": ident, "ones_in": ones,
            "w2rr": w2rr_, "h64": h64_, "h128": h128_,
        })
    return maps





def build_launch2():
    nc = bass.Bass("TRN2", target_bir_lowering=False)
    dt = nc.dram_tensor
    NA = 64
    FLAT = HD * D

    x2i = dt("x2i", [T, D], F32, kind="ExternalInput")
    qin = dt("qin", [D, T], BF16, kind="ExternalInput")
    uin = dt("uin", [D, T], BF16, kind="ExternalInput")
    kva = dt("kva", [4, HD, NH * HD], F32, kind="ExternalInput")
    wo = dt("wo", [D, D], F32, kind="ExternalInput")
    bo_row = dt("bo_row", [1, D], F32, kind="ExternalInput")
    lng = dt("lng", [HD, NH], F32, kind="ExternalInput")
    lnb = dt("lnb", [HD, NH], F32, kind="ExternalInput")
    w1t = dt("w1t", [D, FFN_H], F32, kind="ExternalInput")
    w3t = dt("w3t", [D, FFN_H], F32, kind="ExternalInput")
    w2h = dt("w2h", [FFN_H, D], BF16, kind="ExternalInput")
    ident = dt("ident", [HD, HD], F32, kind="ExternalInput")
    ones_in = dt("ones_in", [1, HD], F32, kind="ExternalInput")
    ones_col = dt("ones_col", [HD, 1], F32, kind="ExternalInput")

    out_o = dt("out_o", [T, D], F32, kind="ExternalOutput")

    x3f = dt("x3f", [T, D], F32, kind="Internal")
    ffd = dt("ffd", [T, D], F32, kind="Internal")

    P = Prog()
    ctx = ExitStack()
    with ctx:
        mm = Sem(ctx, nc, "mm")
        ve = Sem(ctx, nc, "ve")
        es = Sem(ctx, nc, "es")
        gp = Sem(ctx, nc, "gp")
        Lc = Sem(ctx, nc, "Lc")
        Lkv = Sem(ctx, nc, "Lkv")
        Lwo = Sem(ctx, nc, "Lwo")
        Lq = [Sem(ctx, nc, "Lq0"), Sem(ctx, nc, "Lq1")]
        Lx2 = Sem(ctx, nc, "Lx2")
        Lsp = Sem(ctx, nc, "Lsp")
        Lx3r = Sem(ctx, nc, "Lx3r")
        Lw13 = [Sem(ctx, nc, "Lw13_0"), Sem(ctx, nc, "Lw13_1")]
        Lw2hr = Sem(ctx, nc, "Lw2hr")
        Lffa = [Sem(ctx, nc, "Lffa0"), Sem(ctx, nc, "Lffa1")]
        Lrl = [Sem(ctx, nc, "Lrl0"), Sem(ctx, nc, "Lrl1")]
        Lout = Sem(ctx, nc, "Lout")

        def sb(name, shape, dtp):
            return nc.alloc_sbuf_tensor(name, shape, dtp)

        idf = sb("idf", [HD, HD], F32)
        onesb = sb("onesb", [1, HD], BF16)
        onesf = sb("onesf", [1, HD], F32)
        onescol_b = sb("onescol_b", [HD, 1], BF16)
        lng_s = sb("lng_s", [HD, NH], F32)
        lnb_s = sb("lnb_s", [HD, NH], F32)
        pst = sb("pst", [HD, 4], F32)
        eps_c = sb("eps_c", [HD, 1], F32)
        sq_b = sb("sq_b", [HD, 512], BF16)
        scr512 = sb("scr512", [HD, 512], F32)

        R0h = nc.alloc_sbuf_tensor("R0", [HD, 32768], F32)
        R1h = nc.alloc_sbuf_tensor("R1", [HD, 16384], F32)
        A0 = nc.lookup_mloc(R0h).addr
        A1 = nc.lookup_mloc(R1h).addr

        def carve(name, shape, dtp, base, off):
            return nc.alloc_sbuf_tensor_at(name, shape, dtp, offset=base + off)

        wo_b = carve("wo_b", [HD, NH, D], BF16, A0, 0)
        mu_b = carve("mu_b", [HD, D], F32, A0, 65536)
        rs_b = carve("rs_b", [HD, D], F32, A0, 73728)
        qu_t = carve("qu_t", [HD, 2, D], BF16, A0, 81920)
        x2_t = carve("x2_t", [HD, D], F32, A0, 90112)
        x3_st = carve("x3_st", [HD, D], F32, A0, 98304)
        n_scr = carve("n_scr", [HD, D], F32, A0, 106496)
        s1_scr = carve("s1_scr", [HD, D], F32, A0, 114688)
        kvA = carve("kvA", [HD, NH * HD], F32, A0, 98304)
        kv_b = carve("kv_b", [HD, NH * HD], BF16, A0, 122880)
        bo_b = carve("bo_b", [1, D], BF16, A0, 126976)
        s_sum = carve("s_sum", [1, D], F32, A0, 0)
        s_sq = carve("s_sq", [1, D], F32, A0, 8192)
        mu_row = s_sum
        rs_row = s_sq
        kvB = carve("kvB", [HD, NH * HD], F32, A0, 106496)
        ocm = carve("ocm", [HD, NH, D], BF16, A1, 0)
        x3T = carve("x3T", [HD, NH, D], BF16, A1, 0)
        hidden = carve("hidden", [HD, NH, 1024], BF16, A0, 0)
        w2h_hb = carve("w2h_hb", [HD, NH, D], BF16, A0, 32768)
        w13_t = carve("w13_t", [HD, 2, 2, NH, HD], BF16, A0, 98304)
        fstage = carve("fstage", [HD, 2, D], F32, A0, 114688)


        pp = Pool([nc.alloc_psum_tensor(f"pp{i}", [HD, 512], F32) for i in range(4)])
        ptf = Pool([nc.alloc_psum_tensor(f"ptf{i}", [HD, HD], F32) for i in range(2)])
        ps_sum = nc.alloc_psum_tensor("ps_sum", [1, 512], F32)
        ps_sq = nc.alloc_psum_tensor("ps_sq", [1, 512], F32)

        def W(key, sem, val=None):
            v = sem.v if val is None else val
            P.add(key, lambda e, h=sem.h, v=v: e.wait_ge(h, v))

        def WH(key, hv_list):
            for (h, v) in hv_list:
                P.add(key, lambda e, h=h, v=v: e.wait_ge(h, v))

        def dma(lane, key, out, in_, accum=False):
            lane.v += 16
            def f(e, out=out, in_=in_, accum=accum, h=lane.h):
                if accum:
                    e.dma_start(out=out, in_=in_, accum_op=ALU.add).then_inc(h, 16)
                else:
                    e.dma_start(out=out, in_=in_).then_inc(h, 16)
            P.add(key, f)
            return lane.v

        def SC(fn):
            es.v += 1
            P.add("c", lambda e, fn=fn, h=es.h: fn(e).then_inc(h, 1))
            return es.v

        def VE(fn):
            ve.v += 1
            P.add("v", lambda e, fn=fn, h=ve.h: fn(e).then_inc(h, 1))
            return ve.v

        def TE(fn):
            mm.v += 1
            P.add("t", lambda e, fn=fn, h=mm.h: fn(e).then_inc(h, 1))
            return mm.v

        def TN(fn):
            P.add("t", fn)

        def GP(fn):
            gp.v += 1
            P.add("g", lambda e, fn=fn, h=gp.h: fn(e).then_inc(h, 1))
            return gp.v

        dma(Lc, "g", idf[:, :], ident[:, :])
        dma(Lc, "g", onesb[:, :], ones_in[:, :])
        dma(Lc, "g", onesf[:, :], ones_in[:, :])
        dma(Lc, "g", onescol_b[:, :], ones_col[:, :])
        dma(Lc, "g", bo_b[:, :], bo_row[:, :])
        dma(Lc, "g", lng_s[:, :], lng[:, :])
        c0 = dma(Lc, "g", lnb_s[:, :], lnb[:, :])
        VE(lambda v: v.memset(eps_c[:, :], EPS))
        v1 = dma(Lkv, "s", kvA[:, :], kva.ap()[0])
        v2 = dma(Lkv, "s", kvB[:, :], kva.ap()[1])
        P.add("v", lambda e, h=Lkv.h, v=v2: e.wait_ge(h, v))
        a1 = VE(lambda v: v.tensor_add(kvA[:, :], kvA[:, :], kvB[:, :]))
        P.add("s", lambda e, h=ve.h, v=a1: e.wait_ge(h, v))
        v3 = dma(Lkv, "s", kvB[:, :], kva.ap()[2])
        P.add("v", lambda e, h=Lkv.h, v=v3: e.wait_ge(h, v))
        a2 = VE(lambda v: v.tensor_add(kvA[:, :], kvA[:, :], kvB[:, :]))
        P.add("s", lambda e, h=ve.h, v=a2: e.wait_ge(h, v))
        v4 = dma(Lkv, "s", kvB[:, :], kva.ap()[3])
        P.add("v", lambda e, h=Lkv.h, v=v4: e.wait_ge(h, v))
        VE(lambda v: v.tensor_add(kvA[:, :], kvA[:, :], kvB[:, :]))
        W("v", ve)
        VE(lambda v: v.tensor_copy(kv_b[:, :], kvA[:, :]))
        kvb_done = ve.v
        P.add("t", lambda e, h=Lc.h, v=Lc.v: e.wait_ge(h, v))

        qg = [[], []]
        for ht in range(NH):
            qbi = ht % 2
            WH("s", qg[qbi])
            lv = dma(Lq[qbi], "s", qu_t[:, qbi, :], qin[ht * HD:(ht + 1) * HD, :])
            for tc in range(4):
                tsl = slice(tc * 512, (tc + 1) * 512)
                idx, ptile, gu = pp.next()
                WH("t", gu)
                W("t", Lq[qbi], lv)
                if ht == 0 and tc == 0:
                    W("t", ve, kvb_done)
                mv = TE(lambda t, pt=ptile, ht=ht, qbi=qbi, tsl=tsl:
                        t.matmul(pt[:, :], kv_b[:, ht * HD:(ht + 1) * HD],
                                 qu_t[:, qbi, tsl], start=True, stop=True))
                P.add("v", lambda e, h=mm.h, v=mv: e.wait_ge(h, v))
                evv = VE(lambda v, pt=ptile, ht=ht, tsl=tsl:
                         v.tensor_copy(ocm[:, ht, tsl], pt[:, :]))
                pp.guard(idx, ve, evv)
            qg[qbi] = [(mm.h, mm.v)]
        ocm_done = ve.v

        sq_mm_prev = None
        stats_evac = None
        for tc in range(4):
            tsl = slice(tc * 512, (tc + 1) * 512)
            if stats_evac is not None:
                P.add("t", lambda e, h=ve.h, v=stats_evac: e.wait_ge(h, v))
            for ht in range(NH):
                P.add("c", lambda e, h=ve.h, v=ocm_done: e.wait_ge(h, v))
                if sq_mm_prev is not None:
                    P.add("c", lambda e, h=mm.h, v=sq_mm_prev: e.wait_ge(h, v))
                sv = SC(lambda s, ht=ht, tsl=tsl:
                        s.activation(sq_b[:, :], ocm[:, ht, tsl], AF.Square))
                if ht == 0 and tc == 0:
                    P.add("t", lambda e, h=ve.h, v=ocm_done: e.wait_ge(h, v))
                TN(lambda t, ht=ht, tsl=tsl, first=(ht == 0):
                   t.matmul(ps_sum[:, :], onescol_b[:, :], ocm[:, ht, tsl],
                            start=first, stop=(ht == NH - 1)))
                P.add("t", lambda e, h=es.h, v=sv: e.wait_ge(h, v))
                mv = TE(lambda t, ht=ht, first=(ht == 0):
                        t.matmul(ps_sq[:, :], onescol_b[:, :], sq_b[:, :],
                                 start=first, stop=(ht == NH - 1)))
                sq_mm_prev = mv
            smv = sq_mm_prev
            P.add("v", lambda e, h=mm.h, v=smv: e.wait_ge(h, v))
            VE(lambda v, tsl=tsl: v.tensor_copy(s_sum[0:1, tsl], ps_sum[0:1, :]))
            W("v", ve)
            VE(lambda v, tsl=tsl: v.tensor_copy(s_sq[0:1, tsl], ps_sq[0:1, :]))
            W("v", ve)
            stats_evac = ve.v
        VE(lambda v: v.tensor_scalar_mul(mu_row[:, :], s_sum[:, :], 1.0 / D))
        W("v", ve)
        VE(lambda v: v.tensor_scalar_mul(rs_row[:, :], s_sq[:, :], 1.0 / D))
        W("v", ve)
        VE(lambda v: v.tensor_mul(n_scr[0:1, :], mu_row[:, :], mu_row[:, :]))
        W("v", ve)
        VE(lambda v: v.tensor_sub(rs_row[:, :], rs_row[:, :], n_scr[0:1, :]))
        W("c", ve)
        SC(lambda s: s.activation(rs_row[:, :], rs_row[:, :], AF.Sqrt,
                                  bias=eps_c[0:1, 0:1]))
        W("v", es)
        VE(lambda v: v.reciprocal(rs_row[:, :], rs_row[:, :]))
        rows_done = ve.v
        for tc in range(4):
            tsl = slice(tc * 512, (tc + 1) * 512)
            for (rowt, dstt) in ((mu_row, mu_b), (rs_row, rs_b)):
                idx, ptile, gu = pp.next()
                WH("t", gu)
                P.add("t", lambda e, h=ve.h, v=rows_done: e.wait_ge(h, v))
                mv = TE(lambda t, pt=ptile, rowt=rowt, tsl=tsl:
                        t.matmul(pt[:, :], onesf[0:1, :], rowt[0:1, tsl],
                                 start=True, stop=True))
                P.add("v", lambda e, h=mm.h, v=mv: e.wait_ge(h, v))
                evv = VE(lambda v, pt=ptile, dstt=dstt, tsl=tsl:
                         v.tensor_copy(dstt[:, tsl], pt[:, :]))
                pp.guard(idx, ve, evv)
        bcast_done = ve.v
        P.add("g", lambda e, h=mm.h, v=mm.v: e.wait_ge(h, v))
        wol = dma(Lwo, "g", wo_b[:, :, :], wo.ap().rearrange("(t p) d -> p t d", p=HD))

        ug2 = [list(qg[0]), list(qg[1])]
        nsc = [n_scr, s1_scr]
        nsc_guard = [[], []]
        prev_fv = []
        for ht in range(NH):
            ubi = ht % 2
            WH("s", ug2[ubi])
            lv = dma(Lq[ubi], "s", qu_t[:, ubi, :], uin[ht * HD:(ht + 1) * HD, :])
            if ht == 0:
                P.add("g", lambda e, h=ve.h, v=bcast_done: e.wait_ge(h, v))
            WH("g", nsc_guard[ubi])
            GP(lambda g, ht=ht, ubi=ubi:
               g.tensor_sub(nsc[ubi][:, :], ocm[:, ht, :], mu_b[:, :]))
            P.add("g", lambda e, h=gp.h, v=gp.v: e.wait_ge(h, v))
            gsub = GP(lambda g, ubi=ubi:
                      g.tensor_mul(nsc[ubi][:, :], nsc[ubi][:, :], rs_b[:, :]))
            P.add("v", lambda e, h=gp.h, v=gsub: e.wait_ge(h, v))
            P.add("v", lambda e, h=Lq[ubi].h, v=lv: e.wait_ge(h, v))
            W("v", ve)
            VE(lambda v, ht=ht, ubi=ubi:
               v.tensor_scalar(nsc[ubi][:, :], nsc[ubi][:, :], lng_s[:, ht:ht + 1],
                               lnb_s[:, ht:ht + 1], ALU.mult, ALU.add))
            W("v", ve)
            fv = VE(lambda v, ht=ht, ubi=ubi:
                    v.tensor_mul(ocm[:, ht, :], qu_t[:, ubi, :], nsc[ubi][:, :]))
            nsc_guard[ubi] = [(ve.h, fv)]
            ug2[ubi] = [(ve.h, fv)]
        scm_done = ve.v

        x2g = []
        prev_ln = None
        for tt in range(NT):
            WH("s", x2g)
            xlv = dma(Lx2, "s", x2_t[:, :], x2i[tt * HD:(tt + 1) * HD, :])
            dn = None
            for dc in range(4):
                dsl = slice(dc * 512, (dc + 1) * 512)
                idx, ptile, gu = pp.next()
                WH("t", gu)
                if tt == 0 and dc == 0:
                    W("t", ve, scm_done)
                    W("t", Lwo, wol)
                for ht in range(NH):
                    TN(lambda t, pt=ptile, ht=ht, tt=tt, dsl=dsl, first=(ht == 0):
                       t.matmul(pt[:, :], ocm[:, ht, tt * HD:(tt + 1) * HD],
                                wo_b[:, ht, dsl], start=first, stop=False))
                mv = TE(lambda t, pt=ptile, dsl=dsl:
                        t.matmul(pt[:, :], onesb[0:1, :], bo_b[0:1, dsl],
                                 start=False, stop=True))
                dn = mv
                P.add("v", lambda e, h=mm.h, v=mv: e.wait_ge(h, v))
                P.add("v", lambda e, h=Lx2.h, v=xlv: e.wait_ge(h, v))
                if prev_ln is not None and dc == 0:
                    P.add("v", lambda e, h=Lsp.h, v=prev_ln: e.wait_ge(h, v))
                evv = VE(lambda v, pt=ptile, dsl=dsl:
                         v.tensor_add(x3_st[:, dsl], pt[:, :], x2_t[:, dsl]))
                pp.guard(idx, ve, evv)
            x2g = [(ve.h, ve.v)]
            W("v", ve)
            VE(lambda v: v.reduce_sum(pst[:, 0:1], x3_st[:, :], axis=AX.X))
            W("c", ve)
            W("c", es)
            SC(lambda s: s.activation(n_scr[:, :], x3_st[:, :], AF.Square,
                                      accum_out=pst[:, 1:2]))
            W("v", es)
            VE(lambda v: v.tensor_scalar_mul(pst[:, 0:1], pst[:, 0:1], 1.0 / D))
            W("v", ve)
            VE(lambda v: v.tensor_scalar_mul(pst[:, 1:2], pst[:, 1:2], 1.0 / D))
            W("v", ve)
            VE(lambda v: v.tensor_mul(pst[:, 2:3], pst[:, 0:1], pst[:, 0:1]))
            W("v", ve)
            VE(lambda v: v.tensor_sub(pst[:, 1:2], pst[:, 1:2], pst[:, 2:3]))
            W("c", ve)
            SC(lambda s: s.activation(pst[:, 1:2], pst[:, 1:2], AF.Sqrt,
                                      bias=eps_c[:, 0:1]))
            W("v", es)
            VE(lambda v: v.reciprocal(pst[:, 1:2], pst[:, 1:2]))
            W("v", ve)
            lnv = VE(lambda v: v.tensor_scalar(x3_st[:, :], x3_st[:, :], pst[:, 0:1],
                                               pst[:, 1:2], ALU.subtract, ALU.mult))
            P.add("s", lambda e, h=ve.h, v=lnv: e.wait_ge(h, v))
            prev_ln = dma(Lsp, "s", x3f[tt * HD:(tt + 1) * HD, :], x3_st[:, :])

        x3_all = Lsp.v
        x3g = []
        for tt in range(NT):
            WH("s", x3g)
            if tt == 0:
                P.add("s", lambda e, h=Lsp.h, v=x3_all: e.wait_ge(h, v))
            rlv = dma(Lx3r, "s", x3_st[:, :], x3f[tt * HD:(tt + 1) * HD, :])
            trl = None
            for dtk in range(NH):
                idx, ptile, gu = ptf.next()
                P.add("t", lambda e, h=Lx3r.h, v=rlv: e.wait_ge(h, v))
                if tt == 0 and dtk == 0:
                    W("t", ve, scm_done)
                WH("t", gu)
                tv = TE(lambda t, pt=ptile, dtk=dtk:
                        t.transpose(pt[:, :], x3_st[:, dtk * HD:(dtk + 1) * HD],
                                    idf[:, :]))
                trl = tv
                P.add("v", lambda e, h=mm.h, v=tv: e.wait_ge(h, v))
                evv = VE(lambda v, pt=ptile, tt=tt, dtk=dtk:
                         v.tensor_copy(x3T[:, dtk, tt * HD:(tt + 1) * HD], pt[:, :]))
                ptf.guard(idx, ve, evv)
            x3g = [(mm.h, trl)]
        x3T_done = ve.v

        P.add("g", lambda e, h=ve.h, v=ve.v: e.wait_ge(h, v))
        P.add("g", lambda e, h=es.h, v=es.v: e.wait_ge(h, v))
        P.add("g", lambda e, h=mm.h, v=mm.v: e.wait_ge(h, v))
        w13g = [[], []]
        prev_hv6 = None
        ffa_guard = [[], []]
        w2hr_guard = []
        prev_hb_acc = None
        first6 = True
        def w13_load(hidx, wbi):
            WH("g", w13g[wbi])
            dma(Lw13[wbi], "g", w13_t[:, wbi, 0, :, :],
                w1t.ap().rearrange("(t p) h -> p t h", p=HD)
                [:, :, hidx * HD:(hidx + 1) * HD])
            return dma(Lw13[wbi], "g", w13_t[:, wbi, 1, :, :],
                       w3t.ap().rearrange("(t p) h -> p t h", p=HD)
                       [:, :, hidx * HD:(hidx + 1) * HD])

        for tc2 in range(2):
            tbase = tc2 * 1024
            for hb in range(4):
                w13_lv = {0: w13_load(hb * NH + 0, 0), 1: w13_load(hb * NH + 1, 1)}
                WH("g", w2hr_guard)
                w2l = None
                for ht8 in range(NH):
                    hidx = hb * NH + ht8
                    w2l = dma(Lw2hr, "g", w2h_hb[:, ht8, :],
                              w2h.ap()[hidx * HD:(hidx + 1) * HD, :])
                up_last = None
                for ht8 in range(NH):
                    hidx = hb * NH + ht8
                    wbi = ht8 % 2
                    lv = w13_lv[ht8]
                    for th in range(2):
                        tsl = slice(tbase + th * 512, tbase + (th + 1) * 512)
                        i1, p1, g1 = pp.next()
                        i2, p2, g2 = pp.next()
                        WH("t", g1)
                        WH("t", g2)
                        W("t", Lw13[wbi], lv)
                        if first6:
                            W("t", ve, x3T_done)
                            first6 = False
                        for dtk in range(NH):
                            TN(lambda t, p1=p1, wbi=wbi, dtk=dtk, tsl=tsl:
                               t.matmul(p1[:, :], w13_t[:, wbi, 0, dtk, :],
                                        x3T[:, dtk, tsl], start=(dtk == 0),
                                        stop=(dtk == NH - 1)))
                        for dtk in range(NH):
                            if dtk < NH - 1:
                                TN(lambda t, p2=p2, wbi=wbi, dtk=dtk, tsl=tsl:
                                   t.matmul(p2[:, :], w13_t[:, wbi, 1, dtk, :],
                                            x3T[:, dtk, tsl], start=(dtk == 0), stop=False))
                            else:
                                mv = TE(lambda t, p2=p2, wbi=wbi, dtk=dtk, tsl=tsl:
                                        t.matmul(p2[:, :], w13_t[:, wbi, 1, dtk, :],
                                                 x3T[:, dtk, tsl], start=False, stop=True))
                        up_last = mv
                        P.add("c", lambda e, h=mm.h, v=mv: e.wait_ge(h, v))
                        W("c", es)
                        if prev_hv6 is not None:
                            P.add("c", lambda e, h=ve.h, v=prev_hv6: e.wait_ge(h, v))
                        s1v = SC(lambda s, p1=p1: s.activation(scr512[:, :], p1[:, :], AF.Silu))
                        pp.guard(i1, es, s1v)
                        P.add("v", lambda e, h=es.h, v=s1v: e.wait_ge(h, v))
                        hv = VE(lambda v, p2=p2, ht8=ht8, th=th:
                                v.tensor_mul(hidden[:, ht8, th * 512:(th + 1) * 512],
                                             scr512[:, :], p2[:, :]))
                        pp.guard(i2, ve, hv)
                        prev_hv6 = hv
                    w13g[wbi] = [(mm.h, up_last)]
                    if ht8 + 2 <= NH - 1:
                        w13_lv[ht8 + 2] = w13_load(hb * NH + ht8 + 2, (ht8 + 2) % 2)
                hid_done = ve.v
                dn_last = None
                for tt2 in range(8):
                    sbi = tt2 % 2
                    for dc in range(4):
                        dsl = slice(dc * 512, (dc + 1) * 512)
                        idx, ptile, gu = pp.next()
                        WH("t", gu)
                        if tt2 == 0 and dc == 0:
                            W("t", ve, hid_done)
                            W("t", Lw2hr, w2l)
                        for ht8 in range(NH):
                            if ht8 < NH - 1:
                                TN(lambda t, pt=ptile, ht8=ht8, tt2=tt2, dsl=dsl:
                                   t.matmul(pt[:, :],
                                            hidden[:, ht8, tt2 * HD:(tt2 + 1) * HD],
                                            w2h_hb[:, ht8, dsl], start=(ht8 == 0),
                                            stop=False))
                            else:
                                mv = TE(lambda t, pt=ptile, ht8=ht8, tt2=tt2, dsl=dsl:
                                        t.matmul(pt[:, :],
                                                 hidden[:, ht8, tt2 * HD:(tt2 + 1) * HD],
                                                 w2h_hb[:, ht8, dsl], start=False,
                                                 stop=True))
                        dn_last = mv
                        P.add("v", lambda e, h=mm.h, v=mv: e.wait_ge(h, v))
                        WH("v", ffa_guard[sbi] if dc == 0 else [])
                        evv = VE(lambda v, pt=ptile, sbi=sbi, dsl=dsl:
                                 v.tensor_copy(fstage[:, sbi, dsl], pt[:, :]))
                        pp.guard(idx, ve, evv)
                    P.add("g", lambda e, h=ve.h, v=ve.v: e.wait_ge(h, v))
                    if prev_hb_acc is not None and hb > 0:
                        WH("g", prev_hb_acc)
                    row = tc2 * 1024 + tt2 * HD
                    av = dma(Lffa[sbi], "g", ffd[row:row + HD, :], fstage[:, sbi, :],
                             accum=(hb > 0))
                    ffa_guard[sbi] = [(Lffa[sbi].h, av)]
                prev_hb_acc = [(Lffa[0].h, Lffa[0].v), (Lffa[1].h, Lffa[1].v)]
                w2hr_guard = [(mm.h, dn_last)]
            prev_hb_acc = None

        fst2 = carve("fst2", [HD, 2, D], F32, A0, 0)
        x3s2 = carve("x3s2", [HD, D], F32, A0, 32768)
        pst2 = sb("pst2", [HD, 4], F32)
        ffd_done = [(Lffa[0].h, Lffa[0].v), (Lffa[1].h, Lffa[1].v)]
        WH("s", ffd_done)
        Lr2 = [Sem(ctx, nc, "Lr2_0"), Sem(ctx, nc, "Lr2_1")]
        Lo2 = [Lout, Sem(ctx, nc, "LoutB")]
        tile_guard = [[], []]
        for tt in range(NT):
            pi = tt % 2
            fs = fstage if pi == 0 else fst2
            xs = x3_st if pi == 0 else x3s2
            ps_ = pst if pi == 0 else pst2
            lane0 = Lrl[pi]
            lane1 = Lr2[pi]
            WH("s", tile_guard[pi])
            f1 = dma(lane0, "s", fs[:, 0, :], ffd[tt * HD:(tt + 1) * HD, :])
            f2 = dma(lane1, "s", fs[:, 1, :], x3f[tt * HD:(tt + 1) * HD, :])
            if tt < 2:
                P.add("g", lambda e, h=mm.h, v=mm.v: e.wait_ge(h, v))
            P.add("g", lambda e, h=lane0.h, v=f1: e.wait_ge(h, v))
            P.add("g", lambda e, h=lane1.h, v=f2: e.wait_ge(h, v))
            addv = GP(lambda g, fs=fs, xs=xs:
                      g.tensor_add(xs[:, :], fs[:, 0, :], fs[:, 1, :]))
            P.add("v", lambda e, h=gp.h, v=addv: e.wait_ge(h, v))
            W("v", ve)
            VE(lambda v, xs=xs, ps_=ps_:
               v.reduce_sum(ps_[:, 0:1], xs[:, :], axis=AX.X))
            P.add("c", lambda e, h=gp.h, v=addv: e.wait_ge(h, v))
            W("c", es)
            sqv = SC(lambda s, xs=xs, ps_=ps_:
                     s.activation(n_scr[:, :], xs[:, :], AF.Square,
                                  accum_out=ps_[:, 1:2]))
            P.add("v", lambda e, h=es.h, v=sqv: e.wait_ge(h, v))
            W("v", ve)
            VE(lambda v, ps_=ps_:
               v.tensor_scalar(ps_[:, 0:2], ps_[:, 0:2], 1.0 / D, None, ALU.mult))
            W("v", ve)
            VE(lambda v, ps_=ps_:
               v.tensor_mul(ps_[:, 2:3], ps_[:, 0:1], ps_[:, 0:1]))
            W("v", ve)
            VE(lambda v, ps_=ps_:
               v.tensor_sub(ps_[:, 1:2], ps_[:, 1:2], ps_[:, 2:3]))
            W("c", ve)
            SC(lambda s, ps_=ps_:
               s.activation(ps_[:, 1:2], ps_[:, 1:2], AF.Sqrt, bias=eps_c[:, 0:1]))
            W("v", es)
            VE(lambda v, ps_=ps_: v.reciprocal(ps_[:, 1:2], ps_[:, 1:2]))
            W("v", ve)
            apv = VE(lambda v, xs=xs, ps_=ps_:
                     v.tensor_scalar(xs[:, :], xs[:, :], ps_[:, 0:1], ps_[:, 1:2],
                                     ALU.subtract, ALU.mult))
            P.add("s", lambda e, h=ve.h, v=apv: e.wait_ge(h, v))
            ov = dma(Lo2[pi], "s", out_o[tt * HD:(tt + 1) * HD, :], xs[:, :])
            tile_guard[pi] = [(Lo2[pi].h, ov)]
        for lane in (Lout, Lo2[1], Lrl[0], Lrl[1], Lr2[0], Lr2[1], Lsp, Lx3r,
                     Lx2, Lq[0], Lq[1], Lkv):
            W("s", lane)
        for lane in (Lffa[0], Lffa[1], Lw13[0], Lw13[1], Lw2hr, Lwo, Lc):
            W("g", lane)

        with nc.Block() as blk:
            @blk.gpsimd
            def _(g):
                for f in P.ops["g"]:
                    f(g)

            @blk.sync
            def _(s):
                for f in P.ops["s"]:
                    f(s)

            @blk.tensor
            def _(t):
                for f in P.ops["t"]:
                    f(t)

            @blk.vector
            def _(v):
                for f in P.ops["v"]:
                    f(v)

            @blk.scalar
            def _(c):
                for f in P.ops["c"]:
                    f(c)

    return nc


def _sylvester(k):
    H = np.array([[1.0]], np.float32)
    while H.shape[0] < k:
        H = np.block([[H, H], [H, -H]]).astype(np.float32)
    return H


def prep_launch2_inputs(inp, res1):
    f32 = np.float32
    wo_ = np.ascontiguousarray(np.asarray(inp["L1_attn_Wo"], f32))
    bo_ = np.asarray(inp["L1_attn_bo"], f32).reshape(1, D)
    lng_ = np.ascontiguousarray(np.asarray(inp["L1_attn_ln_g"], f32).reshape(NH, HD).T)
    lnb_ = np.ascontiguousarray(np.asarray(inp["L1_attn_ln_b"], f32).reshape(NH, HD).T)
    w1t_ = np.ascontiguousarray(np.asarray(inp["L1_ffn_W1"], f32).T)
    w3t_ = np.ascontiguousarray(np.asarray(inp["L1_ffn_W3"], f32).T)
    ident = np.eye(HD, dtype=f32)
    ones = np.ones((1, HD), f32)
    onesc = np.ones((HD, 1), f32)
    maps = []
    for c in range(NC):
        g0 = (c // 4) * 4
        kva_ = np.stack([np.asarray(res1[g0 + j]["kv_out"]) for j in range(4)])
        maps.append({
            "x2i": np.asarray(res1[c]["x2_out"]),
            "qin": np.asarray(res1[c]["q_out"]),
            "uin": np.asarray(res1[c]["u_out"]),
            "kva": kva_, "wo": wo_, "bo_row": bo_, "lng": lng_, "lnb": lnb_,
            "w1t": w1t_, "w3t": w3t_, "w2h": np.asarray(res1[c]["w2h_out"]),
            "ident": ident, "ones_in": ones, "ones_col": onesc,
        })
    return maps


_BUILD_CACHE = {}


def kernel(**inputs):
    if "l1" not in _BUILD_CACHE:
        _BUILD_CACHE["l1"] = build_launch1()
        _BUILD_CACHE["l2"] = build_launch2()
    nc1, nc2 = _BUILD_CACHE["l1"], _BUILD_CACHE["l2"]
    maps1 = prep_launch1_inputs(inputs)
    res1 = run_bass_kernel_spmd(nc1, maps1, core_ids=list(range(NC))).results
    maps2 = prep_launch2_inputs(inputs, res1)
    res2 = run_bass_kernel_spmd(nc2, maps2, core_ids=list(range(NC))).results
    halves = []
    for b in range(B):
        halves.append(np.concatenate(
            [np.asarray(res2[b * 4 + q]["out_o"]) for q in range(4)], axis=0))
    return np.stack(halves, axis=0)


# version 06
# speedup vs baseline: 1.0699x; 1.0699x over previous
import numpy as np
import ml_dtypes
import concourse.bass as bass
import concourse.mybir as mybir
from concourse.bass_utils import run_bass_kernel_spmd
from contextlib import ExitStack

F32 = mybir.dt.float32
BF16 = mybir.dt.bfloat16
AF = mybir.ActivationFunctionType
ALU = mybir.AluOpType
AX = mybir.AxisListType

B, L, D = 2, 8192, 2048
NC, Q = 8, 4
T = L // Q
NH, HD = 16, 128
NT = T // HD
MOE_F, NU = 1408, 12
NFC = MOE_F // HD
FFN_H = 8192
EPS = 1e-5
bf16 = ml_dtypes.bfloat16

ENGS = ("g", "s", "t", "v", "c")


class Prog:
    def __init__(self):
        self.ops = {e: [] for e in ENGS}

    def add(self, key, fn):
        self.ops[key].append(fn)


class Sem:
    def __init__(self, ctx, nc, name):
        self.h = ctx.enter_context(nc.semaphore(name))
        self.v = 0


class Pool:

    def __init__(self, tiles):
        self.tiles = tiles
        self.guards = [[] for _ in tiles]
        self.i = 0

    def next(self):
        idx = self.i
        self.i = (self.i + 1) % len(self.tiles)
        g = self.guards[idx]
        self.guards[idx] = []
        return idx, self.tiles[idx], g

    def guard(self, idx, sem, val):
        self.guards[idx].append((sem.h, val))


def build_launch1(phases="ABC"):
    nc = bass.Bass("TRN2", target_bir_lowering=False)
    dt = nc.dram_tensor

    xTpre = dt("xTpre", [D, 3 * T], F32, kind="ExternalInput")
    xTown = dt("xTown", [D, T], F32, kind="ExternalInput")
    rp = dt("rp", [HD, NH], F32, kind="ExternalInput")
    igw = dt("igw", [NH, HD, HD], F32, kind="ExternalInput")
    igb = dt("igb", [HD, NH], F32, kind="ExternalInput")
    rgw = dt("rgw", [NH, HD, HD], F32, kind="ExternalInput")
    rgb = dt("rgb", [HD, NH], F32, kind="ExternalInput")
    msel = dt("msel", [HD, 5], F32, kind="ExternalInput")
    wr9 = dt("wr9", [D, 9], F32, kind="ExternalInput")
    wg_all = dt("wg_all", [NU, D, MOE_F], F32, kind="ExternalInput")
    wu_all = dt("wu_all", [NU, D, MOE_F], F32, kind="ExternalInput")
    wd_all = dt("wd_all", [NU, MOE_F, D], F32, kind="ExternalInput")
    wqk = dt("wqk", [D, D], F32, kind="ExternalInput")
    bqk = dt("bqk", [HD, NH], F32, kind="ExternalInput")
    wv = dt("wv", [D, D], F32, kind="ExternalInput")
    bv_row = dt("bv_row", [1, D], F32, kind="ExternalInput")
    wu1 = dt("wu1", [D, D], F32, kind="ExternalInput")
    bu = dt("bu", [HD, NH], F32, kind="ExternalInput")
    ident = dt("ident", [HD, HD], F32, kind="ExternalInput")
    ones_in = dt("ones_in", [1, HD], F32, kind="ExternalInput")
    w2rr = dt("w2rr", [64, HD * D], F32, kind="ExternalInput")
    h64 = dt("h64", [64, 64], F32, kind="ExternalInput")
    h128 = dt("h128", [HD, HD], F32, kind="ExternalInput")

    x2_out = dt("x2_out", [T, D], F32, kind="ExternalOutput")
    w2h_out = dt("w2h_out", [FFN_H, D], BF16, kind="ExternalOutput")
    q_out = dt("q_out", [D, T], BF16, kind="ExternalOutput")
    u_out = dt("u_out", [D, T], BF16, kind="ExternalOutput")
    kv_out = dt("kv_out", [HD, NH * HD], F32, kind="ExternalOutput")

    x1f = dt("x1f", [T, D], BF16, kind="Internal")
    w2s = dt("w2s", [64, HD * D], BF16, kind="Internal")
    moe_d = dt("moe_d", [T, D], F32, kind="Internal")

    P = Prog()
    ctx = ExitStack()
    with ctx:
        mm = Sem(ctx, nc, "mm")
        ve = Sem(ctx, nc, "ve")
        es = Sem(ctx, nc, "es")
        gp = Sem(ctx, nc, "gp")
        Lconst = Sem(ctx, nc, "Lconst")
        Lxtb = [Sem(ctx, nc, "Lxtb0"), Sem(ctx, nc, "Lxtb1")]
        Lxtf = [Sem(ctx, nc, "Lxtf0"), Sem(ctx, nc, "Lxtf1")]
        Lwd = Sem(ctx, nc, "Lwd")
        Lwg = [Sem(ctx, nc, "Lwg0"), Sem(ctx, nc, "Lwg1")]
        Lacc = [Sem(ctx, nc, "Lacc0"), Sem(ctx, nc, "Lacc1")]
        Lwq = [Sem(ctx, nc, "Lwq0"), Sem(ctx, nc, "Lwq1")]
        Lwvh = Sem(ctx, nc, "Lwvh")
        Lsp = Sem(ctx, nc, "Lsp")
        Lx2o = Sem(ctx, nc, "Lx2o")
        Lrl = [Sem(ctx, nc, "Lrl0"), Sem(ctx, nc, "Lrl1")]
        Lqout = Sem(ctx, nc, "Lqout")
        Lkv = Sem(ctx, nc, "Lkv")
        Lw2 = [Sem(ctx, nc, "Lw2_0"), Sem(ctx, nc, "Lw2_1")]
        Lw2s = Sem(ctx, nc, "Lw2s")
        Lu1 = [Sem(ctx, nc, "Lu1_0"), Sem(ctx, nc, "Lu1_1")]
        Lw2h = [Sem(ctx, nc, "Lw2h0"), Sem(ctx, nc, "Lw2h1")]

        def sb(name, shape, dtp):
            return nc.alloc_sbuf_tensor(name, shape, dtp)

        nsp = sb("nsp", [HD, NH], F32)
        sp_t1 = sb("sp_t1", [HD, NH], F32)
        igb_s = sb("igb_s", [HD, NH], F32)
        rgb_s = sb("rgb_s", [HD, NH], F32)
        msel_s = sb("msel_s", [HD, 5], F32)
        bqk_s = sb("bqk_s", [HD, NH], F32)
        bu_s = sb("bu_s", [HD, NH], F32)
        igw_b = sb("igw_b", [HD, NH, HD], BF16)
        rgw_b = sb("rgw_b", [HD, NH, HD], BF16)
        idf = sb("idf", [HD, HD], F32)
        idb = sb("idb", [HD, HD], BF16)
        onesb = sb("onesb", [1, HD], BF16)
        wr9_b = sb("wr9_b", [HD, NH, 9], BF16)
        ej = sb("ej", [HD, NH * 3], F32)
        Aj = sb("Aj", [HD, NH * 3], F32)
        carry = sb("carry", [HD, NH], F32)
        cc1 = sb("cc1", [HD, 1], F32)
        cc2 = sb("cc2", [HD, 1], F32)
        cc3 = sb("cc3", [HD, 1], F32)
        m0fix = sb("m0fix", [HD, 1], F32)
        s1_all = sb("s1_all", [HD, NT], F32)
        s2_all = sb("s2_all", [HD, NT], F32)
        mu_all = sb("mu_all", [HD, NT], F32)
        rs_all = sb("rs_all", [HD, NT], F32)
        scl = sb("scl", [HD, NT * NU], F32)
        prob = sb("prob", [HD, 8], F32)
        h64b = sb("h64b", [64, 64], BF16)
        h128b = sb("h128b", [HD, HD], BF16)
        hsc_c = sb("hsc_c", [HD, 1], F32)
        pmsk = sb("pmsk", [HD, 8], F32)
        pstat = sb("pstat", [HD, 4], F32)
        eps_c = sb("eps_c", [HD, 1], F32)

        R0h = nc.alloc_sbuf_tensor("R0", [HD, 32768], F32)
        R1h = nc.alloc_sbuf_tensor("R1", [HD, 16384], F32)
        A0 = nc.lookup_mloc(R0h).addr
        A1 = nc.lookup_mloc(R1h).addr

        def carve(name, shape, dtp, base, off):
            return nc.alloc_sbuf_tensor_at(name, shape, dtp, offset=base + off)

        x1tm_h = carve("x1tm", [HD, NT, D], BF16, A0, 0)
        sq_scrA = carve("sq_scrA", [HD, D], F32, A0, 65536)
        xt_f = [carve("xtf0", [HD, T], F32, A1, 0),
                carve("xtf1", [HD, T], F32, A1, 8192)]
        xt_b = [carve("xtb0", [HD, T], BF16, A1, 16384),
                carve("xtb1", [HD, T], BF16, A1, 20480)]
        B1p = [carve("B1a", [HD, T], F32, A1, 24576),
               carve("B1b", [HD, T], F32, A1, 49152)]
        B2p = [carve("B2a", [HD, T], F32, A1, 32768),
               carve("B2b", [HD, T], F32, A1, 57344)]
        B3 = carve("B3", [HD, T], F32, A1, 40960)
        w2rr_b = carve("w2rr_b", [64, 2, 8192], BF16, A0, 73728)
        stg2 = carve("stg2", [64, 8192], BF16, A0, 106496)
        u1t = carve("u1t", [HD, 2, 1024], BF16, A0, 122880)
        w2h_st = carve("w2h_st", [HD, 2, 1024], BF16, A0, 126976)
        h_e = carve("h_e", [HD, NFC, D], BF16, A0, 0)
        wd_b = carve("wd_b", [HD, NFC, D], BF16, A0, 45056)
        wfc = carve("wfc", [HD, 2, 2 * NH, HD], BF16, A0, 90112)
        stage = carve("stage", [HD, 2, D], F32, A0, 106496)
        sq_scrB = carve("sq_scrB", [HD, D], F32, A0, 122880)
        x2t_st = carve("x2t_st", [HD, D], F32, A0, 0)
        x1rb = carve("x1rb", [HD, D], BF16, A0, 114688)
        x1T_h = carve("x1T", [HD, NH, T], BF16, A1, 0)
        kh_h = carve("kh", [HD, NT, D], BF16, A0, 0)
        wvh = carve("wvh", [HD, NH, 512], BF16, A0, 65536)
        wqk_t = carve("wqk_t", [HD, 2, NH, HD], BF16, A0, 81920)
        vtt = carve("vtt", [HD, 512], BF16, A0, 90112)
        elu_r = carve("elu_r", [HD, 512], F32, A0, 91136)
        elu_t = carve("elu_t", [HD, 512], F32, A0, 93184)
        qcm_t = carve("qcm_t", [HD, T], BF16, A0, 95232)
        bvr_b = carve("bvr_b", [1, D], BF16, A0, 99328)
        kv_sb = carve("kv_sb", [HD, NH * HD], F32, A0, 103424)

        pp_t = [nc.alloc_psum_tensor(f"pp{i}", [HD, 512], F32) for i in range(4)]
        pp = Pool(pp_t)
        ptf_t = [nc.alloc_psum_tensor(f"ptf{i}", [HD, HD], F32) for i in range(2)]
        ptb_t = [nc.alloc_psum_tensor(f"ptb{i}", [HD, 1024], BF16) for i in range(2)]
        ptf = Pool(ptf_t)
        ptb = Pool(ptb_t)

        def W(key, sem, val=None):
            v = sem.v if val is None else val
            P.add(key, lambda e, h=sem.h, v=v: e.wait_ge(h, v))

        def WH(key, hv_list):
            for (h, v) in hv_list:
                P.add(key, lambda e, h=h, v=v: e.wait_ge(h, v))

        def dma(lane, key, out, in_, accum=False):
            lane.v += 16
            def f(e, out=out, in_=in_, accum=accum, h=lane.h):
                if accum:
                    e.dma_start(out=out, in_=in_, accum_op=ALU.add).then_inc(h, 16)
                else:
                    e.dma_start(out=out, in_=in_).then_inc(h, 16)
            P.add(key, f)
            return lane.v

        def SC(fn):
            es.v += 1
            P.add("c", lambda e, fn=fn, h=es.h: fn(e).then_inc(h, 1))
            return es.v

        def VE(fn):
            ve.v += 1
            P.add("v", lambda e, fn=fn, h=ve.h: fn(e).then_inc(h, 1))
            return ve.v

        def TE(fn):
            mm.v += 1
            P.add("t", lambda e, fn=fn, h=mm.h: fn(e).then_inc(h, 1))
            return mm.v

        def TN(fn):
            P.add("t", fn)

        def GP(fn):
            gp.v += 1
            P.add("g", lambda e, fn=fn, h=gp.h: fn(e).then_inc(h, 1))
            return gp.v

        dma(Lconst, "g", nsp[:, :], rp[:, :])
        dma(Lconst, "g", igb_s[:, :], igb[:, :])
        dma(Lconst, "g", rgb_s[:, :], rgb[:, :])
        dma(Lconst, "g", msel_s[:, :], msel[:, :])
        dma(Lconst, "g", bqk_s[:, :], bqk[:, :])
        dma(Lconst, "g", bu_s[:, :], bu[:, :])
        dma(Lconst, "g", igw_b[:, :, :], igw.ap().transpose([1, 0, 2]))
        dma(Lconst, "g", rgw_b[:, :, :], rgw.ap().transpose([1, 0, 2]))
        dma(Lconst, "g", idf[:, :], ident[:, :])
        dma(Lconst, "g", idb[:, :], ident[:, :])
        dma(Lconst, "g", onesb[:, :], ones_in[:, :])
        dma(Lconst, "g", h64b[:, :], h64[:, :])
        dma(Lconst, "g", h128b[:, :], h128[:, :])
        c0 = dma(Lconst, "g", wr9_b[:, :, :], wr9.ap().rearrange("(t p) n -> p t n", p=HD))

        VE(lambda v: v.memset(eps_c[:, :], EPS))
        W("v", ve)
        VE(lambda v: v.memset(hsc_c[:, :], 1.0 / float(np.sqrt(FFN_H))))
        P.add("t", lambda e, h=Lconst.h, v=Lconst.v: e.wait_ge(h, v))
        W("c", Lconst, c0)
        SC(lambda s: s.activation(sp_t1[:, :], nsp[:, :], AF.Abs))
        W("c", es)
        SC(lambda s: s.activation(sp_t1[:, :], sp_t1[:, :], AF.Exp, scale=-1.0))
        W("c", es)
        SC(lambda s: s.activation(sp_t1[:, :], sp_t1[:, :], AF.Ln, bias=1.0))
        W("c", es)
        SC(lambda s: s.activation(nsp[:, :], nsp[:, :], AF.Relu))
        W("v", es)
        VE(lambda v: v.tensor_add(nsp[:, :], nsp[:, :], sp_t1[:, :]))
        W("v", ve)
        VE(lambda v: v.tensor_scalar_mul(nsp[:, :], nsp[:, :], -8.0))
        nsp_done = ve.v

        w2v = w2rr.ap()
        w2sv = w2s.ap().rearrange("a (b d) -> a b d", d=D)
        prep = {"wg2": [[], []], "stg_guard": [], "ug": [[], []],
                "first_r": True, "w2s_done": None}

        def emit_prep_load(ch):
            if ch >= 32:
                return
            wbi = ch % 2
            WH("g", prep["wg2"][wbi])
            prep.setdefault("lv", {})[ch] = dma(
                Lw2[wbi], "g", w2rr_b[:, wbi, :],
                w2v[:, ch * 8192:(ch + 1) * 8192])

        def emit_prep_s2(ch):
            wbi = ch % 2
            if ch == 0:
                emit_prep_load(0)
            emit_prep_load(ch + 1)
            lv = prep["lv"][ch]
            lastmm = None
            for jj in range(16):
                idx, ptile, gu = pp.next()
                WH("t", gu)
                W("t", Lw2[wbi], lv)
                mv = TE(lambda t, pt=ptile, wbi=wbi, jj=jj:
                        t.matmul(pt[0:64, :], h64b[:, :],
                                 w2rr_b[:, wbi, jj * 512:(jj + 1) * 512],
                                 start=True, stop=True))
                lastmm = mv
                if jj % 2 == 0:
                    P.add("v", lambda e, h2=mm.h, v=mv: e.wait_ge(h2, v))
                    if jj == 0:
                        WH("v", prep["stg_guard"])
                    evv = VE(lambda v, pt=ptile, jj=jj:
                             v.tensor_copy(stg2[:, jj * 512:(jj + 1) * 512], pt[0:64, :]))
                    pp.guard(idx, ve, evv)
                else:
                    P.add("c", lambda e, h2=mm.h, v=mv: e.wait_ge(h2, v))
                    if jj == 1:
                        WH("c", prep["stg_guard"])
                    evv = SC(lambda s, pt=ptile, jj=jj:
                             s.copy(stg2[:, jj * 512:(jj + 1) * 512], pt[0:64, :]))
                    pp.guard(idx, es, evv)
            prep["wg2"][wbi] = [(mm.h, lastmm)]
            P.add("s", lambda e, h2=ve.h, v=ve.v: e.wait_ge(h2, v))
            P.add("s", lambda e, h2=es.h, v=es.v: e.wait_ge(h2, v))
            sv = dma(Lw2s, "s", w2s.ap()[:, ch * 8192:(ch + 1) * 8192], stg2[:, :])
            prep["stg_guard"] = [(Lw2s.h, sv)]

        def emit_prep_s1(step):
            a, dc2 = divmod(step, 4)
            ubi = step % 2
            WH("g", prep["ug"][ubi])
            if prep["first_r"]:
                prep["w2s_done"] = Lw2s.v
                P.add("g", lambda e, h2=Lw2s.h, v=Lw2s.v: e.wait_ge(h2, v))
                prep["first_r"] = False
            lv = dma(Lu1[ubi], "g", u1t[:, ubi, 0:512],
                     w2sv[a, :, dc2 * 512:(dc2 + 1) * 512])
            idx, ptile, gu = pp.next()
            WH("t", gu)
            W("t", Lu1[ubi], lv)
            mv = TE(lambda t, pt=ptile, ubi=ubi:
                    t.matmul(pt[:, :], h128b[:, :], u1t[:, ubi, 0:512],
                             start=True, stop=True))
            prep["ug"][ubi] = [(mm.h, mv)]
            P.add("c", lambda e, h2=mm.h, v=mv: e.wait_ge(h2, v))
            P.add("c", lambda e, h2=Lw2h[ubi].h, v=Lw2h[ubi].v: e.wait_ge(h2, v))
            evv = SC(lambda s, pt=ptile, ubi=ubi:
                     s.activation(w2h_st[:, ubi, 0:512], pt[:, :], AF.Copy,
                                  scale=hsc_c[:, 0:1]))
            pp.guard(idx, es, evv)
            P.add("s", lambda e, h2=es.h, v=evv: e.wait_ge(h2, v))
            dma(Lw2h[ubi], "s",
                w2h_out.ap()[a * HD:(a + 1) * HD, dc2 * 512:(dc2 + 1) * 512],
                w2h_st[:, ubi, 0:512])

        def emit_prep_slice(it):
            if it < 32:
                emit_prep_s2(it)
            else:
                for step in range((it - 32) * 8, (it - 32) * 8 + 8):
                    emit_prep_s1(step)

        lastf = [[], []]
        lastb = [[], []]
        prep_scan_guard = []
        gates_free = [[(ve.h, nsp_done)], [(ve.h, nsp_done)]]
        x1tm = x1tm_h
        itrs = [(0, False), (1, False), (2, False), (3, True)]
        xt_lv = {}

        def emit_xt_load(it2):
            if it2 > 63:
                return
            j2, h2 = divmod(it2, NH)
            b2 = it2 % 2
            if j2 == 3:
                src2 = xTown[h2 * HD:(h2 + 1) * HD, :]
            else:
                src2 = xTpre[h2 * HD:(h2 + 1) * HD, j2 * T:(j2 + 1) * T]
            WH("s", lastf[b2])
            vf2 = dma(Lxtf[b2], "s", xt_f[b2][:, :], src2)
            WH("g", lastb[b2])
            vb2 = dma(Lxtb[b2], "g", xt_b[b2][:, :], src2)
            xt_lv[it2] = (vf2, vb2)

        for (j, own) in itrs:
            for h in range(NH):
                it = j * NH + h
                bi = it % 2
                B1 = B1p[bi]
                B2 = B2p[bi]
                if it == 0:
                    emit_xt_load(0)
                emit_xt_load(it + 1)
                vf = xt_lv[it][0]
                vb = xt_lv[it][1]

                first_evac = True
                last_gmm = None
                for lc in range(4):
                    sl = slice(lc * 512, (lc + 1) * 512)
                    for (gw, gb, gout) in ((igw_b, igb_s, B1), (rgw_b, rgb_s, B2)):
                        idx, ptile, gu = pp.next()
                        WH("t", gu)
                        W("t", Lxtb[bi], vb)
                        mv = TE(lambda t, pt=ptile, gw=gw, h=h, sl=sl, bi=bi:
                                t.matmul(pt[:, :], gw[:, h, :], xt_b[bi][:, sl],
                                         start=True, stop=True))
                        last_gmm = mv
                        P.add("c", lambda e, h2=mm.h, v=mv: e.wait_ge(h2, v))
                        if first_evac:
                            WH("c", gates_free[bi])
                            first_evac = False
                        evv = SC(lambda s, pt=ptile, go=gout, sl=sl, gb=gb, h=h:
                                 s.activation(go[:, sl], pt[:, :], AF.Sigmoid,
                                              bias=gb[:, h:h + 1]))
                        pp.guard(idx, es, evv)
                ev_last = es.v
                lastb[bi] = [(mm.h, last_gmm)]

                emit_prep_slice(it)
                P.add("g", lambda e, h2=Lxtf[bi].h, v=vf: e.wait_ge(h2, v))
                P.add("g", lambda e, h2=es.h, v=ev_last: e.wait_ge(h2, v))
                WH("g", prep_scan_guard)
                z1v = GP(lambda g, bi=bi, B1=B1:
                         g.tensor_mul(B3[:, :], xt_f[bi][:, :], B1[:, :]))
                if not own:
                    P.add("v", lambda e, h2=es.h, v=ev_last: e.wait_ge(h2, v))
                    sgav = VE(lambda v, B2=B2: v.reduce_sum(cc1[:, :], B2[:, :], axis=AX.X))
                    P.add("c", lambda e, h2=ve.h, v=sgav: e.wait_ge(h2, v))
                    SC(lambda s, h=h, j=j:
                       s.activation(Aj[:, h * 3 + j:h * 3 + j + 1],
                                    cc1[:, :], AF.Exp, scale=nsp[:, h:h + 1]))
                W("c", es)
                av = SC(lambda s, h=h, B2=B2:
                        s.activation(B2[:, :], B2[:, :], AF.Exp, scale=nsp[:, h:h + 1]))
                P.add("c", lambda e, h2=gp.h, v=z1v: e.wait_ge(h2, v))
                W("c", es)
                SC(lambda s, B1=B1, B2=B2: s.activation(B1[:, :], B2[:, :], AF.Square))
                W("c", es)
                m_v = SC(lambda s, B1=B1: s.activation(B1[:, :], B1[:, :], AF.Sqrt,
                                                       scale=-1.0, bias=1.0))
                mfix = (es.h, m_v)
                if j == 0:
                    P.add("v", lambda e, h2=es.h, v=m_v: e.wait_ge(h2, v))
                    fv = VE(lambda v, B1=B1: v.memset(B1[:, 0:1], 1.0))
                    mfix = (ve.h, fv)
                elif own:
                    P.add("v", lambda e, h2=es.h, v=m_v: e.wait_ge(h2, v))
                    VE(lambda v, B1=B1:
                       v.scalar_tensor_tensor(m0fix[:, :], B1[:, 0:1],
                                              msel_s[:, 4:5], msel_s[:, 3:4],
                                              ALU.mult, ALU.add))
                    W("v", ve)
                    fv = VE(lambda v, B1=B1: v.tensor_copy(B1[:, 0:1], m0fix[:, :]))
                    mfix = (ve.h, fv)
                P.add("g", lambda e, h2=mfix[0], v=mfix[1]: e.wait_ge(h2, v))
                z2v = GP(lambda g, B1=B1: g.tensor_mul(B3[:, :], B3[:, :], B1[:, :]))
                P.add("v", lambda e, h2=gp.h, v=z2v: e.wait_ge(h2, v))
                P.add("v", lambda e, h2=es.h, v=av: e.wait_ge(h2, v))

                if not own:
                    VE(lambda v, B1=B1, B2=B2:
                       v.tensor_tensor_scan(B1[:, :], B2[:, :], B3[:, :],
                                            0.0, ALU.mult, ALU.add))
                    W("v", ve)
                    scanv = VE(lambda v, h=h, j=j, B1=B1:
                               v.tensor_copy(ej[:, h * 3 + j:h * 3 + j + 1],
                                             B1[:, T - 1:T]))
                    gates_free[bi] = [(ve.h, scanv)]
                    prep_scan_guard = [(ve.h, ve.v - 1)]
                    lastf[bi] = [(gp.h, z1v)]
                else:
                    e0 = ej[:, h * 3 + 0:h * 3 + 1]
                    e1 = ej[:, h * 3 + 1:h * 3 + 2]
                    e2 = ej[:, h * 3 + 2:h * 3 + 3]
                    A1c = Aj[:, h * 3 + 1:h * 3 + 2]
                    A2c = Aj[:, h * 3 + 2:h * 3 + 3]
                    VE(lambda v, e0=e0, e1=e1, A1c=A1c:
                       v.scalar_tensor_tensor(cc2[:, :], e0, A1c, e1, ALU.mult, ALU.add))
                    W("v", ve)
                    VE(lambda v, e2=e2, A2c=A2c:
                       v.scalar_tensor_tensor(cc3[:, :], cc2[:, :], A2c, e2, ALU.mult, ALU.add))
                    W("v", ve)
                    VE(lambda v, e0=e0, h=h:
                       v.tensor_single_scalar(carry[:, h:h + 1], e0, msel_s[:, 0:1], ALU.mult))
                    W("v", ve)
                    VE(lambda v, h=h:
                       v.scalar_tensor_tensor(carry[:, h:h + 1], cc2[:, :], msel_s[:, 1:2],
                                              carry[:, h:h + 1], ALU.mult, ALU.add))
                    W("v", ve)
                    VE(lambda v, h=h:
                       v.scalar_tensor_tensor(carry[:, h:h + 1], cc3[:, :], msel_s[:, 2:3],
                                              carry[:, h:h + 1], ALU.mult, ALU.add))
                    W("v", ve)
                    VE(lambda v, h=h, B1=B1, B2=B2:
                       v.tensor_tensor_scan(B1[:, :], B2[:, :], B3[:, :],
                                            carry[:, h:h + 1], ALU.mult, ALU.add))
                    W("v", ve)
                    resv = VE(lambda v, bi=bi, B1=B1:
                              v.tensor_add(B3[:, :], B1[:, :], xt_f[bi][:, :]))
                    prep_scan_guard = [(mm.h, mm.v), (ve.h, resv)]
                    lastf[bi] = [(ve.h, resv)]
                    trl = None
                    for lt in range(NT):
                        idx, ptile, gu = ptf.next()
                        P.add("t", lambda e, h2=ve.h, v=resv: e.wait_ge(h2, v))
                        WH("t", gu)
                        tv = TE(lambda t, lt=lt, pt=ptile:
                                t.transpose(pt[:, :], B3[:, lt * HD:(lt + 1) * HD],
                                            idf[:, :]))
                        P.add("v", lambda e, h2=mm.h, v=tv: e.wait_ge(h2, v))
                        evv = VE(lambda v, lt=lt, h=h, pt=ptile:
                                 v.tensor_copy(x1tm[:, lt, h * HD:(h + 1) * HD],
                                               pt[:, :]))
                        ptf.guard(idx, ve, evv)
                        trl = tv
                    gates_free[bi] = [(ve.h, ve.v), (mm.h, trl)]
        x1_assembled = ve.v
        for lt in range(NT):
            P.add("v", lambda e, h=ve.h, v=x1_assembled: e.wait_ge(h, v))
            VE(lambda v, lt=lt:
               v.reduce_sum(s1_all[:, lt:lt + 1], x1tm[:, lt, :], axis=AX.X))
            P.add("c", lambda e, h=ve.h, v=x1_assembled: e.wait_ge(h, v))
            W("c", es)
            SC(lambda s, lt=lt:
               s.activation(sq_scrA[:, :], x1tm[:, lt, :], AF.Square,
                            accum_out=s2_all[:, lt:lt + 1]))
        W("v", ve)
        W("v", es)
        VE(lambda v: v.tensor_scalar_mul(mu_all[:, :], s1_all[:, :], 1.0 / D))
        W("v", ve)
        VE(lambda v: v.tensor_scalar_mul(rs_all[:, :], s2_all[:, :], 1.0 / D))
        W("v", ve)
        VE(lambda v: v.tensor_mul(s1_all[:, :], mu_all[:, :], mu_all[:, :]))
        W("v", ve)
        VE(lambda v: v.tensor_sub(rs_all[:, :], rs_all[:, :], s1_all[:, :]))
        W("c", ve)
        SC(lambda s: s.activation(rs_all[:, :], rs_all[:, :], AF.Sqrt, bias=eps_c[:, 0:1]))
        W("v", es)
        VE(lambda v: v.reciprocal(rs_all[:, :], rs_all[:, :]))
        rs_done = ve.v
        x1T = x1T_h
        spill_last = Lsp.v
        x1T_trl = None
        for lt in range(NT):
            P.add("v", lambda e, h=ve.h, v=rs_done: e.wait_ge(h, v))
            av = VE(lambda v, lt=lt:
                    v.tensor_scalar(x1tm[:, lt, :], x1tm[:, lt, :],
                                    mu_all[:, lt:lt + 1], rs_all[:, lt:lt + 1],
                                    ALU.subtract, ALU.mult))
            P.add("s", lambda e, h=ve.h, v=av: e.wait_ge(h, v))
            spill_last = dma(Lsp, "s", x1f[lt * HD:(lt + 1) * HD, :], x1tm[:, lt, :])
            for dtk in range(NH):
                idx, ptile, gu = ptb.next()
                P.add("t", lambda e, h=ve.h, v=av: e.wait_ge(h, v))
                WH("t", gu)
                tv = TE(lambda t, lt=lt, dtk=dtk, pt=ptile:
                        t.transpose(pt[:, 0:HD], x1tm[:, lt, dtk * HD:(dtk + 1) * HD],
                                    idb[:, :]))
                x1T_trl = tv
                P.add("v", lambda e, h=mm.h, v=tv: e.wait_ge(h, v))
                evv = VE(lambda v, lt=lt, dtk=dtk, pt=ptile:
                         v.tensor_copy(x1T[:, dtk, lt * HD:(lt + 1) * HD], pt[:, 0:HD]))
                ptb.guard(idx, ve, evv)
        x1T_ready = ve.v

        if "B" not in phases:
            phases_skip_b = True
        for lt in range(NT if "B" in phases else 0):
            idx, ptile, gu = pp.next()
            WH("t", gu)
            if lt == 0:
                W("t", ve, x1T_ready)
            for dtk in range(NH):
                if dtk < NH - 1:
                    TN(lambda t, pt=ptile, lt=lt, dtk=dtk:
                       t.matmul(pt[:, 0:9], x1T[:, dtk, lt * HD:(lt + 1) * HD],
                                wr9_b[:, dtk, :], start=(dtk == 0), stop=False))
                else:
                    mv = TE(lambda t, pt=ptile, lt=lt, dtk=dtk:
                            t.matmul(pt[:, 0:9], x1T[:, dtk, lt * HD:(lt + 1) * HD],
                                     wr9_b[:, dtk, :], start=False, stop=True))
            P.add("v", lambda e, h=mm.h, v=mv: e.wait_ge(h, v))
            VE(lambda v, pt=ptile:
               v.reduce_max(pstat[:, 0:1], pt[:, 0:8], axis=AX.X, negate=True))
            W("c", ve)
            SC(lambda s, pt=ptile:
               s.activation(prob[:, :], pt[:, 0:8], AF.Exp, bias=pstat[:, 0:1],
                            accum_out=pstat[:, 1:2]))
            evv = SC(lambda s, pt=ptile, lt=lt:
                     s.activation(scl[:, lt * NU + 8:lt * NU + 9], pt[:, 8:9], AF.Sigmoid))
            pp.guard(idx, es, evv)
            W("v", es)
            VE(lambda v: v.reciprocal(pstat[:, 2:3], pstat[:, 1:2]))
            W("v", ve)
            VE(lambda v: v.tensor_scalar_mul(prob[:, :], prob[:, :], pstat[:, 2:3]))
            W("v", ve)
            VE(lambda v: v.reduce_max(pstat[:, 0:1], prob[:, :], axis=AX.X))
            W("v", ve)
            VE(lambda v: v.tensor_single_scalar(pmsk[:, :], prob[:, :],
                                                pstat[:, 0:1], ALU.is_ge))
            W("v", ve)
            VE(lambda v: v.scalar_tensor_tensor(pmsk[:, :], pmsk[:, :], -2.0,
                                                prob[:, :], ALU.mult, ALU.add))
            W("v", ve)
            VE(lambda v: v.reduce_max(pstat[:, 3:4], pmsk[:, :], axis=AX.X))
            W("v", ve)
            VE(lambda v: v.tensor_single_scalar(pmsk[:, :], prob[:, :],
                                                pstat[:, 3:4], ALU.is_ge))
            W("v", ve)
            VE(lambda v, lt=lt:
               v.tensor_mul(scl[:, lt * NU:lt * NU + 8], prob[:, :], pmsk[:, :]))
            W("v", ve)
            for r in range(1, 4):
                VE(lambda v, lt=lt, r=r:
                   v.tensor_copy(scl[:, lt * NU + 8 + r:lt * NU + 9 + r],
                                 scl[:, lt * NU + 8:lt * NU + 9]))
        scl_ready = ve.v

        P.add("g", lambda e, h=mm.h, v=x1T_trl: e.wait_ge(h, v))
        P.add("g", lambda e, h=Lsp.h, v=spill_last: e.wait_ge(h, v))
        for _lane in (Lw2s, Lw2h[0], Lw2h[1], Lu1[0], Lu1[1], Lw2[0], Lw2[1]):
            P.add("g", lambda e, h=_lane.h, v=_lane.v: e.wait_ge(h, v))

        units_range = range(NU if "B" in phases else 0)
        wfc_guard = [[], []]
        stage_guard = [[], []]
        prev_wd_reader = None
        prev_accum = None
        prev_silu_reader = None
        for u in units_range:
            wdl = None
            for fc in range(NFC):
                wbi = fc % 2
                if fc == 2:
                    if prev_wd_reader is not None:
                        P.add("g", lambda e, h=mm.h, v=prev_wd_reader: e.wait_ge(h, v))
                    wdl = dma(Lwd, "g", wd_b[:, :, :],
                              wd_all.ap()[u].rearrange("(t p) d -> p t d", p=HD))
                WH("g", wfc_guard[wbi])
                dma(Lwg[wbi], "g", wfc[:, wbi, 0:NH, :],
                    wg_all.ap()[u].rearrange("(t p) f -> p t f", p=HD)
                    [:, :, fc * HD:(fc + 1) * HD])
                l2 = dma(Lwg[wbi], "g", wfc[:, wbi, NH:2 * NH, :],
                         wu_all.ap()[u].rearrange("(t p) f -> p t f", p=HD)
                         [:, :, fc * HD:(fc + 1) * HD])
                tiles = [pp.next() for _ in range(4)]
                for (i_, p_, g_) in tiles:
                    WH("t", g_)
                W("t", Lwg[wbi], l2)
                if u == 0 and fc == 0:
                    W("t", ve, scl_ready)
                gmm = [None] * 4
                for dtk in range(NH):
                    for tc in range(4):
                        p1 = tiles[tc][1]
                        tsl = slice(tc * 512, (tc + 1) * 512)
                        if dtk < NH - 1:
                            TN(lambda t, p1=p1, wbi=wbi, dtk=dtk, tsl=tsl:
                               t.matmul(p1[:, :], wfc[:, wbi, dtk, :], x1T[:, dtk, tsl],
                                        start=(dtk == 0), stop=False))
                        else:
                            gmm[tc] = TE(lambda t, p1=p1, wbi=wbi, dtk=dtk, tsl=tsl:
                                         t.matmul(p1[:, :], wfc[:, wbi, dtk, :],
                                                  x1T[:, dtk, tsl], start=False, stop=True))
                for tc in range(4):
                    P.add("c", lambda e, h=mm.h, v=gmm[tc]: e.wait_ge(h, v))
                    if tc == 0:
                        if prev_silu_reader is not None:
                            P.add("c", lambda e, h=ve.h, v=prev_silu_reader: e.wait_ge(h, v))
                        W("c", es)
                    s1v = SC(lambda s, p1=tiles[tc][1], tc=tc:
                             s.activation(sq_scrB[:, tc * 512:(tc + 1) * 512],
                                          p1[:, :], AF.Silu))
                    pp.guard(tiles[tc][0], es, s1v)
                tiles2 = [pp.next() for _ in range(4)]
                for (i_, p_, g_) in tiles2:
                    WH("t", g_)
                umm = [None] * 4
                for dtk in range(NH):
                    for tc in range(4):
                        p2 = tiles2[tc][1]
                        tsl = slice(tc * 512, (tc + 1) * 512)
                        if dtk < NH - 1:
                            TN(lambda t, p2=p2, wbi=wbi, dtk=dtk, tsl=tsl:
                               t.matmul(p2[:, :], wfc[:, wbi, NH + dtk, :],
                                        x1T[:, dtk, tsl], start=(dtk == 0), stop=False))
                        else:
                            umm[tc] = TE(lambda t, p2=p2, wbi=wbi, dtk=dtk, tsl=tsl:
                                         t.matmul(p2[:, :], wfc[:, wbi, NH + dtk, :],
                                                  x1T[:, dtk, tsl], start=False, stop=True))
                fcmm_last = umm[3]
                hv = None
                for tc in range(4):
                    P.add("v", lambda e, h=mm.h, v=umm[tc]: e.wait_ge(h, v))
                    P.add("v", lambda e, h=es.h, v=es.v: e.wait_ge(h, v))
                    hv = VE(lambda v, p2=tiles2[tc][1], fc=fc, tc=tc:
                            v.tensor_mul(h_e[:, fc, tc * 512:(tc + 1) * 512],
                                         sq_scrB[:, tc * 512:(tc + 1) * 512], p2[:, :]))
                    pp.guard(tiles2[tc][0], ve, hv)
                prev_silu_reader = hv
                wfc_guard[wbi] = [(mm.h, fcmm_last)]
            up_all = ve.v
            for tt in range(NT):
                sbi = tt % 2
                tiles = [pp.next() for _ in range(4)]
                for (i_, p_, g_) in tiles:
                    WH("t", g_)
                if tt == 0:
                    W("t", ve, up_all)
                    W("t", Lwd, wdl)
                dmm = [None] * 4
                for fc in range(NFC):
                    for dc in range(4):
                        pt = tiles[dc][1]
                        dsl = slice(dc * 512, (dc + 1) * 512)
                        if fc < NFC - 1:
                            TN(lambda t, pt=pt, fc=fc, tt=tt, dsl=dsl:
                               t.matmul(pt[:, :], h_e[:, fc, tt * HD:(tt + 1) * HD],
                                        wd_b[:, fc, dsl], start=(fc == 0), stop=False))
                        else:
                            dmm[dc] = TE(lambda t, pt=pt, fc=fc, tt=tt, dsl=dsl:
                                         t.matmul(pt[:, :], h_e[:, fc, tt * HD:(tt + 1) * HD],
                                                  wd_b[:, fc, dsl], start=False, stop=True))
                dn_last = dmm[3]
                for dc in range(4):
                    dsl = slice(dc * 512, (dc + 1) * 512)
                    P.add("v", lambda e, h=mm.h, v=dmm[dc]: e.wait_ge(h, v))
                    WH("v", stage_guard[sbi] if dc == 0 else [])
                    evv = VE(lambda v, pt=tiles[dc][1], sbi=sbi, dsl=dsl, tt=tt, u=u:
                             v.tensor_scalar(stage[:, sbi, dsl], pt[:, :],
                                             scl[:, tt * NU + u:tt * NU + u + 1], None,
                                             ALU.mult))
                    pp.guard(tiles[dc][0], ve, evv)
                P.add("g", lambda e, h=ve.h, v=ve.v: e.wait_ge(h, v))
                if prev_accum is not None:
                    WH("g", prev_accum)
                avd = dma(Lacc[sbi], "g", moe_d[tt * HD:(tt + 1) * HD, :],
                          stage[:, sbi, :], accum=(u > 0))
                stage_guard[sbi] = [(Lacc[sbi].h, avd)]
            prev_accum = [(Lacc[0].h, Lacc[0].v), (Lacc[1].h, Lacc[1].v)]
            prev_wd_reader = dn_last

        moe_done = [(Lacc[0].h, Lacc[0].v), (Lacc[1].h, Lacc[1].v)]
        x2T = x1T
        WH("s", moe_done)
        prev_add = None
        prev_x2_users = []
        for lt in range(NT if "B" in phases else 0):
            if prev_add is not None:
                P.add("s", lambda e, h=ve.h, v=prev_add: e.wait_ge(h, v))
            v1 = dma(Lrl[0], "s", stage[:, 0, :], moe_d[lt * HD:(lt + 1) * HD, :])
            v2 = dma(Lrl[1], "s", x1rb[:, :], x1f[lt * HD:(lt + 1) * HD, :])
            P.add("v", lambda e, h=Lrl[0].h, v=v1: e.wait_ge(h, v))
            P.add("v", lambda e, h=Lrl[1].h, v=v2: e.wait_ge(h, v))
            WH("v", prev_x2_users)
            VE(lambda v: v.tensor_add(x2t_st[:, :], stage[:, 0, :], x1rb[:, :]))
            W("v", ve)
            VE(lambda v: v.reduce_sum(pstat[:, 0:1], x2t_st[:, :], axis=AX.X))
            W("c", ve)
            W("c", es)
            SC(lambda s: s.activation(sq_scrB[:, :], x2t_st[:, :], AF.Square,
                                      accum_out=pstat[:, 1:2]))
            W("v", es)
            VE(lambda v: v.tensor_scalar_mul(pstat[:, 0:1], pstat[:, 0:1], 1.0 / D))
            W("v", ve)
            VE(lambda v: v.tensor_scalar_mul(pstat[:, 1:2], pstat[:, 1:2], 1.0 / D))
            W("v", ve)
            VE(lambda v: v.tensor_mul(pstat[:, 2:3], pstat[:, 0:1], pstat[:, 0:1]))
            W("v", ve)
            VE(lambda v: v.tensor_sub(pstat[:, 1:2], pstat[:, 1:2], pstat[:, 2:3]))
            W("c", ve)
            SC(lambda s: s.activation(pstat[:, 1:2], pstat[:, 1:2], AF.Sqrt, bias=eps_c[:, 0:1]))
            W("v", es)
            VE(lambda v: v.reciprocal(pstat[:, 1:2], pstat[:, 1:2]))
            W("v", ve)
            lnv = VE(lambda v: v.tensor_scalar(x2t_st[:, :], x2t_st[:, :],
                                               pstat[:, 0:1], pstat[:, 1:2],
                                               ALU.subtract, ALU.mult))
            prev_add = lnv
            P.add("s", lambda e, h=ve.h, v=lnv: e.wait_ge(h, v))
            x2dma = dma(Lx2o, "s", x2_out[lt * HD:(lt + 1) * HD, :], x2t_st[:, :])
            users = [(Lx2o.h, x2dma)]
            for dtk in range(NH):
                idx, ptile, gu = ptf.next()
                P.add("t", lambda e, h=ve.h, v=lnv: e.wait_ge(h, v))
                WH("t", gu)
                tv = TE(lambda t, dtk=dtk, pt=ptile:
                        t.transpose(pt[:, :], x2t_st[:, dtk * HD:(dtk + 1) * HD],
                                    idf[:, :]))
                P.add("v", lambda e, h=mm.h, v=tv: e.wait_ge(h, v))
                evv = VE(lambda v, lt=lt, dtk=dtk, pt=ptile:
                         v.tensor_copy(x2T[:, dtk, lt * HD:(lt + 1) * HD], pt[:, :]))
                ptf.guard(idx, ve, evv)
            users.append((mm.h, tv))
            prev_x2_users = users
        x2T_ready = ve.v
        x2_out_done = Lx2o.v

        kh_tm = kh_h
        wq_guard = [[], []]
        P.add("g", lambda e, h=mm.h, v=mm.v: e.wait_ge(h, v))
        bvl = dma(Lwvh, "g", bvr_b[:, :], bv_row[:, :])
        kv0 = VE(lambda v: v.memset(kv_sb[:, :], 0.0))
        qcm_guard = [(Lx2o.h, x2_out_done)]

        def proj_cm(ht, wsrc, bias_s, kind, out_dram, do_kh):
            nonlocal qcm_guard
            bi = ht % 2
            WH("g", wq_guard[bi])
            lv = dma(Lwq[bi], "g", wqk_t[:, bi, :, :],
                     wsrc.ap().rearrange("(t p) h -> p t h", p=HD)
                     [:, :, ht * HD:(ht + 1) * HD])
            lastmm = None
            done = None
            for tc in range(4):
                tsl = slice(tc * 512, (tc + 1) * 512)
                idx, ptile, gu = pp.next()
                WH("t", gu)
                W("t", Lwq[bi], lv)
                if ht == 0 and tc == 0:
                    W("t", ve, x2T_ready)
                for dtk in range(NH):
                    if dtk < NH - 1:
                        TN(lambda t, pt=ptile, bi=bi, dtk=dtk, tsl=tsl:
                           t.matmul(pt[:, :], wqk_t[:, bi, dtk, :], x2T[:, dtk, tsl],
                                    start=(dtk == 0), stop=False))
                    else:
                        mv = TE(lambda t, pt=ptile, bi=bi, dtk=dtk, tsl=tsl:
                                t.matmul(pt[:, :], wqk_t[:, bi, dtk, :], x2T[:, dtk, tsl],
                                         start=False, stop=True))
                lastmm = mv
                if kind == "silu":
                    P.add("c", lambda e, h=mm.h, v=mv: e.wait_ge(h, v))
                    if tc == 0:
                        WH("c", qcm_guard)
                    evv = SC(lambda s, pt=ptile, ht=ht, tsl=tsl:
                             s.activation(qcm_t[:, tsl], pt[:, :], AF.Silu,
                                          bias=bias_s[:, ht:ht + 1]))
                    pp.guard(idx, es, evv)
                    done = (es.h, evv)
                elif "4" in phases:
                    P.add("c", lambda e, h=mm.h, v=mv: e.wait_ge(h, v))
                    if tc == 0:
                        WH("c", qcm_guard)
                    evv = SC(lambda s, pt=ptile, ht=ht, tsl=tsl:
                             s.activation(qcm_t[:, tsl], pt[:, :], AF.Relu,
                                          bias=bias_s[:, ht:ht + 1]))
                    pp.guard(idx, es, evv)
                    done = (es.h, evv)
                else:
                    P.add("c", lambda e, h=mm.h, v=mv: e.wait_ge(h, v))
                    if tc > 0:
                        W("c", ve)
                    rv = SC(lambda s, pt=ptile, ht=ht:
                            s.activation(elu_r[:, :], pt[:, :], AF.Relu,
                                         bias=bias_s[:, ht:ht + 1]))
                    P.add("v", lambda e, h=mm.h, v=mv: e.wait_ge(h, v))
                    W("v", es)
                    if "1" in phases:
                        tv_ = VE(lambda v, pt=ptile: v.tensor_copy(elu_t[:, :], pt[:, :]))
                    else:
                        tv_ = VE(lambda v, pt=ptile, ht=ht:
                                 v.tensor_scalar(elu_t[:, :], pt[:, :], bias_s[:, ht:ht + 1],
                                                 0.0, ALU.add, ALU.min))
                    pp.guard(idx, ve, tv_)
                    pp.guard(idx, es, rv)
                    W("c", ve, tv_)
                    if "2" in phases:
                        ev_ = SC(lambda s: s.activation(elu_t[:, :], elu_t[:, :], AF.Copy))
                    else:
                        ev_ = SC(lambda s: s.activation(elu_t[:, :], elu_t[:, :], AF.Exp))
                    P.add("v", lambda e, h=es.h, v=ev_: e.wait_ge(h, v))
                    P.add("v", lambda e, h=es.h, v=rv: e.wait_ge(h, v))
                    if tc == 0:
                        WH("v", qcm_guard)
                    if "3" in phases:
                        fv = VE(lambda v, tsl=tsl: v.tensor_copy(qcm_t[:, tsl], elu_t[:, :]))
                    else:
                        fv = VE(lambda v, ht=ht, tsl=tsl:
                                v.scalar_tensor_tensor(qcm_t[:, tsl], elu_t[:, :], -1.0,
                                                       elu_r[:, :], ALU.add, ALU.add))
                    done = (ve.h, fv)
            wq_guard[bi] = [(mm.h, lastmm)]
            P.add("s", lambda e, h=done[0], v=done[1]: e.wait_ge(h, v))
            ov = dma(Lqout, "s", out_dram[ht * HD:(ht + 1) * HD, :], qcm_t[:, :])
            users = [(Lqout.h, ov)]
            if do_kh:
                for lt in range(NT):
                    idx, ptile, gu = ptb.next()
                    P.add("t", lambda e, h=done[0], v=done[1]: e.wait_ge(h, v))
                    WH("t", gu)
                    tv2 = TE(lambda t, lt=lt, pt=ptile:
                             t.transpose(pt[:, 0:HD], qcm_t[:, lt * HD:(lt + 1) * HD],
                                         idb[:, :]))
                    P.add("v", lambda e, h=mm.h, v=tv2: e.wait_ge(h, v))
                    if ht == 0 and lt == 0:
                        P.add("v", lambda e, h=Lx2o.h, v=x2_out_done: e.wait_ge(h, v))
                    evv = VE(lambda v, lt=lt, ht=ht, pt=ptile:
                             v.tensor_copy(kh_tm[:, lt, ht * HD:(ht + 1) * HD],
                                           pt[:, 0:HD]))
                    ptb.guard(idx, ve, evv)
                users.append((mm.h, tv2))
            qcm_guard = users

        for ht in range(NH if ("C" in phases or "q" in phases) else 0):
            proj_cm(ht, wqk, bqk_s, "silu" if "E" in phases else "elu", q_out,
                    do_kh=("K" not in phases))
        kh_ready = ve.v
        for ht in range(NH if ("C" in phases or "U" in phases) else 0):
            proj_cm(ht, wu1, bu_s, "silu", u_out, do_kh=False)

        wv_guard = []
        vtt_guard = []
        for hc in range(4 if "C" in phases else 0):
            WH("g", wv_guard)
            lv = dma(Lwvh, "g", wvh[:, :, :],
                     wv.ap().rearrange("(t p) h -> p t h", p=HD)
                     [:, :, hc * 512:(hc + 1) * 512])
            lastmm = None
            for tt in range(NT):
                idx, ptile, gu = pp.next()
                WH("t", gu)
                W("t", Lwvh, lv)
                for dtk in range(NH):
                    TN(lambda t, pt=ptile, dtk=dtk, tt=tt:
                       t.matmul(pt[:, :], x2T[:, dtk, tt * HD:(tt + 1) * HD],
                                wvh[:, dtk, :], start=(dtk == 0), stop=False))
                mv = TE(lambda t, pt=ptile, hc=hc:
                        t.matmul(pt[:, :], onesb[0:1, :],
                                 bvr_b[0:1, hc * 512:(hc + 1) * 512],
                                 start=False, stop=True))
                lastmm = mv
                P.add("c", lambda e, h=mm.h, v=mv: e.wait_ge(h, v))
                WH("c", vtt_guard)
                sv = SC(lambda s, pt=ptile: s.activation(vtt[:, :], pt[:, :], AF.Silu))
                pp.guard(idx, es, sv)
                kvmms = []
                for hh in range(4):
                    habs = hc * 4 + hh
                    i2, pt2, gu2 = pp.next()
                    WH("t", gu2)
                    P.add("t", lambda e, h=es.h, v=sv: e.wait_ge(h, v))
                    if hc == 0 and tt == 0 and hh == 0:
                        W("t", ve, kh_ready)
                    tv3 = TE(lambda t, pt2=pt2, tt=tt, habs=habs, hh=hh:
                             t.matmul(pt2[:, 0:HD],
                                      kh_tm[:, tt, habs * HD:(habs + 1) * HD],
                                      vtt[:, hh * HD:(hh + 1) * HD],
                                      start=True, stop=True))
                    kvmms.append(tv3)
                    P.add("v", lambda e, h=mm.h, v=tv3: e.wait_ge(h, v))
                    if hc == 0 and tt == 0 and hh == 0:
                        W("v", ve, kv0)
                    evv = VE(lambda v, pt2=pt2, habs=habs:
                             v.tensor_add(kv_sb[:, habs * HD:(habs + 1) * HD],
                                          kv_sb[:, habs * HD:(habs + 1) * HD],
                                          pt2[:, 0:HD]))
                    pp.guard(i2, ve, evv)
                vtt_guard = [(mm.h, kvmms[-1])]
            wv_guard = [(mm.h, lastmm)]
        P.add("s", lambda e, h=ve.h, v=ve.v: e.wait_ge(h, v))
        dma(Lkv, "s", kv_out[:, :], kv_sb[:, :])

        for lane in (Lkv, Lqout, Lx2o, Lsp, Lrl[0], Lrl[1], Lacc[0], Lacc[1]):
            W("s", lane)
        for lane in (Lwvh, Lwq[0], Lwq[1], Lwd, Lwg[0], Lwg[1],
                     Lw2[0], Lw2[1], Lu1[0], Lu1[1]):
            W("g", lane)
        for lane in (Lw2s, Lw2h[0], Lw2h[1]):
            W("s", lane)

        with nc.Block() as blk:
            @blk.gpsimd
            def _(g):
                for f in P.ops["g"]:
                    f(g)

            @blk.sync
            def _(s):
                for f in P.ops["s"]:
                    f(s)

            @blk.tensor
            def _(t):
                for f in P.ops["t"]:
                    f(t)

            @blk.vector
            def _(v):
                for f in P.ops["v"]:
                    f(v)

            @blk.scalar
            def _(c):
                for f in P.ops["c"]:
                    f(c)

    return nc


def prep_launch1_inputs(inp):
    f32 = np.float32
    x = np.asarray(inp["L0_attn_x"], f32)
    xT = [np.ascontiguousarray(x[b].T) for b in range(B)]
    rp = np.ascontiguousarray(np.asarray(inp["L0_attn_recurrent_param"], f32)
                              .reshape(NH, HD).T)
    igb_ = np.ascontiguousarray(np.asarray(inp["L0_attn_input_gate_bias"], f32).T)
    rgb_ = np.ascontiguousarray(np.asarray(inp["L0_attn_recurrent_gate_bias"], f32).T)
    igw_ = np.ascontiguousarray(np.asarray(inp["L0_attn_input_gate_weight"], f32))
    rgw_ = np.ascontiguousarray(np.asarray(inp["L0_attn_recurrent_gate_weight"], f32))
    wr9_ = np.ascontiguousarray(
        np.concatenate([np.asarray(inp["L0_ffn_Wr"], f32),
                        np.asarray(inp["L0_ffn_Wsg"], f32)], axis=1))
    sgq = np.asarray(inp["L0_ffn_Sg"], f32).reshape(D, 4, MOE_F).transpose(1, 0, 2)
    suq = np.asarray(inp["L0_ffn_Su"], f32).reshape(D, 4, MOE_F).transpose(1, 0, 2)
    sdq = np.asarray(inp["L0_ffn_Sd"], f32).reshape(4, MOE_F, D)
    wg_ = np.ascontiguousarray(np.concatenate([np.asarray(inp["L0_ffn_Wg"], f32), sgq], 0))
    wu_ = np.ascontiguousarray(np.concatenate([np.asarray(inp["L0_ffn_Wu"], f32), suq], 0))
    wd_ = np.ascontiguousarray(np.concatenate([np.asarray(inp["L0_ffn_Wd"], f32), sdq], 0))
    wqk_ = np.ascontiguousarray(np.asarray(inp["L1_attn_Wqk"], f32))
    bqk_ = np.ascontiguousarray(np.asarray(inp["L1_attn_bqk"], f32).reshape(NH, HD).T)
    wv_ = np.ascontiguousarray(np.asarray(inp["L1_attn_Wv"], f32))
    bvr_ = np.asarray(inp["L1_attn_bv"], f32).reshape(1, D)
    wu1_ = np.ascontiguousarray(np.asarray(inp["L1_attn_Wu"], f32))
    bu_ = np.ascontiguousarray(np.asarray(inp["L1_attn_bu"], f32).reshape(NH, HD).T)
    ident = np.eye(HD, dtype=f32)
    ones = np.ones((1, HD), f32)
    w2rr_ = np.ascontiguousarray(np.asarray(inp["L1_ffn_W2"], f32).T).reshape(64, HD * D)
    h64_ = _sylvester(64)
    h128_ = _sylvester(128)

    maps = []
    for c in range(NC):
        b, q = c // 4, c % 4
        msel_ = np.zeros((HD, 5), f32)
        if q >= 1:
            msel_[:, q - 1] = 1.0
        msel_[:, 3] = 1.0 if q == 0 else 0.0
        msel_[:, 4] = 0.0 if q == 0 else 1.0
        maps.append({
            "xTpre": xT[b][:, 0:3 * T],
            "xTown": np.ascontiguousarray(xT[b][:, q * T:(q + 1) * T]),
            "rp": rp, "igw": igw_, "igb": igb_, "rgw": rgw_, "rgb": rgb_,
            "msel": msel_, "wr9": wr9_,
            "wg_all": wg_, "wu_all": wu_, "wd_all": wd_,
            "wqk": wqk_, "bqk": bqk_, "wv": wv_, "bv_row": bvr_,
            "wu1": wu1_, "bu": bu_, "ident": ident, "ones_in": ones,
            "w2rr": w2rr_, "h64": h64_, "h128": h128_,
        })
    return maps





def build_launch2():
    nc = bass.Bass("TRN2", target_bir_lowering=False)
    dt = nc.dram_tensor
    NA = 64
    FLAT = HD * D

    x2i = dt("x2i", [T, D], F32, kind="ExternalInput")
    qin = dt("qin", [D, T], BF16, kind="ExternalInput")
    uin = dt("uin", [D, T], BF16, kind="ExternalInput")
    kva = dt("kva", [4, HD, NH * HD], F32, kind="ExternalInput")
    wo = dt("wo", [D, D], F32, kind="ExternalInput")
    bo_row = dt("bo_row", [1, D], F32, kind="ExternalInput")
    lng = dt("lng", [HD, NH], F32, kind="ExternalInput")
    lnb = dt("lnb", [HD, NH], F32, kind="ExternalInput")
    w1t = dt("w1t", [D, FFN_H], F32, kind="ExternalInput")
    w3t = dt("w3t", [D, FFN_H], F32, kind="ExternalInput")
    w2h = dt("w2h", [FFN_H, D], BF16, kind="ExternalInput")
    ident = dt("ident", [HD, HD], F32, kind="ExternalInput")
    ones_in = dt("ones_in", [1, HD], F32, kind="ExternalInput")
    ones_col = dt("ones_col", [HD, 1], F32, kind="ExternalInput")

    out_o = dt("out_o", [T, D], F32, kind="ExternalOutput")

    x3f = dt("x3f", [T, D], F32, kind="Internal")
    ffd = dt("ffd", [T, D], F32, kind="Internal")

    P = Prog()
    ctx = ExitStack()
    with ctx:
        mm = Sem(ctx, nc, "mm")
        ve = Sem(ctx, nc, "ve")
        es = Sem(ctx, nc, "es")
        gp = Sem(ctx, nc, "gp")
        Lc = Sem(ctx, nc, "Lc")
        Lkv = Sem(ctx, nc, "Lkv")
        Lwo = Sem(ctx, nc, "Lwo")
        Lq = [Sem(ctx, nc, "Lq0"), Sem(ctx, nc, "Lq1")]
        Lx2 = Sem(ctx, nc, "Lx2")
        Lsp = Sem(ctx, nc, "Lsp")
        Lx3r = Sem(ctx, nc, "Lx3r")
        Lw13 = [Sem(ctx, nc, "Lw13_0"), Sem(ctx, nc, "Lw13_1")]
        Lw2hr = Sem(ctx, nc, "Lw2hr")
        Lffa = [Sem(ctx, nc, "Lffa0"), Sem(ctx, nc, "Lffa1")]
        Lrl = [Sem(ctx, nc, "Lrl0"), Sem(ctx, nc, "Lrl1")]
        Lout = Sem(ctx, nc, "Lout")

        def sb(name, shape, dtp):
            return nc.alloc_sbuf_tensor(name, shape, dtp)

        idf = sb("idf", [HD, HD], F32)
        onesb = sb("onesb", [1, HD], BF16)
        onesf = sb("onesf", [1, HD], F32)
        onescol_b = sb("onescol_b", [HD, 1], BF16)
        lng_s = sb("lng_s", [HD, NH], F32)
        lnb_s = sb("lnb_s", [HD, NH], F32)
        pst = sb("pst", [HD, 4], F32)
        eps_c = sb("eps_c", [HD, 1], F32)
        sq_b = sb("sq_b", [HD, 512], BF16)
        scr512 = sb("scr512", [HD, 512], F32)

        R0h = nc.alloc_sbuf_tensor("R0", [HD, 32768], F32)
        R1h = nc.alloc_sbuf_tensor("R1", [HD, 16384], F32)
        A0 = nc.lookup_mloc(R0h).addr
        A1 = nc.lookup_mloc(R1h).addr

        def carve(name, shape, dtp, base, off):
            return nc.alloc_sbuf_tensor_at(name, shape, dtp, offset=base + off)

        wo_b = carve("wo_b", [HD, NH, D], BF16, A0, 0)
        mu_b = carve("mu_b", [HD, D], BF16, A0, 65536)
        rs_b = carve("rs_b", [HD, D], BF16, A0, 69632)
        qu_t = carve("qu_t", [HD, 2, D], BF16, A0, 81920)
        x2_t = carve("x2_t", [HD, D], F32, A0, 90112)
        x3_st = carve("x3_st", [HD, D], F32, A0, 98304)
        n_scr = carve("n_scr", [HD, D], F32, A0, 106496)
        s1_scr = carve("s1_scr", [HD, D], F32, A0, 114688)
        kvA = carve("kvA", [HD, NH * HD], F32, A0, 98304)
        kv_b = carve("kv_b", [HD, NH * HD], BF16, A0, 122880)
        bo_b = carve("bo_b", [1, D], BF16, A0, 126976)
        s_sum = carve("s_sum", [1, D], F32, A0, 0)
        s_sq = carve("s_sq", [1, D], F32, A0, 8192)
        mu_row = s_sum
        rs_row = s_sq
        kvB = carve("kvB", [HD, NH * HD], F32, A0, 106496)
        ocm = carve("ocm", [HD, NH, D], BF16, A1, 0)
        x3T = carve("x3T", [HD, NH, D], BF16, A1, 0)
        hidden = carve("hidden", [HD, NH, 1024], BF16, A0, 0)
        w2h_hb = carve("w2h_hb", [HD, NH, D], BF16, A0, 32768)
        w13_t = carve("w13_t", [HD, 2, 2, NH, HD], BF16, A0, 98304)
        fstage = carve("fstage", [HD, 2, D], F32, A0, 114688)


        pp = Pool([nc.alloc_psum_tensor(f"pp{i}", [HD, 512], F32) for i in range(4)])
        ptf = Pool([nc.alloc_psum_tensor(f"ptf{i}", [HD, HD], F32) for i in range(2)])
        ps_sum = nc.alloc_psum_tensor("ps_sum", [1, 512], F32)
        ps_sq = nc.alloc_psum_tensor("ps_sq", [1, 512], F32)

        def W(key, sem, val=None):
            v = sem.v if val is None else val
            P.add(key, lambda e, h=sem.h, v=v: e.wait_ge(h, v))

        def WH(key, hv_list):
            for (h, v) in hv_list:
                P.add(key, lambda e, h=h, v=v: e.wait_ge(h, v))

        def dma(lane, key, out, in_, accum=False):
            lane.v += 16
            def f(e, out=out, in_=in_, accum=accum, h=lane.h):
                if accum:
                    e.dma_start(out=out, in_=in_, accum_op=ALU.add).then_inc(h, 16)
                else:
                    e.dma_start(out=out, in_=in_).then_inc(h, 16)
            P.add(key, f)
            return lane.v

        def SC(fn):
            es.v += 1
            P.add("c", lambda e, fn=fn, h=es.h: fn(e).then_inc(h, 1))
            return es.v

        def VE(fn):
            ve.v += 1
            P.add("v", lambda e, fn=fn, h=ve.h: fn(e).then_inc(h, 1))
            return ve.v

        def TE(fn):
            mm.v += 1
            P.add("t", lambda e, fn=fn, h=mm.h: fn(e).then_inc(h, 1))
            return mm.v

        def TN(fn):
            P.add("t", fn)

        def GP(fn):
            gp.v += 1
            P.add("g", lambda e, fn=fn, h=gp.h: fn(e).then_inc(h, 1))
            return gp.v

        dma(Lc, "g", idf[:, :], ident[:, :])
        dma(Lc, "g", onesb[:, :], ones_in[:, :])
        dma(Lc, "g", onesf[:, :], ones_in[:, :])
        dma(Lc, "g", onescol_b[:, :], ones_col[:, :])
        dma(Lc, "g", bo_b[:, :], bo_row[:, :])
        dma(Lc, "g", lng_s[:, :], lng[:, :])
        c0 = dma(Lc, "g", lnb_s[:, :], lnb[:, :])
        VE(lambda v: v.memset(eps_c[:, :], EPS))
        v1 = dma(Lkv, "s", kvA[:, :], kva.ap()[0])
        v2 = dma(Lkv, "s", kvB[:, :], kva.ap()[1])
        P.add("v", lambda e, h=Lkv.h, v=v2: e.wait_ge(h, v))
        a1 = VE(lambda v: v.tensor_add(kvA[:, :], kvA[:, :], kvB[:, :]))
        P.add("s", lambda e, h=ve.h, v=a1: e.wait_ge(h, v))
        v3 = dma(Lkv, "s", kvB[:, :], kva.ap()[2])
        P.add("v", lambda e, h=Lkv.h, v=v3: e.wait_ge(h, v))
        a2 = VE(lambda v: v.tensor_add(kvA[:, :], kvA[:, :], kvB[:, :]))
        P.add("s", lambda e, h=ve.h, v=a2: e.wait_ge(h, v))
        v4 = dma(Lkv, "s", kvB[:, :], kva.ap()[3])
        P.add("v", lambda e, h=Lkv.h, v=v4: e.wait_ge(h, v))
        VE(lambda v: v.tensor_add(kvA[:, :], kvA[:, :], kvB[:, :]))
        W("v", ve)
        VE(lambda v: v.tensor_copy(kv_b[:, :], kvA[:, :]))
        kvb_done = ve.v
        P.add("t", lambda e, h=Lc.h, v=Lc.v: e.wait_ge(h, v))

        qg = [[], []]
        for ht in range(NH):
            qbi = ht % 2
            WH("s", qg[qbi])
            lv = dma(Lq[qbi], "s", qu_t[:, qbi, :], qin[ht * HD:(ht + 1) * HD, :])
            for tc in range(4):
                tsl = slice(tc * 512, (tc + 1) * 512)
                idx, ptile, gu = pp.next()
                WH("t", gu)
                W("t", Lq[qbi], lv)
                if ht == 0 and tc == 0:
                    W("t", ve, kvb_done)
                mv = TE(lambda t, pt=ptile, ht=ht, qbi=qbi, tsl=tsl:
                        t.matmul(pt[:, :], kv_b[:, ht * HD:(ht + 1) * HD],
                                 qu_t[:, qbi, tsl], start=True, stop=True))
                P.add("v", lambda e, h=mm.h, v=mv: e.wait_ge(h, v))
                evv = VE(lambda v, pt=ptile, ht=ht, tsl=tsl:
                         v.tensor_copy(ocm[:, ht, tsl], pt[:, :]))
                pp.guard(idx, ve, evv)
            qg[qbi] = [(mm.h, mm.v)]
        ocm_done = ve.v

        sq_mm_prev = None
        stats_evac = None
        for tc in range(4):
            tsl = slice(tc * 512, (tc + 1) * 512)
            if stats_evac is not None:
                P.add("t", lambda e, h=ve.h, v=stats_evac: e.wait_ge(h, v))
            for ht in range(NH):
                P.add("c", lambda e, h=ve.h, v=ocm_done: e.wait_ge(h, v))
                if sq_mm_prev is not None:
                    P.add("c", lambda e, h=mm.h, v=sq_mm_prev: e.wait_ge(h, v))
                sv = SC(lambda s, ht=ht, tsl=tsl:
                        s.activation(sq_b[:, :], ocm[:, ht, tsl], AF.Square))
                if ht == 0 and tc == 0:
                    P.add("t", lambda e, h=ve.h, v=ocm_done: e.wait_ge(h, v))
                TN(lambda t, ht=ht, tsl=tsl, first=(ht == 0):
                   t.matmul(ps_sum[:, :], onescol_b[:, :], ocm[:, ht, tsl],
                            start=first, stop=(ht == NH - 1)))
                P.add("t", lambda e, h=es.h, v=sv: e.wait_ge(h, v))
                mv = TE(lambda t, ht=ht, first=(ht == 0):
                        t.matmul(ps_sq[:, :], onescol_b[:, :], sq_b[:, :],
                                 start=first, stop=(ht == NH - 1)))
                sq_mm_prev = mv
            smv = sq_mm_prev
            P.add("v", lambda e, h=mm.h, v=smv: e.wait_ge(h, v))
            VE(lambda v, tsl=tsl: v.tensor_copy(s_sum[0:1, tsl], ps_sum[0:1, :]))
            W("v", ve)
            VE(lambda v, tsl=tsl: v.tensor_copy(s_sq[0:1, tsl], ps_sq[0:1, :]))
            W("v", ve)
            stats_evac = ve.v
        VE(lambda v: v.tensor_scalar_mul(mu_row[:, :], s_sum[:, :], 1.0 / D))
        W("v", ve)
        VE(lambda v: v.tensor_scalar_mul(rs_row[:, :], s_sq[:, :], 1.0 / D))
        W("v", ve)
        VE(lambda v: v.tensor_mul(n_scr[0:1, :], mu_row[:, :], mu_row[:, :]))
        W("v", ve)
        VE(lambda v: v.tensor_sub(rs_row[:, :], rs_row[:, :], n_scr[0:1, :]))
        W("c", ve)
        SC(lambda s: s.activation(rs_row[:, :], rs_row[:, :], AF.Ln,
                                  bias=eps_c[0:1, 0:1]))
        W("c", es)
        SC(lambda s: s.activation(rs_row[:, :], rs_row[:, :], AF.Exp, scale=-0.5))
        rows_done = es.v
        P.add("v", lambda e, h=es.h, v=rows_done: e.wait_ge(h, v))
        ve.v += 0
        for tc in range(4):
            tsl = slice(tc * 512, (tc + 1) * 512)
            for (rowt, dstt) in ((mu_row, mu_b), (rs_row, rs_b)):
                idx, ptile, gu = pp.next()
                WH("t", gu)
                P.add("t", lambda e, h=es.h, v=rows_done: e.wait_ge(h, v))
                mv = TE(lambda t, pt=ptile, rowt=rowt, tsl=tsl:
                        t.matmul(pt[:, :], onesf[0:1, :], rowt[0:1, tsl],
                                 start=True, stop=True))
                P.add("v", lambda e, h=mm.h, v=mv: e.wait_ge(h, v))
                evv = VE(lambda v, pt=ptile, dstt=dstt, tsl=tsl:
                         v.tensor_copy(dstt[:, tsl], pt[:, :]))
                pp.guard(idx, ve, evv)
        bcast_done = ve.v
        P.add("g", lambda e, h=mm.h, v=mm.v: e.wait_ge(h, v))
        wol = dma(Lwo, "g", wo_b[:, :, :], wo.ap().rearrange("(t p) d -> p t d", p=HD))

        ug2 = [list(qg[0]), list(qg[1])]
        nscb = [carve("nscb0", [HD, D], BF16, A0, 114688),
                carve("nscb1", [HD, D], BF16, A0, 118784)]
        nsc = nscb
        nsc_guard = [[], []]
        prev_fv = []
        for ht in range(NH):
            ubi = ht % 2
            WH("s", ug2[ubi])
            lv = dma(Lq[ubi], "s", qu_t[:, ubi, :], uin[ht * HD:(ht + 1) * HD, :])
            if ht == 0:
                P.add("g", lambda e, h=ve.h, v=bcast_done: e.wait_ge(h, v))
            WH("g", nsc_guard[ubi])
            GP(lambda g, ht=ht, ubi=ubi:
               g.tensor_sub(nsc[ubi][:, :], ocm[:, ht, :], mu_b[:, :]))
            P.add("g", lambda e, h=gp.h, v=gp.v: e.wait_ge(h, v))
            gsub = GP(lambda g, ubi=ubi:
                      g.tensor_mul(nsc[ubi][:, :], nsc[ubi][:, :], rs_b[:, :]))
            P.add("v", lambda e, h=gp.h, v=gsub: e.wait_ge(h, v))
            P.add("v", lambda e, h=Lq[ubi].h, v=lv: e.wait_ge(h, v))
            W("v", ve)
            VE(lambda v, ht=ht, ubi=ubi:
               v.tensor_scalar(nsc[ubi][:, :], nsc[ubi][:, :], lng_s[:, ht:ht + 1],
                               lnb_s[:, ht:ht + 1], ALU.mult, ALU.add))
            W("v", ve)
            fv = VE(lambda v, ht=ht, ubi=ubi:
                    v.tensor_mul(ocm[:, ht, :], qu_t[:, ubi, :], nsc[ubi][:, :]))
            nsc_guard[ubi] = [(ve.h, fv)]
            ug2[ubi] = [(ve.h, fv)]
        scm_done = ve.v

        x2g = []
        prev_ln = None
        for tt in range(NT):
            WH("s", x2g)
            xlv = dma(Lx2, "s", x2_t[:, :], x2i[tt * HD:(tt + 1) * HD, :])
            dn = None
            for dc in range(4):
                dsl = slice(dc * 512, (dc + 1) * 512)
                idx, ptile, gu = pp.next()
                WH("t", gu)
                if tt == 0 and dc == 0:
                    W("t", ve, scm_done)
                    W("t", Lwo, wol)
                for ht in range(NH):
                    TN(lambda t, pt=ptile, ht=ht, tt=tt, dsl=dsl, first=(ht == 0):
                       t.matmul(pt[:, :], ocm[:, ht, tt * HD:(tt + 1) * HD],
                                wo_b[:, ht, dsl], start=first, stop=False))
                mv = TE(lambda t, pt=ptile, dsl=dsl:
                        t.matmul(pt[:, :], onesb[0:1, :], bo_b[0:1, dsl],
                                 start=False, stop=True))
                dn = mv
                P.add("v", lambda e, h=mm.h, v=mv: e.wait_ge(h, v))
                P.add("v", lambda e, h=Lx2.h, v=xlv: e.wait_ge(h, v))
                if prev_ln is not None and dc == 0:
                    P.add("v", lambda e, h=Lsp.h, v=prev_ln: e.wait_ge(h, v))
                evv = VE(lambda v, pt=ptile, dsl=dsl:
                         v.tensor_add(x3_st[:, dsl], pt[:, :], x2_t[:, dsl]))
                pp.guard(idx, ve, evv)
            x2g = [(ve.h, ve.v)]
            W("v", ve)
            VE(lambda v: v.reduce_sum(pst[:, 0:1], x3_st[:, :], axis=AX.X))
            W("c", ve)
            W("c", es)
            SC(lambda s: s.activation(n_scr[:, :], x3_st[:, :], AF.Square,
                                      accum_out=pst[:, 1:2]))
            W("v", es)
            VE(lambda v: v.tensor_scalar_mul(pst[:, 0:1], pst[:, 0:1], 1.0 / D))
            W("v", ve)
            VE(lambda v: v.tensor_scalar_mul(pst[:, 1:2], pst[:, 1:2], 1.0 / D))
            W("v", ve)
            VE(lambda v: v.tensor_mul(pst[:, 2:3], pst[:, 0:1], pst[:, 0:1]))
            W("v", ve)
            VE(lambda v: v.tensor_sub(pst[:, 1:2], pst[:, 1:2], pst[:, 2:3]))
            W("c", ve)
            SC(lambda s: s.activation(pst[:, 1:2], pst[:, 1:2], AF.Sqrt,
                                      bias=eps_c[:, 0:1]))
            W("v", es)
            VE(lambda v: v.reciprocal(pst[:, 1:2], pst[:, 1:2]))
            W("v", ve)
            lnv = VE(lambda v: v.tensor_scalar(x3_st[:, :], x3_st[:, :], pst[:, 0:1],
                                               pst[:, 1:2], ALU.subtract, ALU.mult))
            P.add("s", lambda e, h=ve.h, v=lnv: e.wait_ge(h, v))
            prev_ln = dma(Lsp, "s", x3f[tt * HD:(tt + 1) * HD, :], x3_st[:, :])

        x3_all = Lsp.v
        x3g = []
        for tt in range(NT):
            WH("s", x3g)
            if tt == 0:
                P.add("s", lambda e, h=Lsp.h, v=x3_all: e.wait_ge(h, v))
            rlv = dma(Lx3r, "s", x3_st[:, :], x3f[tt * HD:(tt + 1) * HD, :])
            trl = None
            for dtk in range(NH):
                idx, ptile, gu = ptf.next()
                P.add("t", lambda e, h=Lx3r.h, v=rlv: e.wait_ge(h, v))
                if tt == 0 and dtk == 0:
                    W("t", ve, scm_done)
                WH("t", gu)
                tv = TE(lambda t, pt=ptile, dtk=dtk:
                        t.transpose(pt[:, :], x3_st[:, dtk * HD:(dtk + 1) * HD],
                                    idf[:, :]))
                trl = tv
                P.add("v", lambda e, h=mm.h, v=tv: e.wait_ge(h, v))
                evv = VE(lambda v, pt=ptile, tt=tt, dtk=dtk:
                         v.tensor_copy(x3T[:, dtk, tt * HD:(tt + 1) * HD], pt[:, :]))
                ptf.guard(idx, ve, evv)
            x3g = [(mm.h, trl)]
        x3T_done = ve.v

        P.add("g", lambda e, h=ve.h, v=ve.v: e.wait_ge(h, v))
        P.add("g", lambda e, h=es.h, v=es.v: e.wait_ge(h, v))
        P.add("g", lambda e, h=mm.h, v=mm.v: e.wait_ge(h, v))
        w13g = [[], []]
        prev_hv6 = None
        ffa_guard = [[], []]
        w2hr_guard = []
        prev_hb_acc = None
        first6 = True
        def w13_load(hidx, wbi):
            WH("g", w13g[wbi])
            dma(Lw13[wbi], "g", w13_t[:, wbi, 0, :, :],
                w1t.ap().rearrange("(t p) h -> p t h", p=HD)
                [:, :, hidx * HD:(hidx + 1) * HD])
            return dma(Lw13[wbi], "g", w13_t[:, wbi, 1, :, :],
                       w3t.ap().rearrange("(t p) h -> p t h", p=HD)
                       [:, :, hidx * HD:(hidx + 1) * HD])

        for tc2 in range(2):
            tbase = tc2 * 1024
            for hb in range(4):
                w13_lv = {0: w13_load(hb * NH + 0, 0), 1: w13_load(hb * NH + 1, 1)}
                WH("g", w2hr_guard)
                w2l = None
                for ht8 in range(NH):
                    hidx = hb * NH + ht8
                    w2l = dma(Lw2hr, "g", w2h_hb[:, ht8, :],
                              w2h.ap()[hidx * HD:(hidx + 1) * HD, :])
                up_last = None
                for ht8 in range(NH):
                    hidx = hb * NH + ht8
                    wbi = ht8 % 2
                    lv = w13_lv[ht8]
                    for th in range(2):
                        tsl = slice(tbase + th * 512, tbase + (th + 1) * 512)
                        i1, p1, g1 = pp.next()
                        i2, p2, g2 = pp.next()
                        WH("t", g1)
                        WH("t", g2)
                        W("t", Lw13[wbi], lv)
                        if first6:
                            W("t", ve, x3T_done)
                            first6 = False
                        for dtk in range(NH):
                            TN(lambda t, p1=p1, wbi=wbi, dtk=dtk, tsl=tsl:
                               t.matmul(p1[:, :], w13_t[:, wbi, 0, dtk, :],
                                        x3T[:, dtk, tsl], start=(dtk == 0),
                                        stop=(dtk == NH - 1)))
                        for dtk in range(NH):
                            if dtk < NH - 1:
                                TN(lambda t, p2=p2, wbi=wbi, dtk=dtk, tsl=tsl:
                                   t.matmul(p2[:, :], w13_t[:, wbi, 1, dtk, :],
                                            x3T[:, dtk, tsl], start=(dtk == 0), stop=False))
                            else:
                                mv = TE(lambda t, p2=p2, wbi=wbi, dtk=dtk, tsl=tsl:
                                        t.matmul(p2[:, :], w13_t[:, wbi, 1, dtk, :],
                                                 x3T[:, dtk, tsl], start=False, stop=True))
                        up_last = mv
                        P.add("c", lambda e, h=mm.h, v=mv: e.wait_ge(h, v))
                        W("c", es)
                        if prev_hv6 is not None:
                            P.add("c", lambda e, h=ve.h, v=prev_hv6: e.wait_ge(h, v))
                        s1v = SC(lambda s, p1=p1: s.activation(scr512[:, :], p1[:, :], AF.Silu))
                        pp.guard(i1, es, s1v)
                        P.add("v", lambda e, h=es.h, v=s1v: e.wait_ge(h, v))
                        hv = VE(lambda v, p2=p2, ht8=ht8, th=th:
                                v.tensor_mul(hidden[:, ht8, th * 512:(th + 1) * 512],
                                             scr512[:, :], p2[:, :]))
                        pp.guard(i2, ve, hv)
                        prev_hv6 = hv
                    w13g[wbi] = [(mm.h, up_last)]
                    if ht8 + 2 <= NH - 1:
                        w13_lv[ht8 + 2] = w13_load(hb * NH + ht8 + 2, (ht8 + 2) % 2)
                hid_done = ve.v
                dn_last = None
                for tt2 in range(8):
                    sbi = tt2 % 2
                    for dc in range(4):
                        dsl = slice(dc * 512, (dc + 1) * 512)
                        idx, ptile, gu = pp.next()
                        WH("t", gu)
                        if tt2 == 0 and dc == 0:
                            W("t", ve, hid_done)
                            W("t", Lw2hr, w2l)
                        for ht8 in range(NH):
                            if ht8 < NH - 1:
                                TN(lambda t, pt=ptile, ht8=ht8, tt2=tt2, dsl=dsl:
                                   t.matmul(pt[:, :],
                                            hidden[:, ht8, tt2 * HD:(tt2 + 1) * HD],
                                            w2h_hb[:, ht8, dsl], start=(ht8 == 0),
                                            stop=False))
                            else:
                                mv = TE(lambda t, pt=ptile, ht8=ht8, tt2=tt2, dsl=dsl:
                                        t.matmul(pt[:, :],
                                                 hidden[:, ht8, tt2 * HD:(tt2 + 1) * HD],
                                                 w2h_hb[:, ht8, dsl], start=False,
                                                 stop=True))
                        dn_last = mv
                        P.add("v", lambda e, h=mm.h, v=mv: e.wait_ge(h, v))
                        WH("v", ffa_guard[sbi] if dc == 0 else [])
                        evv = VE(lambda v, pt=ptile, sbi=sbi, dsl=dsl:
                                 v.tensor_copy(fstage[:, sbi, dsl], pt[:, :]))
                        pp.guard(idx, ve, evv)
                    P.add("g", lambda e, h=ve.h, v=ve.v: e.wait_ge(h, v))
                    if prev_hb_acc is not None and hb > 0:
                        WH("g", prev_hb_acc)
                    row = tc2 * 1024 + tt2 * HD
                    av = dma(Lffa[sbi], "g", ffd[row:row + HD, :], fstage[:, sbi, :],
                             accum=(hb > 0))
                    ffa_guard[sbi] = [(Lffa[sbi].h, av)]
                prev_hb_acc = [(Lffa[0].h, Lffa[0].v), (Lffa[1].h, Lffa[1].v)]
                w2hr_guard = [(mm.h, dn_last)]
            prev_hb_acc = None

        fst2 = carve("fst2", [HD, 2, D], F32, A0, 0)
        x3s2 = carve("x3s2", [HD, D], F32, A0, 32768)
        pst2 = sb("pst2", [HD, 4], F32)
        ffd_done = [(Lffa[0].h, Lffa[0].v), (Lffa[1].h, Lffa[1].v)]
        Lr2 = [Sem(ctx, nc, "Lr2_0"), Sem(ctx, nc, "Lr2_1")]
        Lo2 = [Lout, Sem(ctx, nc, "LoutB")]
        tile_guard = [[], []]
        p7io = {}

        last_add = {0: None, 1: None}
        last_out = {0: None, 1: None}

        def p7_load(tt2):
            if tt2 >= NT:
                return
            pi2 = tt2 % 2
            fs2 = fstage if pi2 == 0 else fst2
            if last_add[pi2] is not None:
                P.add("g", lambda e, h=gp.h, v=last_add[pi2]: e.wait_ge(h, v))
            a1 = dma(Lrl[pi2], "g", fs2[:, 0, :], ffd[tt2 * HD:(tt2 + 1) * HD, :])
            a2 = dma(Lr2[pi2], "g", fs2[:, 1, :], x3f[tt2 * HD:(tt2 + 1) * HD, :])
            p7io[tt2] = (a1, a2)

        for tt in range(NT):
            pi = tt % 2
            fs = fstage if pi == 0 else fst2
            xs = x3_st if pi == 0 else x3s2
            ps_ = pst if pi == 0 else pst2
            if tt == 0:
                P.add("g", lambda e, h=mm.h, v=mm.v: e.wait_ge(h, v))
                WH("g", ffd_done)
                p7_load(0)
                p7_load(1)
            f1, f2 = p7io[tt]
            P.add("g", lambda e, h=Lrl[pi].h, v=f1: e.wait_ge(h, v))
            P.add("g", lambda e, h=Lr2[pi].h, v=f2: e.wait_ge(h, v))
            if last_out[pi] is not None:
                P.add("g", lambda e, h=Lo2[pi].h, v=last_out[pi]: e.wait_ge(h, v))
            addv = GP(lambda g, fs=fs, xs=xs:
                      g.tensor_add(xs[:, :], fs[:, 0, :], fs[:, 1, :]))
            last_add[pi] = addv
            p7_load(tt + 2)
            P.add("v", lambda e, h=gp.h, v=addv: e.wait_ge(h, v))
            W("v", ve)
            VE(lambda v, xs=xs, ps_=ps_:
               v.reduce_sum(ps_[:, 0:1], xs[:, :], axis=AX.X))
            P.add("c", lambda e, h=gp.h, v=addv: e.wait_ge(h, v))
            W("c", es)
            sqv = SC(lambda s, xs=xs, ps_=ps_:
                     s.activation(n_scr[:, :], xs[:, :], AF.Square,
                                  accum_out=ps_[:, 1:2]))
            P.add("v", lambda e, h=es.h, v=sqv: e.wait_ge(h, v))
            W("v", ve)
            VE(lambda v, ps_=ps_:
               v.tensor_scalar(ps_[:, 0:2], ps_[:, 0:2], 1.0 / D, None, ALU.mult))
            W("v", ve)
            VE(lambda v, ps_=ps_:
               v.tensor_mul(ps_[:, 2:3], ps_[:, 0:1], ps_[:, 0:1]))
            W("v", ve)
            VE(lambda v, ps_=ps_:
               v.tensor_sub(ps_[:, 1:2], ps_[:, 1:2], ps_[:, 2:3]))
            W("c", ve)
            SC(lambda s, ps_=ps_:
               s.activation(ps_[:, 1:2], ps_[:, 1:2], AF.Sqrt, bias=eps_c[:, 0:1]))
            W("v", es)
            VE(lambda v, ps_=ps_: v.reciprocal(ps_[:, 1:2], ps_[:, 1:2]))
            W("v", ve)
            apv = VE(lambda v, xs=xs, ps_=ps_:
                     v.tensor_scalar(xs[:, :], xs[:, :], ps_[:, 0:1], ps_[:, 1:2],
                                     ALU.subtract, ALU.mult))
            P.add("s", lambda e, h=ve.h, v=apv: e.wait_ge(h, v))
            ov = dma(Lo2[pi], "s", out_o[tt * HD:(tt + 1) * HD, :], xs[:, :])
            last_out[pi] = ov
        for lane in (Lout, Lo2[1], Lrl[0], Lrl[1], Lr2[0], Lr2[1], Lsp, Lx3r,
                     Lx2, Lq[0], Lq[1], Lkv):
            W("s", lane)
        for lane in (Lffa[0], Lffa[1], Lw13[0], Lw13[1], Lw2hr, Lwo, Lc):
            W("g", lane)

        with nc.Block() as blk:
            @blk.gpsimd
            def _(g):
                for f in P.ops["g"]:
                    f(g)

            @blk.sync
            def _(s):
                for f in P.ops["s"]:
                    f(s)

            @blk.tensor
            def _(t):
                for f in P.ops["t"]:
                    f(t)

            @blk.vector
            def _(v):
                for f in P.ops["v"]:
                    f(v)

            @blk.scalar
            def _(c):
                for f in P.ops["c"]:
                    f(c)

    return nc


def _sylvester(k):
    H = np.array([[1.0]], np.float32)
    while H.shape[0] < k:
        H = np.block([[H, H], [H, -H]]).astype(np.float32)
    return H


def prep_launch2_inputs(inp, res1):
    f32 = np.float32
    wo_ = np.ascontiguousarray(np.asarray(inp["L1_attn_Wo"], f32))
    bo_ = np.asarray(inp["L1_attn_bo"], f32).reshape(1, D)
    lng_ = np.ascontiguousarray(np.asarray(inp["L1_attn_ln_g"], f32).reshape(NH, HD).T)
    lnb_ = np.ascontiguousarray(np.asarray(inp["L1_attn_ln_b"], f32).reshape(NH, HD).T)
    w1t_ = np.ascontiguousarray(np.asarray(inp["L1_ffn_W1"], f32).T)
    w3t_ = np.ascontiguousarray(np.asarray(inp["L1_ffn_W3"], f32).T)
    ident = np.eye(HD, dtype=f32)
    ones = np.ones((1, HD), f32)
    onesc = np.ones((HD, 1), f32)
    maps = []
    for c in range(NC):
        g0 = (c // 4) * 4
        kva_ = np.stack([np.asarray(res1[g0 + j]["kv_out"]) for j in range(4)])
        maps.append({
            "x2i": np.asarray(res1[c]["x2_out"]),
            "qin": np.asarray(res1[c]["q_out"]),
            "uin": np.asarray(res1[c]["u_out"]),
            "kva": kva_, "wo": wo_, "bo_row": bo_, "lng": lng_, "lnb": lnb_,
            "w1t": w1t_, "w3t": w3t_, "w2h": np.asarray(res1[c]["w2h_out"]),
            "ident": ident, "ones_in": ones, "ones_col": onesc,
        })
    return maps


_BUILD_CACHE = {}


def kernel(**inputs):
    if "l1" not in _BUILD_CACHE:
        _BUILD_CACHE["l1"] = build_launch1()
        _BUILD_CACHE["l2"] = build_launch2()
    nc1, nc2 = _BUILD_CACHE["l1"], _BUILD_CACHE["l2"]
    maps1 = prep_launch1_inputs(inputs)
    res1 = run_bass_kernel_spmd(nc1, maps1, core_ids=list(range(NC))).results
    maps2 = prep_launch2_inputs(inputs, res1)
    res2 = run_bass_kernel_spmd(nc2, maps2, core_ids=list(range(NC))).results
    halves = []
    for b in range(B):
        halves.append(np.concatenate(
            [np.asarray(res2[b * 4 + q]["out_o"]) for q in range(4)], axis=0))
    return np.stack(halves, axis=0)


# version 07 (final)
# speedup vs baseline: 1.1643x; 1.0883x over previous
import numpy as np
import ml_dtypes
import concourse.bass as bass
import concourse.mybir as mybir
from concourse.bass_utils import run_bass_kernel_spmd
from contextlib import ExitStack

F32 = mybir.dt.float32
BF16 = mybir.dt.bfloat16
AF = mybir.ActivationFunctionType
ALU = mybir.AluOpType
AX = mybir.AxisListType

B, L, D = 2, 8192, 2048
NC, Q = 8, 4
T = L // Q
NH, HD = 16, 128
NT = T // HD
MOE_F, NU = 1408, 12
NFC = MOE_F // HD
FFN_H = 8192
EPS = 1e-5
bf16 = ml_dtypes.bfloat16

ENGS = ("g", "s", "t", "v", "c")


class Prog:
    def __init__(self):
        self.ops = {e: [] for e in ENGS}

    def add(self, key, fn):
        self.ops[key].append(fn)


class Sem:
    def __init__(self, ctx, nc, name):
        self.h = ctx.enter_context(nc.semaphore(name))
        self.v = 0


class Pool:

    def __init__(self, tiles):
        self.tiles = tiles
        self.guards = [[] for _ in tiles]
        self.i = 0

    def next(self):
        idx = self.i
        self.i = (self.i + 1) % len(self.tiles)
        g = self.guards[idx]
        self.guards[idx] = []
        return idx, self.tiles[idx], g

    def guard(self, idx, sem, val):
        self.guards[idx].append((sem.h, val))


def build_launch1(phases="ABC"):
    nc = bass.Bass("TRN2", target_bir_lowering=False)
    dt = nc.dram_tensor

    xTpre = dt("xTpre", [D, 3 * T], F32, kind="ExternalInput")
    xTown = dt("xTown", [D, T], F32, kind="ExternalInput")
    rp = dt("rp", [HD, NH], F32, kind="ExternalInput")
    igw = dt("igw", [NH, HD, HD], F32, kind="ExternalInput")
    igb = dt("igb", [HD, NH], F32, kind="ExternalInput")
    rgw = dt("rgw", [NH, HD, HD], F32, kind="ExternalInput")
    rgb = dt("rgb", [HD, NH], F32, kind="ExternalInput")
    msel = dt("msel", [HD, 5], F32, kind="ExternalInput")
    wr9 = dt("wr9", [D, 9], F32, kind="ExternalInput")
    wg_all = dt("wg_all", [NU, D, MOE_F], F32, kind="ExternalInput")
    wu_all = dt("wu_all", [NU, D, MOE_F], F32, kind="ExternalInput")
    wd_all = dt("wd_all", [NU, MOE_F, D], F32, kind="ExternalInput")
    wqk = dt("wqk", [D, D], F32, kind="ExternalInput")
    bqk = dt("bqk", [HD, NH], F32, kind="ExternalInput")
    wv = dt("wv", [D, D], F32, kind="ExternalInput")
    bv_row = dt("bv_row", [1, D], F32, kind="ExternalInput")
    wu1 = dt("wu1", [D, D], F32, kind="ExternalInput")
    bu = dt("bu", [HD, NH], F32, kind="ExternalInput")
    ident = dt("ident", [HD, HD], F32, kind="ExternalInput")
    ones_in = dt("ones_in", [1, HD], F32, kind="ExternalInput")
    w2rr = dt("w2rr", [64, HD * D], F32, kind="ExternalInput")
    h64 = dt("h64", [64, 64], F32, kind="ExternalInput")
    h128 = dt("h128", [HD, HD], F32, kind="ExternalInput")

    x2_out = dt("x2_out", [T, D], F32, kind="ExternalOutput")
    w2h_out = dt("w2h_out", [FFN_H, D], BF16, kind="ExternalOutput")
    q_out = dt("q_out", [D, T], BF16, kind="ExternalOutput")
    u_out = dt("u_out", [D, T], BF16, kind="ExternalOutput")
    kv_out = dt("kv_out", [HD, NH * HD], F32, kind="ExternalOutput")

    x1f = dt("x1f", [T, D], BF16, kind="Internal")
    w2s = dt("w2s", [64, HD * D], BF16, kind="Internal")
    moe_d = dt("moe_d", [T, D], F32, kind="Internal")

    P = Prog()
    ctx = ExitStack()
    with ctx:
        mm = Sem(ctx, nc, "mm")
        ve = Sem(ctx, nc, "ve")
        es = Sem(ctx, nc, "es")
        gp = Sem(ctx, nc, "gp")
        Lconst = Sem(ctx, nc, "Lconst")
        Lxtb = [Sem(ctx, nc, "Lxtb0"), Sem(ctx, nc, "Lxtb1")]
        Lxtf = [Sem(ctx, nc, "Lxtf0"), Sem(ctx, nc, "Lxtf1")]
        Lwd = Sem(ctx, nc, "Lwd")
        Lwg = [Sem(ctx, nc, "Lwg0"), Sem(ctx, nc, "Lwg1")]
        Lacc = [Sem(ctx, nc, "Lacc0"), Sem(ctx, nc, "Lacc1")]
        Lwq = [Sem(ctx, nc, "Lwq0"), Sem(ctx, nc, "Lwq1")]
        Lwvh = Sem(ctx, nc, "Lwvh")
        Lsp = Sem(ctx, nc, "Lsp")
        Lx2o = Sem(ctx, nc, "Lx2o")
        Lrl = [Sem(ctx, nc, "Lrl0"), Sem(ctx, nc, "Lrl1")]
        Lqout = Sem(ctx, nc, "Lqout")
        Lkv = Sem(ctx, nc, "Lkv")
        Lw2 = [Sem(ctx, nc, "Lw2_0"), Sem(ctx, nc, "Lw2_1")]
        Lw2s = Sem(ctx, nc, "Lw2s")
        Lu1 = [Sem(ctx, nc, "Lu1_0"), Sem(ctx, nc, "Lu1_1")]
        Lw2h = [Sem(ctx, nc, "Lw2h0"), Sem(ctx, nc, "Lw2h1")]

        def sb(name, shape, dtp):
            return nc.alloc_sbuf_tensor(name, shape, dtp)

        nsp = sb("nsp", [HD, NH], F32)
        sp_t1 = sb("sp_t1", [HD, NH], F32)
        igb_s = sb("igb_s", [HD, NH], F32)
        rgb_s = sb("rgb_s", [HD, NH], F32)
        msel_s = sb("msel_s", [HD, 5], F32)
        bqk_s = sb("bqk_s", [HD, NH], F32)
        bu_s = sb("bu_s", [HD, NH], F32)
        igw_b = sb("igw_b", [HD, NH, HD], BF16)
        rgw_b = sb("rgw_b", [HD, NH, HD], BF16)
        idf = sb("idf", [HD, HD], F32)
        idb = sb("idb", [HD, HD], BF16)
        onesb = sb("onesb", [1, HD], BF16)
        wr9_b = sb("wr9_b", [HD, NH, 9], BF16)
        ej = sb("ej", [HD, NH * 3], F32)
        Aj = sb("Aj", [HD, NH * 3], F32)
        carry = sb("carry", [HD, NH], F32)
        cc1 = sb("cc1", [HD, 1], F32)
        cc2 = sb("cc2", [HD, 1], F32)
        cc3 = sb("cc3", [HD, 1], F32)
        m0fix = sb("m0fix", [HD, 1], F32)
        s1_all = sb("s1_all", [HD, NT], F32)
        s2_all = sb("s2_all", [HD, NT], F32)
        mu_all = sb("mu_all", [HD, NT], F32)
        rs_all = sb("rs_all", [HD, NT], F32)
        scl = sb("scl", [HD, NT * NU], F32)
        prob = sb("prob", [HD, 8], F32)
        h64b = sb("h64b", [64, 64], BF16)
        h128b = sb("h128b", [HD, HD], BF16)
        hsc_c = sb("hsc_c", [HD, 1], F32)
        pmsk = sb("pmsk", [HD, 8], F32)
        pstat = sb("pstat", [HD, 4], F32)
        eps_c = sb("eps_c", [HD, 1], F32)

        R0h = nc.alloc_sbuf_tensor("R0", [HD, 32768], F32)
        R1h = nc.alloc_sbuf_tensor("R1", [HD, 16384], F32)
        A0 = nc.lookup_mloc(R0h).addr
        A1 = nc.lookup_mloc(R1h).addr

        def carve(name, shape, dtp, base, off):
            return nc.alloc_sbuf_tensor_at(name, shape, dtp, offset=base + off)

        x1tm_h = carve("x1tm", [HD, NT, D], BF16, A0, 0)
        sq_scrA = carve("sq_scrA", [HD, D], F32, A0, 65536)
        xt_f = [carve("xtf0", [HD, T], F32, A1, 0),
                carve("xtf1", [HD, T], F32, A1, 8192)]
        xt_b = [carve("xtb0", [HD, T], BF16, A1, 16384),
                carve("xtb1", [HD, T], BF16, A1, 20480)]
        B1p = [carve("B1a", [HD, T], F32, A1, 24576),
               carve("B1b", [HD, T], F32, A1, 49152)]
        B2p = [carve("B2a", [HD, T], F32, A1, 32768),
               carve("B2b", [HD, T], F32, A1, 57344)]
        B3 = carve("B3", [HD, T], F32, A1, 40960)
        w2rr_b = carve("w2rr_b", [64, 2, 8192], BF16, A0, 73728)
        stg2 = carve("stg2", [64, 8192], BF16, A0, 106496)
        u1t = carve("u1t", [HD, 2, 1024], BF16, A0, 122880)
        w2h_st = carve("w2h_st", [HD, 2, 1024], BF16, A0, 126976)
        h_e = carve("h_e", [HD, NFC, D], BF16, A0, 0)
        wd_b = carve("wd_b", [HD, NFC, D], BF16, A0, 45056)
        wfc = carve("wfc", [HD, 2, 2 * NH, HD], BF16, A0, 90112)
        stage = carve("stage", [HD, 2, D], F32, A0, 106496)
        sq_scrB = carve("sq_scrB", [HD, D], F32, A0, 122880)
        x2t_st = carve("x2t_st", [HD, D], F32, A0, 0)
        x1rb = carve("x1rb", [HD, D], BF16, A0, 114688)
        x1T_h = carve("x1T", [HD, NH, T], BF16, A1, 0)
        kh_h = carve("kh", [HD, NT, D], BF16, A0, 0)
        wvh = carve("wvh", [HD, NH, 512], BF16, A0, 65536)
        wqk_t = carve("wqk_t", [HD, 2, NH, HD], BF16, A0, 81920)
        vtt = carve("vtt", [HD, 512], BF16, A0, 90112)
        elu_r = carve("elu_r", [HD, 512], F32, A0, 91136)
        elu_t = carve("elu_t", [HD, 512], F32, A0, 93184)
        qcm_t = carve("qcm_t", [HD, T], BF16, A0, 95232)
        bvr_b = carve("bvr_b", [1, D], BF16, A0, 99328)
        kv_sb = carve("kv_sb", [HD, NH * HD], F32, A0, 103424)

        pp_t = [nc.alloc_psum_tensor(f"pp{i}", [HD, 512], F32) for i in range(4)]
        pp = Pool(pp_t)
        ptf_t = [nc.alloc_psum_tensor(f"ptf{i}", [HD, HD], F32) for i in range(2)]
        ptb_t = [nc.alloc_psum_tensor(f"ptb{i}", [HD, 1024], BF16) for i in range(2)]
        ptf = Pool(ptf_t)
        ptb = Pool(ptb_t)

        def W(key, sem, val=None):
            v = sem.v if val is None else val
            P.add(key, lambda e, h=sem.h, v=v: e.wait_ge(h, v))

        def WH(key, hv_list):
            for (h, v) in hv_list:
                P.add(key, lambda e, h=h, v=v: e.wait_ge(h, v))

        def dma(lane, key, out, in_, accum=False):
            lane.v += 16
            def f(e, out=out, in_=in_, accum=accum, h=lane.h):
                if accum:
                    e.dma_start(out=out, in_=in_, accum_op=ALU.add).then_inc(h, 16)
                else:
                    e.dma_start(out=out, in_=in_).then_inc(h, 16)
            P.add(key, f)
            return lane.v

        def SC(fn):
            es.v += 1
            P.add("c", lambda e, fn=fn, h=es.h: fn(e).then_inc(h, 1))
            return es.v

        def VE(fn):
            ve.v += 1
            P.add("v", lambda e, fn=fn, h=ve.h: fn(e).then_inc(h, 1))
            return ve.v

        def TE(fn):
            mm.v += 1
            P.add("t", lambda e, fn=fn, h=mm.h: fn(e).then_inc(h, 1))
            return mm.v

        def TN(fn):
            P.add("t", fn)

        def GP(fn):
            gp.v += 1
            P.add("g", lambda e, fn=fn, h=gp.h: fn(e).then_inc(h, 1))
            return gp.v

        dma(Lconst, "g", nsp[:, :], rp[:, :])
        dma(Lconst, "g", igb_s[:, :], igb[:, :])
        dma(Lconst, "g", rgb_s[:, :], rgb[:, :])
        dma(Lconst, "g", msel_s[:, :], msel[:, :])
        dma(Lconst, "g", bqk_s[:, :], bqk[:, :])
        dma(Lconst, "g", bu_s[:, :], bu[:, :])
        dma(Lconst, "g", igw_b[:, :, :], igw.ap().transpose([1, 0, 2]))
        dma(Lconst, "g", rgw_b[:, :, :], rgw.ap().transpose([1, 0, 2]))
        dma(Lconst, "g", idf[:, :], ident[:, :])
        dma(Lconst, "g", idb[:, :], ident[:, :])
        dma(Lconst, "g", onesb[:, :], ones_in[:, :])
        dma(Lconst, "g", h64b[:, :], h64[:, :])
        dma(Lconst, "g", h128b[:, :], h128[:, :])
        c0 = dma(Lconst, "g", wr9_b[:, :, :], wr9.ap().rearrange("(t p) n -> p t n", p=HD))

        VE(lambda v: v.memset(eps_c[:, :], EPS))
        W("v", ve)
        VE(lambda v: v.memset(hsc_c[:, :], 1.0 / float(np.sqrt(FFN_H))))
        P.add("t", lambda e, h=Lconst.h, v=Lconst.v: e.wait_ge(h, v))
        W("c", Lconst, c0)
        SC(lambda s: s.activation(sp_t1[:, :], nsp[:, :], AF.Abs))
        W("c", es)
        SC(lambda s: s.activation(sp_t1[:, :], sp_t1[:, :], AF.Exp, scale=-1.0))
        W("c", es)
        SC(lambda s: s.activation(sp_t1[:, :], sp_t1[:, :], AF.Ln, bias=1.0))
        W("c", es)
        SC(lambda s: s.activation(nsp[:, :], nsp[:, :], AF.Relu))
        W("v", es)
        VE(lambda v: v.tensor_add(nsp[:, :], nsp[:, :], sp_t1[:, :]))
        W("v", ve)
        VE(lambda v: v.tensor_scalar_mul(nsp[:, :], nsp[:, :], -8.0))
        nsp_done = ve.v

        w2v = w2rr.ap()
        w2sv = w2s.ap().rearrange("a (b d) -> a b d", d=D)
        prep = {"wg2": [[], []], "stg_guard": [], "ug": [[], []],
                "first_r": True, "w2s_done": None}

        def emit_prep_load(ch):
            if ch >= 32:
                return
            wbi = ch % 2
            WH("g", prep["wg2"][wbi])
            prep.setdefault("lv", {})[ch] = dma(
                Lw2[wbi], "g", w2rr_b[:, wbi, :],
                w2v[:, ch * 8192:(ch + 1) * 8192])

        def emit_prep_s2(ch):
            wbi = ch % 2
            if ch == 0:
                emit_prep_load(0)
            emit_prep_load(ch + 1)
            lv = prep["lv"][ch]
            lastmm = None
            for jj in range(16):
                idx, ptile, gu = pp.next()
                WH("t", gu)
                W("t", Lw2[wbi], lv)
                mv = TE(lambda t, pt=ptile, wbi=wbi, jj=jj:
                        t.matmul(pt[0:64, :], h64b[:, :],
                                 w2rr_b[:, wbi, jj * 512:(jj + 1) * 512],
                                 start=True, stop=True))
                lastmm = mv
                if jj % 2 == 0:
                    P.add("v", lambda e, h2=mm.h, v=mv: e.wait_ge(h2, v))
                    if jj == 0:
                        WH("v", prep["stg_guard"])
                    evv = VE(lambda v, pt=ptile, jj=jj:
                             v.tensor_copy(stg2[:, jj * 512:(jj + 1) * 512], pt[0:64, :]))
                    pp.guard(idx, ve, evv)
                else:
                    P.add("c", lambda e, h2=mm.h, v=mv: e.wait_ge(h2, v))
                    if jj == 1:
                        WH("c", prep["stg_guard"])
                    evv = SC(lambda s, pt=ptile, jj=jj:
                             s.copy(stg2[:, jj * 512:(jj + 1) * 512], pt[0:64, :]))
                    pp.guard(idx, es, evv)
            prep["wg2"][wbi] = [(mm.h, lastmm)]
            P.add("s", lambda e, h2=ve.h, v=ve.v: e.wait_ge(h2, v))
            P.add("s", lambda e, h2=es.h, v=es.v: e.wait_ge(h2, v))
            sv = dma(Lw2s, "s", w2s.ap()[:, ch * 8192:(ch + 1) * 8192], stg2[:, :])
            prep["stg_guard"] = [(Lw2s.h, sv)]

        def emit_prep_s1(step):
            a, dc2 = divmod(step, 4)
            ubi = step % 2
            WH("g", prep["ug"][ubi])
            if prep["first_r"]:
                prep["w2s_done"] = Lw2s.v
                P.add("g", lambda e, h2=Lw2s.h, v=Lw2s.v: e.wait_ge(h2, v))
                prep["first_r"] = False
            lv = dma(Lu1[ubi], "g", u1t[:, ubi, 0:512],
                     w2sv[a, :, dc2 * 512:(dc2 + 1) * 512])
            idx, ptile, gu = pp.next()
            WH("t", gu)
            W("t", Lu1[ubi], lv)
            mv = TE(lambda t, pt=ptile, ubi=ubi:
                    t.matmul(pt[:, :], h128b[:, :], u1t[:, ubi, 0:512],
                             start=True, stop=True))
            prep["ug"][ubi] = [(mm.h, mv)]
            P.add("c", lambda e, h2=mm.h, v=mv: e.wait_ge(h2, v))
            P.add("c", lambda e, h2=Lw2h[ubi].h, v=Lw2h[ubi].v: e.wait_ge(h2, v))
            evv = SC(lambda s, pt=ptile, ubi=ubi:
                     s.activation(w2h_st[:, ubi, 0:512], pt[:, :], AF.Copy,
                                  scale=hsc_c[:, 0:1]))
            pp.guard(idx, es, evv)
            P.add("s", lambda e, h2=es.h, v=evv: e.wait_ge(h2, v))
            dma(Lw2h[ubi], "s",
                w2h_out.ap()[a * HD:(a + 1) * HD, dc2 * 512:(dc2 + 1) * 512],
                w2h_st[:, ubi, 0:512])

        def emit_prep_slice(it):
            if it < 32:
                emit_prep_s2(it)
            else:
                for step in range((it - 32) * 8, (it - 32) * 8 + 8):
                    emit_prep_s1(step)

        lastf = [[], []]
        lastb = [[], []]
        prep_scan_guard = []
        gates_free = [[(ve.h, nsp_done)], [(ve.h, nsp_done)]]
        x1tm = x1tm_h
        itrs = [(0, False), (1, False), (2, False), (3, True)]
        xt_lv = {}

        def emit_xt_load(it2):
            if it2 > 63:
                return
            j2, h2 = divmod(it2, NH)
            b2 = it2 % 2
            if j2 == 3:
                src2 = xTown[h2 * HD:(h2 + 1) * HD, :]
            else:
                src2 = xTpre[h2 * HD:(h2 + 1) * HD, j2 * T:(j2 + 1) * T]
            WH("s", lastf[b2])
            vf2 = dma(Lxtf[b2], "s", xt_f[b2][:, :], src2)
            WH("g", lastb[b2])
            vb2 = dma(Lxtb[b2], "g", xt_b[b2][:, :], src2)
            xt_lv[it2] = (vf2, vb2)

        for (j, own) in itrs:
            for h in range(NH):
                it = j * NH + h
                bi = it % 2
                B1 = B1p[bi]
                B2 = B2p[bi]
                if it == 0:
                    emit_xt_load(0)
                emit_xt_load(it + 1)
                vf = xt_lv[it][0]
                vb = xt_lv[it][1]

                first_evac = True
                last_gmm = None
                for lc in range(4):
                    sl = slice(lc * 512, (lc + 1) * 512)
                    for (gw, gb, gout) in ((igw_b, igb_s, B1), (rgw_b, rgb_s, B2)):
                        idx, ptile, gu = pp.next()
                        WH("t", gu)
                        W("t", Lxtb[bi], vb)
                        mv = TE(lambda t, pt=ptile, gw=gw, h=h, sl=sl, bi=bi:
                                t.matmul(pt[:, :], gw[:, h, :], xt_b[bi][:, sl],
                                         start=True, stop=True))
                        last_gmm = mv
                        P.add("c", lambda e, h2=mm.h, v=mv: e.wait_ge(h2, v))
                        if first_evac:
                            WH("c", gates_free[bi])
                            first_evac = False
                        evv = SC(lambda s, pt=ptile, go=gout, sl=sl, gb=gb, h=h:
                                 s.activation(go[:, sl], pt[:, :], AF.Sigmoid,
                                              bias=gb[:, h:h + 1]))
                        pp.guard(idx, es, evv)
                ev_last = es.v
                lastb[bi] = [(mm.h, last_gmm)]

                emit_prep_slice(it)
                P.add("g", lambda e, h2=Lxtf[bi].h, v=vf: e.wait_ge(h2, v))
                P.add("g", lambda e, h2=es.h, v=ev_last: e.wait_ge(h2, v))
                WH("g", prep_scan_guard)
                z1v = GP(lambda g, bi=bi, B1=B1:
                         g.tensor_mul(B3[:, :], xt_f[bi][:, :], B1[:, :]))
                if not own:
                    P.add("v", lambda e, h2=es.h, v=ev_last: e.wait_ge(h2, v))
                    sgav = VE(lambda v, B2=B2: v.reduce_sum(cc1[:, :], B2[:, :], axis=AX.X))
                    P.add("c", lambda e, h2=ve.h, v=sgav: e.wait_ge(h2, v))
                    SC(lambda s, h=h, j=j:
                       s.activation(Aj[:, h * 3 + j:h * 3 + j + 1],
                                    cc1[:, :], AF.Exp, scale=nsp[:, h:h + 1]))
                W("c", es)
                av = SC(lambda s, h=h, B2=B2:
                        s.activation(B2[:, :], B2[:, :], AF.Exp, scale=nsp[:, h:h + 1]))
                P.add("c", lambda e, h2=gp.h, v=z1v: e.wait_ge(h2, v))
                W("c", es)
                SC(lambda s, B1=B1, B2=B2: s.activation(B1[:, :], B2[:, :], AF.Square))
                W("c", es)
                m_v = SC(lambda s, B1=B1: s.activation(B1[:, :], B1[:, :], AF.Sqrt,
                                                       scale=-1.0, bias=1.0))
                mfix = (es.h, m_v)
                if j == 0:
                    P.add("v", lambda e, h2=es.h, v=m_v: e.wait_ge(h2, v))
                    fv = VE(lambda v, B1=B1: v.memset(B1[:, 0:1], 1.0))
                    mfix = (ve.h, fv)
                elif own:
                    P.add("v", lambda e, h2=es.h, v=m_v: e.wait_ge(h2, v))
                    VE(lambda v, B1=B1:
                       v.scalar_tensor_tensor(m0fix[:, :], B1[:, 0:1],
                                              msel_s[:, 4:5], msel_s[:, 3:4],
                                              ALU.mult, ALU.add))
                    W("v", ve)
                    fv = VE(lambda v, B1=B1: v.tensor_copy(B1[:, 0:1], m0fix[:, :]))
                    mfix = (ve.h, fv)
                P.add("g", lambda e, h2=mfix[0], v=mfix[1]: e.wait_ge(h2, v))
                z2v = GP(lambda g, B1=B1: g.tensor_mul(B3[:, :], B3[:, :], B1[:, :]))
                P.add("v", lambda e, h2=gp.h, v=z2v: e.wait_ge(h2, v))
                P.add("v", lambda e, h2=es.h, v=av: e.wait_ge(h2, v))

                if not own:
                    VE(lambda v, B1=B1, B2=B2:
                       v.tensor_tensor_scan(B1[:, :], B2[:, :], B3[:, :],
                                            0.0, ALU.mult, ALU.add))
                    W("v", ve)
                    scanv = VE(lambda v, h=h, j=j, B1=B1:
                               v.tensor_copy(ej[:, h * 3 + j:h * 3 + j + 1],
                                             B1[:, T - 1:T]))
                    gates_free[bi] = [(ve.h, scanv)]
                    prep_scan_guard = [(ve.h, ve.v - 1)]
                    lastf[bi] = [(gp.h, z1v)]
                else:
                    e0 = ej[:, h * 3 + 0:h * 3 + 1]
                    e1 = ej[:, h * 3 + 1:h * 3 + 2]
                    e2 = ej[:, h * 3 + 2:h * 3 + 3]
                    A1c = Aj[:, h * 3 + 1:h * 3 + 2]
                    A2c = Aj[:, h * 3 + 2:h * 3 + 3]
                    VE(lambda v, e0=e0, e1=e1, A1c=A1c:
                       v.scalar_tensor_tensor(cc2[:, :], e0, A1c, e1, ALU.mult, ALU.add))
                    W("v", ve)
                    VE(lambda v, e2=e2, A2c=A2c:
                       v.scalar_tensor_tensor(cc3[:, :], cc2[:, :], A2c, e2, ALU.mult, ALU.add))
                    W("v", ve)
                    VE(lambda v, e0=e0, h=h:
                       v.tensor_single_scalar(carry[:, h:h + 1], e0, msel_s[:, 0:1], ALU.mult))
                    W("v", ve)
                    VE(lambda v, h=h:
                       v.scalar_tensor_tensor(carry[:, h:h + 1], cc2[:, :], msel_s[:, 1:2],
                                              carry[:, h:h + 1], ALU.mult, ALU.add))
                    W("v", ve)
                    VE(lambda v, h=h:
                       v.scalar_tensor_tensor(carry[:, h:h + 1], cc3[:, :], msel_s[:, 2:3],
                                              carry[:, h:h + 1], ALU.mult, ALU.add))
                    W("v", ve)
                    VE(lambda v, h=h, B1=B1, B2=B2:
                       v.tensor_tensor_scan(B1[:, :], B2[:, :], B3[:, :],
                                            carry[:, h:h + 1], ALU.mult, ALU.add))
                    W("v", ve)
                    resv = VE(lambda v, bi=bi, B1=B1:
                              v.tensor_add(B3[:, :], B1[:, :], xt_f[bi][:, :]))
                    prep_scan_guard = [(mm.h, mm.v), (ve.h, resv)]
                    lastf[bi] = [(ve.h, resv)]
                    trl = None
                    for lt in range(NT):
                        idx, ptile, gu = ptf.next()
                        P.add("t", lambda e, h2=ve.h, v=resv: e.wait_ge(h2, v))
                        WH("t", gu)
                        tv = TE(lambda t, lt=lt, pt=ptile:
                                t.transpose(pt[:, :], B3[:, lt * HD:(lt + 1) * HD],
                                            idf[:, :]))
                        P.add("v", lambda e, h2=mm.h, v=tv: e.wait_ge(h2, v))
                        evv = VE(lambda v, lt=lt, h=h, pt=ptile:
                                 v.tensor_copy(x1tm[:, lt, h * HD:(h + 1) * HD],
                                               pt[:, :]))
                        ptf.guard(idx, ve, evv)
                        trl = tv
                    gates_free[bi] = [(ve.h, ve.v), (mm.h, trl)]
        x1_assembled = ve.v
        for lt in range(NT):
            P.add("v", lambda e, h=ve.h, v=x1_assembled: e.wait_ge(h, v))
            VE(lambda v, lt=lt:
               v.reduce_sum(s1_all[:, lt:lt + 1], x1tm[:, lt, :], axis=AX.X))
            P.add("c", lambda e, h=ve.h, v=x1_assembled: e.wait_ge(h, v))
            W("c", es)
            SC(lambda s, lt=lt:
               s.activation(sq_scrA[:, :], x1tm[:, lt, :], AF.Square,
                            accum_out=s2_all[:, lt:lt + 1]))
        W("v", ve)
        W("v", es)
        VE(lambda v: v.tensor_scalar_mul(mu_all[:, :], s1_all[:, :], 1.0 / D))
        W("v", ve)
        VE(lambda v: v.tensor_scalar_mul(rs_all[:, :], s2_all[:, :], 1.0 / D))
        W("v", ve)
        VE(lambda v: v.tensor_mul(s1_all[:, :], mu_all[:, :], mu_all[:, :]))
        W("v", ve)
        VE(lambda v: v.tensor_sub(rs_all[:, :], rs_all[:, :], s1_all[:, :]))
        W("c", ve)
        SC(lambda s: s.activation(rs_all[:, :], rs_all[:, :], AF.Sqrt, bias=eps_c[:, 0:1]))
        W("v", es)
        VE(lambda v: v.reciprocal(rs_all[:, :], rs_all[:, :]))
        rs_done = ve.v
        x1T = x1T_h
        spill_last = Lsp.v
        x1T_trl = None
        for lt in range(NT):
            P.add("v", lambda e, h=ve.h, v=rs_done: e.wait_ge(h, v))
            av = VE(lambda v, lt=lt:
                    v.tensor_scalar(x1tm[:, lt, :], x1tm[:, lt, :],
                                    mu_all[:, lt:lt + 1], rs_all[:, lt:lt + 1],
                                    ALU.subtract, ALU.mult))
            P.add("s", lambda e, h=ve.h, v=av: e.wait_ge(h, v))
            spill_last = dma(Lsp, "s", x1f[lt * HD:(lt + 1) * HD, :], x1tm[:, lt, :])
            for dtk in range(NH):
                idx, ptile, gu = ptb.next()
                P.add("t", lambda e, h=ve.h, v=av: e.wait_ge(h, v))
                WH("t", gu)
                tv = TE(lambda t, lt=lt, dtk=dtk, pt=ptile:
                        t.transpose(pt[:, 0:HD], x1tm[:, lt, dtk * HD:(dtk + 1) * HD],
                                    idb[:, :]))
                x1T_trl = tv
                P.add("v", lambda e, h=mm.h, v=tv: e.wait_ge(h, v))
                evv = VE(lambda v, lt=lt, dtk=dtk, pt=ptile:
                         v.tensor_copy(x1T[:, dtk, lt * HD:(lt + 1) * HD], pt[:, 0:HD]))
                ptb.guard(idx, ve, evv)
        x1T_ready = ve.v

        if "B" not in phases:
            phases_skip_b = True
        for lt in range(NT if "B" in phases else 0):
            idx, ptile, gu = pp.next()
            WH("t", gu)
            if lt == 0:
                W("t", ve, x1T_ready)
            for dtk in range(NH):
                if dtk < NH - 1:
                    TN(lambda t, pt=ptile, lt=lt, dtk=dtk:
                       t.matmul(pt[:, 0:9], x1T[:, dtk, lt * HD:(lt + 1) * HD],
                                wr9_b[:, dtk, :], start=(dtk == 0), stop=False))
                else:
                    mv = TE(lambda t, pt=ptile, lt=lt, dtk=dtk:
                            t.matmul(pt[:, 0:9], x1T[:, dtk, lt * HD:(lt + 1) * HD],
                                     wr9_b[:, dtk, :], start=False, stop=True))
            P.add("v", lambda e, h=mm.h, v=mv: e.wait_ge(h, v))
            VE(lambda v, pt=ptile:
               v.reduce_max(pstat[:, 0:1], pt[:, 0:8], axis=AX.X, negate=True))
            W("c", ve)
            SC(lambda s, pt=ptile:
               s.activation(prob[:, :], pt[:, 0:8], AF.Exp, bias=pstat[:, 0:1],
                            accum_out=pstat[:, 1:2]))
            evv = SC(lambda s, pt=ptile, lt=lt:
                     s.activation(scl[:, lt * NU + 8:lt * NU + 9], pt[:, 8:9], AF.Sigmoid))
            pp.guard(idx, es, evv)
            W("v", es)
            VE(lambda v: v.reciprocal(pstat[:, 2:3], pstat[:, 1:2]))
            W("v", ve)
            VE(lambda v: v.tensor_scalar_mul(prob[:, :], prob[:, :], pstat[:, 2:3]))
            W("v", ve)
            VE(lambda v: v.reduce_max(pstat[:, 0:1], prob[:, :], axis=AX.X))
            W("v", ve)
            VE(lambda v: v.tensor_single_scalar(pmsk[:, :], prob[:, :],
                                                pstat[:, 0:1], ALU.is_ge))
            W("v", ve)
            VE(lambda v: v.scalar_tensor_tensor(pmsk[:, :], pmsk[:, :], -2.0,
                                                prob[:, :], ALU.mult, ALU.add))
            W("v", ve)
            VE(lambda v: v.reduce_max(pstat[:, 3:4], pmsk[:, :], axis=AX.X))
            W("v", ve)
            VE(lambda v: v.tensor_single_scalar(pmsk[:, :], prob[:, :],
                                                pstat[:, 3:4], ALU.is_ge))
            W("v", ve)
            VE(lambda v, lt=lt:
               v.tensor_mul(scl[:, lt * NU:lt * NU + 8], prob[:, :], pmsk[:, :]))
            W("v", ve)
            for r in range(1, 4):
                VE(lambda v, lt=lt, r=r:
                   v.tensor_copy(scl[:, lt * NU + 8 + r:lt * NU + 9 + r],
                                 scl[:, lt * NU + 8:lt * NU + 9]))
        scl_ready = ve.v

        P.add("g", lambda e, h=mm.h, v=x1T_trl: e.wait_ge(h, v))
        P.add("g", lambda e, h=Lsp.h, v=spill_last: e.wait_ge(h, v))
        for _lane in (Lw2s, Lw2h[0], Lw2h[1], Lu1[0], Lu1[1], Lw2[0], Lw2[1]):
            P.add("g", lambda e, h=_lane.h, v=_lane.v: e.wait_ge(h, v))

        units_range = range(NU if "B" in phases else 0)
        wfc_guard = [[], []]
        stage_guard = [[], []]
        prev_wd_reader = None
        prev_accum = None
        prev_silu_reader = None
        for u in units_range:
            wdl = None
            for fc in range(NFC):
                wbi = fc % 2
                if fc == 2:
                    if prev_wd_reader is not None:
                        P.add("g", lambda e, h=mm.h, v=prev_wd_reader: e.wait_ge(h, v))
                    wdl = dma(Lwd, "g", wd_b[:, :, :],
                              wd_all.ap()[u].rearrange("(t p) d -> p t d", p=HD))
                WH("g", wfc_guard[wbi])
                dma(Lwg[wbi], "g", wfc[:, wbi, 0:NH, :],
                    wg_all.ap()[u].rearrange("(t p) f -> p t f", p=HD)
                    [:, :, fc * HD:(fc + 1) * HD])
                l2 = dma(Lwg[wbi], "g", wfc[:, wbi, NH:2 * NH, :],
                         wu_all.ap()[u].rearrange("(t p) f -> p t f", p=HD)
                         [:, :, fc * HD:(fc + 1) * HD])
                fcmm_last = None
                for tc in range(4):
                    tsl = slice(tc * 512, (tc + 1) * 512)
                    i1, p1, g1 = pp.next()
                    i2, p2, g2 = pp.next()
                    WH("t", g1)
                    WH("t", g2)
                    W("t", Lwg[wbi], l2)
                    if u == 0 and fc == 0 and tc == 0:
                        W("t", ve, scl_ready)
                    for dtk in range(NH):
                        if dtk < NH - 1:
                            TN(lambda t, p1=p1, wbi=wbi, dtk=dtk, tsl=tsl:
                               t.matmul(p1[:, :], wfc[:, wbi, dtk, :], x1T[:, dtk, tsl],
                                        start=(dtk == 0), stop=False))
                        else:
                            TE(lambda t, p1=p1, wbi=wbi, dtk=dtk, tsl=tsl:
                               t.matmul(p1[:, :], wfc[:, wbi, dtk, :], x1T[:, dtk, tsl],
                                        start=False, stop=True))
                    for dtk in range(NH):
                        if dtk < NH - 1:
                            TN(lambda t, p2=p2, wbi=wbi, dtk=dtk, tsl=tsl:
                               t.matmul(p2[:, :], wfc[:, wbi, NH + dtk, :],
                                        x1T[:, dtk, tsl], start=(dtk == 0), stop=False))
                        else:
                            mv = TE(lambda t, p2=p2, wbi=wbi, dtk=dtk, tsl=tsl:
                                    t.matmul(p2[:, :], wfc[:, wbi, NH + dtk, :],
                                             x1T[:, dtk, tsl], start=False, stop=True))
                    fcmm_last = mv
                    P.add("c", lambda e, h=mm.h, v=mv: e.wait_ge(h, v))
                    if prev_silu_reader is not None:
                        P.add("c", lambda e, h=ve.h, v=prev_silu_reader: e.wait_ge(h, v))
                    W("c", es)
                    s1v = SC(lambda s, p1=p1: s.activation(sq_scrB[:, 0:512], p1[:, :], AF.Silu))
                    pp.guard(i1, es, s1v)
                    P.add("v", lambda e, h=es.h, v=s1v: e.wait_ge(h, v))
                    hv = VE(lambda v, p2=p2, fc=fc, tsl=tsl:
                            v.tensor_mul(h_e[:, fc, tsl], sq_scrB[:, 0:512], p2[:, :]))
                    pp.guard(i2, ve, hv)
                    prev_silu_reader = hv
                wfc_guard[wbi] = [(mm.h, fcmm_last)]
            up_all = ve.v
            for tt in range(NT):
                sbi = tt % 2
                dn_last = None
                for dc in range(4):
                    dsl = slice(dc * 512, (dc + 1) * 512)
                    idx, ptile, gu = pp.next()
                    WH("t", gu)
                    if dc == 0 and tt == 0:
                        W("t", ve, up_all)
                        W("t", Lwd, wdl)
                    for fc in range(NFC):
                        if fc < NFC - 1:
                            TN(lambda t, pt=ptile, fc=fc, tt=tt, dsl=dsl:
                               t.matmul(pt[:, :], h_e[:, fc, tt * HD:(tt + 1) * HD],
                                        wd_b[:, fc, dsl], start=(fc == 0), stop=False))
                        else:
                            mv = TE(lambda t, pt=ptile, fc=fc, tt=tt, dsl=dsl:
                                    t.matmul(pt[:, :], h_e[:, fc, tt * HD:(tt + 1) * HD],
                                             wd_b[:, fc, dsl], start=False, stop=True))
                    dn_last = mv
                    P.add("v", lambda e, h=mm.h, v=mv: e.wait_ge(h, v))
                    WH("v", stage_guard[sbi] if dc == 0 else [])
                    evv = VE(lambda v, pt=ptile, sbi=sbi, dsl=dsl, tt=tt, u=u:
                             v.tensor_scalar(stage[:, sbi, dsl], pt[:, :],
                                             scl[:, tt * NU + u:tt * NU + u + 1], None,
                                             ALU.mult))
                    pp.guard(idx, ve, evv)
                P.add("g", lambda e, h=ve.h, v=ve.v: e.wait_ge(h, v))
                if prev_accum is not None:
                    WH("g", prev_accum)
                avd = dma(Lacc[sbi], "g", moe_d[tt * HD:(tt + 1) * HD, :],
                          stage[:, sbi, :], accum=(u > 0))
                stage_guard[sbi] = [(Lacc[sbi].h, avd)]
            prev_accum = [(Lacc[0].h, Lacc[0].v), (Lacc[1].h, Lacc[1].v)]
            prev_wd_reader = dn_last

        moe_done = [(Lacc[0].h, Lacc[0].v), (Lacc[1].h, Lacc[1].v)]
        x2T = x1T
        WH("s", moe_done)
        prev_add = None
        prev_x2_users = []
        for lt in range(NT if "B" in phases else 0):
            if prev_add is not None:
                P.add("s", lambda e, h=ve.h, v=prev_add: e.wait_ge(h, v))
            v1 = dma(Lrl[0], "s", stage[:, 0, :], moe_d[lt * HD:(lt + 1) * HD, :])
            v2 = dma(Lrl[1], "s", x1rb[:, :], x1f[lt * HD:(lt + 1) * HD, :])
            P.add("v", lambda e, h=Lrl[0].h, v=v1: e.wait_ge(h, v))
            P.add("v", lambda e, h=Lrl[1].h, v=v2: e.wait_ge(h, v))
            WH("v", prev_x2_users)
            VE(lambda v: v.tensor_add(x2t_st[:, :], stage[:, 0, :], x1rb[:, :]))
            W("v", ve)
            VE(lambda v: v.reduce_sum(pstat[:, 0:1], x2t_st[:, :], axis=AX.X))
            W("c", ve)
            W("c", es)
            SC(lambda s: s.activation(sq_scrB[:, :], x2t_st[:, :], AF.Square,
                                      accum_out=pstat[:, 1:2]))
            W("v", es)
            VE(lambda v: v.tensor_scalar_mul(pstat[:, 0:1], pstat[:, 0:1], 1.0 / D))
            W("v", ve)
            VE(lambda v: v.tensor_scalar_mul(pstat[:, 1:2], pstat[:, 1:2], 1.0 / D))
            W("v", ve)
            VE(lambda v: v.tensor_mul(pstat[:, 2:3], pstat[:, 0:1], pstat[:, 0:1]))
            W("v", ve)
            VE(lambda v: v.tensor_sub(pstat[:, 1:2], pstat[:, 1:2], pstat[:, 2:3]))
            W("c", ve)
            SC(lambda s: s.activation(pstat[:, 1:2], pstat[:, 1:2], AF.Sqrt, bias=eps_c[:, 0:1]))
            W("v", es)
            VE(lambda v: v.reciprocal(pstat[:, 1:2], pstat[:, 1:2]))
            W("v", ve)
            lnv = VE(lambda v: v.tensor_scalar(x2t_st[:, :], x2t_st[:, :],
                                               pstat[:, 0:1], pstat[:, 1:2],
                                               ALU.subtract, ALU.mult))
            prev_add = lnv
            P.add("s", lambda e, h=ve.h, v=lnv: e.wait_ge(h, v))
            x2dma = dma(Lx2o, "s", x2_out[lt * HD:(lt + 1) * HD, :], x2t_st[:, :])
            users = [(Lx2o.h, x2dma)]
            for dtk in range(NH):
                idx, ptile, gu = ptf.next()
                P.add("t", lambda e, h=ve.h, v=lnv: e.wait_ge(h, v))
                WH("t", gu)
                tv = TE(lambda t, dtk=dtk, pt=ptile:
                        t.transpose(pt[:, :], x2t_st[:, dtk * HD:(dtk + 1) * HD],
                                    idf[:, :]))
                P.add("v", lambda e, h=mm.h, v=tv: e.wait_ge(h, v))
                evv = VE(lambda v, lt=lt, dtk=dtk, pt=ptile:
                         v.tensor_copy(x2T[:, dtk, lt * HD:(lt + 1) * HD], pt[:, :]))
                ptf.guard(idx, ve, evv)
            users.append((mm.h, tv))
            prev_x2_users = users
        x2T_ready = ve.v
        x2_out_done = Lx2o.v

        kh_tm = kh_h
        wq_guard = [[], []]
        P.add("g", lambda e, h=mm.h, v=mm.v: e.wait_ge(h, v))
        bvl = dma(Lwvh, "g", bvr_b[:, :], bv_row[:, :])
        kv0 = VE(lambda v: v.memset(kv_sb[:, :], 0.0))
        qcm_guard = [(Lx2o.h, x2_out_done)]

        def proj_cm(ht, wsrc, bias_s, kind, out_dram, do_kh):
            nonlocal qcm_guard
            bi = ht % 2
            WH("g", wq_guard[bi])
            lv = dma(Lwq[bi], "g", wqk_t[:, bi, :, :],
                     wsrc.ap().rearrange("(t p) h -> p t h", p=HD)
                     [:, :, ht * HD:(ht + 1) * HD])
            lastmm = None
            done = None
            for tc in range(4):
                tsl = slice(tc * 512, (tc + 1) * 512)
                idx, ptile, gu = pp.next()
                WH("t", gu)
                W("t", Lwq[bi], lv)
                if ht == 0 and tc == 0:
                    W("t", ve, x2T_ready)
                for dtk in range(NH):
                    if dtk < NH - 1:
                        TN(lambda t, pt=ptile, bi=bi, dtk=dtk, tsl=tsl:
                           t.matmul(pt[:, :], wqk_t[:, bi, dtk, :], x2T[:, dtk, tsl],
                                    start=(dtk == 0), stop=False))
                    else:
                        mv = TE(lambda t, pt=ptile, bi=bi, dtk=dtk, tsl=tsl:
                                t.matmul(pt[:, :], wqk_t[:, bi, dtk, :], x2T[:, dtk, tsl],
                                         start=False, stop=True))
                lastmm = mv
                if kind == "silu":
                    P.add("c", lambda e, h=mm.h, v=mv: e.wait_ge(h, v))
                    if tc == 0:
                        WH("c", qcm_guard)
                    evv = SC(lambda s, pt=ptile, ht=ht, tsl=tsl:
                             s.activation(qcm_t[:, tsl], pt[:, :], AF.Silu,
                                          bias=bias_s[:, ht:ht + 1]))
                    pp.guard(idx, es, evv)
                    done = (es.h, evv)
                elif "4" in phases:
                    P.add("c", lambda e, h=mm.h, v=mv: e.wait_ge(h, v))
                    if tc == 0:
                        WH("c", qcm_guard)
                    evv = SC(lambda s, pt=ptile, ht=ht, tsl=tsl:
                             s.activation(qcm_t[:, tsl], pt[:, :], AF.Relu,
                                          bias=bias_s[:, ht:ht + 1]))
                    pp.guard(idx, es, evv)
                    done = (es.h, evv)
                else:
                    P.add("c", lambda e, h=mm.h, v=mv: e.wait_ge(h, v))
                    if tc > 0:
                        W("c", ve)
                    rv = SC(lambda s, pt=ptile, ht=ht:
                            s.activation(elu_r[:, :], pt[:, :], AF.Relu,
                                         bias=bias_s[:, ht:ht + 1]))
                    P.add("v", lambda e, h=mm.h, v=mv: e.wait_ge(h, v))
                    W("v", es)
                    if "1" in phases:
                        tv_ = VE(lambda v, pt=ptile: v.tensor_copy(elu_t[:, :], pt[:, :]))
                    else:
                        tv_ = VE(lambda v, pt=ptile, ht=ht:
                                 v.tensor_scalar(elu_t[:, :], pt[:, :], bias_s[:, ht:ht + 1],
                                                 0.0, ALU.add, ALU.min))
                    pp.guard(idx, ve, tv_)
                    pp.guard(idx, es, rv)
                    W("c", ve, tv_)
                    if "2" in phases:
                        ev_ = SC(lambda s: s.activation(elu_t[:, :], elu_t[:, :], AF.Copy))
                    else:
                        ev_ = SC(lambda s: s.activation(elu_t[:, :], elu_t[:, :], AF.Exp))
                    P.add("v", lambda e, h=es.h, v=ev_: e.wait_ge(h, v))
                    P.add("v", lambda e, h=es.h, v=rv: e.wait_ge(h, v))
                    if tc == 0:
                        WH("v", qcm_guard)
                    if "3" in phases:
                        fv = VE(lambda v, tsl=tsl: v.tensor_copy(qcm_t[:, tsl], elu_t[:, :]))
                    else:
                        fv = VE(lambda v, ht=ht, tsl=tsl:
                                v.scalar_tensor_tensor(qcm_t[:, tsl], elu_t[:, :], -1.0,
                                                       elu_r[:, :], ALU.add, ALU.add))
                    done = (ve.h, fv)
            wq_guard[bi] = [(mm.h, lastmm)]
            P.add("s", lambda e, h=done[0], v=done[1]: e.wait_ge(h, v))
            ov = dma(Lqout, "s", out_dram[ht * HD:(ht + 1) * HD, :], qcm_t[:, :])
            users = [(Lqout.h, ov)]
            if do_kh:
                for lt in range(NT):
                    idx, ptile, gu = ptb.next()
                    P.add("t", lambda e, h=done[0], v=done[1]: e.wait_ge(h, v))
                    WH("t", gu)
                    tv2 = TE(lambda t, lt=lt, pt=ptile:
                             t.transpose(pt[:, 0:HD], qcm_t[:, lt * HD:(lt + 1) * HD],
                                         idb[:, :]))
                    P.add("v", lambda e, h=mm.h, v=tv2: e.wait_ge(h, v))
                    if ht == 0 and lt == 0:
                        P.add("v", lambda e, h=Lx2o.h, v=x2_out_done: e.wait_ge(h, v))
                    evv = VE(lambda v, lt=lt, ht=ht, pt=ptile:
                             v.tensor_copy(kh_tm[:, lt, ht * HD:(ht + 1) * HD],
                                           pt[:, 0:HD]))
                    ptb.guard(idx, ve, evv)
                users.append((mm.h, tv2))
            qcm_guard = users

        for ht in range(NH if ("C" in phases or "q" in phases) else 0):
            proj_cm(ht, wqk, bqk_s, "silu" if "E" in phases else "elu", q_out,
                    do_kh=("K" not in phases))
        kh_ready = ve.v
        for ht in range(NH if ("C" in phases or "U" in phases) else 0):
            proj_cm(ht, wu1, bu_s, "silu", u_out, do_kh=False)

        wv_guard = []
        vtt_guard = []
        for hc in range(4 if "C" in phases else 0):
            WH("g", wv_guard)
            lv = dma(Lwvh, "g", wvh[:, :, :],
                     wv.ap().rearrange("(t p) h -> p t h", p=HD)
                     [:, :, hc * 512:(hc + 1) * 512])
            lastmm = None
            for tt in range(NT):
                idx, ptile, gu = pp.next()
                WH("t", gu)
                W("t", Lwvh, lv)
                for dtk in range(NH):
                    TN(lambda t, pt=ptile, dtk=dtk, tt=tt:
                       t.matmul(pt[:, :], x2T[:, dtk, tt * HD:(tt + 1) * HD],
                                wvh[:, dtk, :], start=(dtk == 0), stop=False))
                mv = TE(lambda t, pt=ptile, hc=hc:
                        t.matmul(pt[:, :], onesb[0:1, :],
                                 bvr_b[0:1, hc * 512:(hc + 1) * 512],
                                 start=False, stop=True))
                lastmm = mv
                P.add("c", lambda e, h=mm.h, v=mv: e.wait_ge(h, v))
                WH("c", vtt_guard)
                sv = SC(lambda s, pt=ptile: s.activation(vtt[:, :], pt[:, :], AF.Silu))
                pp.guard(idx, es, sv)
                kvmms = []
                for hh in range(4):
                    habs = hc * 4 + hh
                    i2, pt2, gu2 = pp.next()
                    WH("t", gu2)
                    P.add("t", lambda e, h=es.h, v=sv: e.wait_ge(h, v))
                    if hc == 0 and tt == 0 and hh == 0:
                        W("t", ve, kh_ready)
                    tv3 = TE(lambda t, pt2=pt2, tt=tt, habs=habs, hh=hh:
                             t.matmul(pt2[:, 0:HD],
                                      kh_tm[:, tt, habs * HD:(habs + 1) * HD],
                                      vtt[:, hh * HD:(hh + 1) * HD],
                                      start=True, stop=True))
                    kvmms.append(tv3)
                    P.add("v", lambda e, h=mm.h, v=tv3: e.wait_ge(h, v))
                    if hc == 0 and tt == 0 and hh == 0:
                        W("v", ve, kv0)
                    evv = VE(lambda v, pt2=pt2, habs=habs:
                             v.tensor_add(kv_sb[:, habs * HD:(habs + 1) * HD],
                                          kv_sb[:, habs * HD:(habs + 1) * HD],
                                          pt2[:, 0:HD]))
                    pp.guard(i2, ve, evv)
                vtt_guard = [(mm.h, kvmms[-1])]
            wv_guard = [(mm.h, lastmm)]
        P.add("s", lambda e, h=ve.h, v=ve.v: e.wait_ge(h, v))
        dma(Lkv, "s", kv_out[:, :], kv_sb[:, :])

        for lane in (Lkv, Lqout, Lx2o, Lsp, Lrl[0], Lrl[1], Lacc[0], Lacc[1]):
            W("s", lane)
        for lane in (Lwvh, Lwq[0], Lwq[1], Lwd, Lwg[0], Lwg[1],
                     Lw2[0], Lw2[1], Lu1[0], Lu1[1]):
            W("g", lane)
        for lane in (Lw2s, Lw2h[0], Lw2h[1]):
            W("s", lane)

        with nc.Block() as blk:
            @blk.gpsimd
            def _(g):
                for f in P.ops["g"]:
                    f(g)

            @blk.sync
            def _(s):
                for f in P.ops["s"]:
                    f(s)

            @blk.tensor
            def _(t):
                for f in P.ops["t"]:
                    f(t)

            @blk.vector
            def _(v):
                for f in P.ops["v"]:
                    f(v)

            @blk.scalar
            def _(c):
                for f in P.ops["c"]:
                    f(c)

    return nc


def prep_launch1_inputs(inp):
    f32 = np.float32
    x = np.asarray(inp["L0_attn_x"], f32)
    xT = [np.ascontiguousarray(x[b].T) for b in range(B)]
    rp = np.ascontiguousarray(np.asarray(inp["L0_attn_recurrent_param"], f32)
                              .reshape(NH, HD).T)
    igb_ = np.ascontiguousarray(np.asarray(inp["L0_attn_input_gate_bias"], f32).T)
    rgb_ = np.ascontiguousarray(np.asarray(inp["L0_attn_recurrent_gate_bias"], f32).T)
    igw_ = np.ascontiguousarray(np.asarray(inp["L0_attn_input_gate_weight"], f32))
    rgw_ = np.ascontiguousarray(np.asarray(inp["L0_attn_recurrent_gate_weight"], f32))
    wr9_ = np.ascontiguousarray(
        np.concatenate([np.asarray(inp["L0_ffn_Wr"], f32),
                        np.asarray(inp["L0_ffn_Wsg"], f32)], axis=1))
    sgq = np.asarray(inp["L0_ffn_Sg"], f32).reshape(D, 4, MOE_F).transpose(1, 0, 2)
    suq = np.asarray(inp["L0_ffn_Su"], f32).reshape(D, 4, MOE_F).transpose(1, 0, 2)
    sdq = np.asarray(inp["L0_ffn_Sd"], f32).reshape(4, MOE_F, D)
    wg_ = np.ascontiguousarray(np.concatenate([np.asarray(inp["L0_ffn_Wg"], f32), sgq], 0))
    wu_ = np.ascontiguousarray(np.concatenate([np.asarray(inp["L0_ffn_Wu"], f32), suq], 0))
    wd_ = np.ascontiguousarray(np.concatenate([np.asarray(inp["L0_ffn_Wd"], f32), sdq], 0))
    wqk_ = np.ascontiguousarray(np.asarray(inp["L1_attn_Wqk"], f32))
    bqk_ = np.ascontiguousarray(np.asarray(inp["L1_attn_bqk"], f32).reshape(NH, HD).T)
    wv_ = np.ascontiguousarray(np.asarray(inp["L1_attn_Wv"], f32))
    bvr_ = np.asarray(inp["L1_attn_bv"], f32).reshape(1, D)
    wu1_ = np.ascontiguousarray(np.asarray(inp["L1_attn_Wu"], f32))
    bu_ = np.ascontiguousarray(np.asarray(inp["L1_attn_bu"], f32).reshape(NH, HD).T)
    ident = np.eye(HD, dtype=f32)
    ones = np.ones((1, HD), f32)
    w2rr_ = np.ascontiguousarray(np.asarray(inp["L1_ffn_W2"], f32).T).reshape(64, HD * D)
    h64_ = _sylvester(64)
    h128_ = _sylvester(128)

    maps = []
    for c in range(NC):
        b, q = c // 4, c % 4
        msel_ = np.zeros((HD, 5), f32)
        if q >= 1:
            msel_[:, q - 1] = 1.0
        msel_[:, 3] = 1.0 if q == 0 else 0.0
        msel_[:, 4] = 0.0 if q == 0 else 1.0
        maps.append({
            "xTpre": xT[b][:, 0:3 * T],
            "xTown": np.ascontiguousarray(xT[b][:, q * T:(q + 1) * T]),
            "rp": rp, "igw": igw_, "igb": igb_, "rgw": rgw_, "rgb": rgb_,
            "msel": msel_, "wr9": wr9_,
            "wg_all": wg_, "wu_all": wu_, "wd_all": wd_,
            "wqk": wqk_, "bqk": bqk_, "wv": wv_, "bv_row": bvr_,
            "wu1": wu1_, "bu": bu_, "ident": ident, "ones_in": ones,
            "w2rr": w2rr_, "h64": h64_, "h128": h128_,
        })
    return maps





def build_launch2():
    nc = bass.Bass("TRN2", target_bir_lowering=False)
    dt = nc.dram_tensor
    NA = 64
    FLAT = HD * D

    x2i = dt("x2i", [T, D], F32, kind="ExternalInput")
    qin = dt("qin", [D, T], BF16, kind="ExternalInput")
    uin = dt("uin", [D, T], BF16, kind="ExternalInput")
    kva = dt("kva", [4, HD, NH * HD], F32, kind="ExternalInput")
    wo = dt("wo", [D, D], F32, kind="ExternalInput")
    bo_row = dt("bo_row", [1, D], F32, kind="ExternalInput")
    lng = dt("lng", [HD, NH], F32, kind="ExternalInput")
    lnb = dt("lnb", [HD, NH], F32, kind="ExternalInput")
    w1t = dt("w1t", [D, FFN_H], F32, kind="ExternalInput")
    w3t = dt("w3t", [D, FFN_H], F32, kind="ExternalInput")
    w2h = dt("w2h", [FFN_H, D], BF16, kind="ExternalInput")
    ident = dt("ident", [HD, HD], F32, kind="ExternalInput")
    ones_in = dt("ones_in", [1, HD], F32, kind="ExternalInput")
    ones_col = dt("ones_col", [HD, 1], F32, kind="ExternalInput")

    out_o = dt("out_o", [T, D], F32, kind="ExternalOutput")

    x3f = dt("x3f", [T, D], F32, kind="Internal")
    ffd = dt("ffd", [T, D], F32, kind="Internal")

    P = Prog()
    ctx = ExitStack()
    with ctx:
        mm = Sem(ctx, nc, "mm")
        ve = Sem(ctx, nc, "ve")
        es = Sem(ctx, nc, "es")
        gp = Sem(ctx, nc, "gp")
        Lc = Sem(ctx, nc, "Lc")
        Lkv = Sem(ctx, nc, "Lkv")
        Lwo = Sem(ctx, nc, "Lwo")
        Lq = [Sem(ctx, nc, "Lq0"), Sem(ctx, nc, "Lq1")]
        Lx2 = Sem(ctx, nc, "Lx2")
        Lsp = Sem(ctx, nc, "Lsp")
        Lx3r = Sem(ctx, nc, "Lx3r")
        Lw13 = [Sem(ctx, nc, "Lw13_0"), Sem(ctx, nc, "Lw13_1")]
        Lw2hr = Sem(ctx, nc, "Lw2hr")
        Lffa = [Sem(ctx, nc, "Lffa0"), Sem(ctx, nc, "Lffa1")]
        Lrl = [Sem(ctx, nc, "Lrl0"), Sem(ctx, nc, "Lrl1")]
        Lout = Sem(ctx, nc, "Lout")

        def sb(name, shape, dtp):
            return nc.alloc_sbuf_tensor(name, shape, dtp)

        idf = sb("idf", [HD, HD], F32)
        onesb = sb("onesb", [1, HD], BF16)
        onesf = sb("onesf", [1, HD], F32)
        onescol_b = sb("onescol_b", [HD, 1], BF16)
        lng_s = sb("lng_s", [HD, NH], F32)
        lnb_s = sb("lnb_s", [HD, NH], F32)
        pst = sb("pst", [HD, 4], F32)
        eps_c = sb("eps_c", [HD, 1], F32)
        sq_b = sb("sq_b", [HD, 512], BF16)
        scr512 = sb("scr512", [HD, 512], F32)

        R0h = nc.alloc_sbuf_tensor("R0", [HD, 32768], F32)
        R1h = nc.alloc_sbuf_tensor("R1", [HD, 16384], F32)
        A0 = nc.lookup_mloc(R0h).addr
        A1 = nc.lookup_mloc(R1h).addr

        def carve(name, shape, dtp, base, off):
            return nc.alloc_sbuf_tensor_at(name, shape, dtp, offset=base + off)

        wo_b = carve("wo_b", [HD, NH, D], BF16, A0, 0)
        mu_b = carve("mu_b", [HD, D], BF16, A0, 65536)
        rs_b = carve("rs_b", [HD, D], BF16, A0, 69632)
        qu_t = carve("qu_t", [HD, 2, D], BF16, A0, 81920)
        x2_t = carve("x2_t", [HD, D], F32, A0, 90112)
        x3_st = carve("x3_st", [HD, D], F32, A0, 98304)
        n_scr = carve("n_scr", [HD, D], F32, A0, 106496)
        s1_scr = carve("s1_scr", [HD, D], F32, A0, 114688)
        kvA = carve("kvA", [HD, NH * HD], F32, A0, 98304)
        kv_b = carve("kv_b", [HD, NH * HD], BF16, A0, 122880)
        bo_b = carve("bo_b", [1, D], BF16, A0, 126976)
        s_sum = carve("s_sum", [1, D], F32, A0, 0)
        s_sq = carve("s_sq", [1, D], F32, A0, 8192)
        mu_row = s_sum
        rs_row = s_sq
        kvB = carve("kvB", [HD, NH * HD], F32, A0, 106496)
        ocm = carve("ocm", [HD, NH, D], BF16, A1, 0)
        x3T = carve("x3T", [HD, NH, D], BF16, A1, 0)
        hidden = carve("hidden", [HD, NH, 1024], BF16, A0, 0)
        w2h_hb = carve("w2h_hb", [HD, NH, D], BF16, A0, 32768)
        w13_t = carve("w13_t", [HD, 2, 2, NH, HD], BF16, A0, 98304)
        fstage = carve("fstage", [HD, 2, D], F32, A0, 114688)


        pp = Pool([nc.alloc_psum_tensor(f"pp{i}", [HD, 512], F32) for i in range(4)])
        ptf = Pool([nc.alloc_psum_tensor(f"ptf{i}", [HD, HD], F32) for i in range(2)])
        ps_sum = nc.alloc_psum_tensor("ps_sum", [1, 512], F32)
        ps_sq = nc.alloc_psum_tensor("ps_sq", [1, 512], F32)

        def W(key, sem, val=None):
            v = sem.v if val is None else val
            P.add(key, lambda e, h=sem.h, v=v: e.wait_ge(h, v))

        def WH(key, hv_list):
            for (h, v) in hv_list:
                P.add(key, lambda e, h=h, v=v: e.wait_ge(h, v))

        def dma(lane, key, out, in_, accum=False):
            lane.v += 16
            def f(e, out=out, in_=in_, accum=accum, h=lane.h):
                if accum:
                    e.dma_start(out=out, in_=in_, accum_op=ALU.add).then_inc(h, 16)
                else:
                    e.dma_start(out=out, in_=in_).then_inc(h, 16)
            P.add(key, f)
            return lane.v

        def SC(fn):
            es.v += 1
            P.add("c", lambda e, fn=fn, h=es.h: fn(e).then_inc(h, 1))
            return es.v

        def VE(fn):
            ve.v += 1
            P.add("v", lambda e, fn=fn, h=ve.h: fn(e).then_inc(h, 1))
            return ve.v

        def TE(fn):
            mm.v += 1
            P.add("t", lambda e, fn=fn, h=mm.h: fn(e).then_inc(h, 1))
            return mm.v

        def TN(fn):
            P.add("t", fn)

        def GP(fn):
            gp.v += 1
            P.add("g", lambda e, fn=fn, h=gp.h: fn(e).then_inc(h, 1))
            return gp.v

        dma(Lc, "g", idf[:, :], ident[:, :])
        dma(Lc, "g", onesb[:, :], ones_in[:, :])
        dma(Lc, "g", onesf[:, :], ones_in[:, :])
        dma(Lc, "g", onescol_b[:, :], ones_col[:, :])
        dma(Lc, "g", bo_b[:, :], bo_row[:, :])
        dma(Lc, "g", lng_s[:, :], lng[:, :])
        c0 = dma(Lc, "g", lnb_s[:, :], lnb[:, :])
        VE(lambda v: v.memset(eps_c[:, :], EPS))
        v1 = dma(Lkv, "s", kvA[:, :], kva.ap()[0])
        v2 = dma(Lkv, "s", kvB[:, :], kva.ap()[1])
        P.add("v", lambda e, h=Lkv.h, v=v2: e.wait_ge(h, v))
        a1 = VE(lambda v: v.tensor_add(kvA[:, :], kvA[:, :], kvB[:, :]))
        P.add("s", lambda e, h=ve.h, v=a1: e.wait_ge(h, v))
        v3 = dma(Lkv, "s", kvB[:, :], kva.ap()[2])
        P.add("v", lambda e, h=Lkv.h, v=v3: e.wait_ge(h, v))
        a2 = VE(lambda v: v.tensor_add(kvA[:, :], kvA[:, :], kvB[:, :]))
        P.add("s", lambda e, h=ve.h, v=a2: e.wait_ge(h, v))
        v4 = dma(Lkv, "s", kvB[:, :], kva.ap()[3])
        P.add("v", lambda e, h=Lkv.h, v=v4: e.wait_ge(h, v))
        VE(lambda v: v.tensor_add(kvA[:, :], kvA[:, :], kvB[:, :]))
        W("v", ve)
        VE(lambda v: v.tensor_copy(kv_b[:, :], kvA[:, :]))
        kvb_done = ve.v
        P.add("t", lambda e, h=Lc.h, v=Lc.v: e.wait_ge(h, v))

        qg = [[], []]
        for ht in range(NH):
            qbi = ht % 2
            WH("s", qg[qbi])
            lv = dma(Lq[qbi], "s", qu_t[:, qbi, :], qin[ht * HD:(ht + 1) * HD, :])
            for tc in range(4):
                tsl = slice(tc * 512, (tc + 1) * 512)
                idx, ptile, gu = pp.next()
                WH("t", gu)
                W("t", Lq[qbi], lv)
                if ht == 0 and tc == 0:
                    W("t", ve, kvb_done)
                mv = TE(lambda t, pt=ptile, ht=ht, qbi=qbi, tsl=tsl:
                        t.matmul(pt[:, :], kv_b[:, ht * HD:(ht + 1) * HD],
                                 qu_t[:, qbi, tsl], start=True, stop=True))
                P.add("v", lambda e, h=mm.h, v=mv: e.wait_ge(h, v))
                evv = VE(lambda v, pt=ptile, ht=ht, tsl=tsl:
                         v.tensor_copy(ocm[:, ht, tsl], pt[:, :]))
                pp.guard(idx, ve, evv)
            qg[qbi] = [(mm.h, mm.v)]
        ocm_done = ve.v

        sq_mm_prev = None
        stats_evac = None
        for tc in range(4):
            tsl = slice(tc * 512, (tc + 1) * 512)
            if stats_evac is not None:
                P.add("t", lambda e, h=ve.h, v=stats_evac: e.wait_ge(h, v))
            for ht in range(NH):
                P.add("c", lambda e, h=ve.h, v=ocm_done: e.wait_ge(h, v))
                if sq_mm_prev is not None:
                    P.add("c", lambda e, h=mm.h, v=sq_mm_prev: e.wait_ge(h, v))
                sv = SC(lambda s, ht=ht, tsl=tsl:
                        s.activation(sq_b[:, :], ocm[:, ht, tsl], AF.Square))
                if ht == 0 and tc == 0:
                    P.add("t", lambda e, h=ve.h, v=ocm_done: e.wait_ge(h, v))
                TN(lambda t, ht=ht, tsl=tsl, first=(ht == 0):
                   t.matmul(ps_sum[:, :], onescol_b[:, :], ocm[:, ht, tsl],
                            start=first, stop=(ht == NH - 1)))
                P.add("t", lambda e, h=es.h, v=sv: e.wait_ge(h, v))
                mv = TE(lambda t, ht=ht, first=(ht == 0):
                        t.matmul(ps_sq[:, :], onescol_b[:, :], sq_b[:, :],
                                 start=first, stop=(ht == NH - 1)))
                sq_mm_prev = mv
            smv = sq_mm_prev
            P.add("v", lambda e, h=mm.h, v=smv: e.wait_ge(h, v))
            VE(lambda v, tsl=tsl: v.tensor_copy(s_sum[0:1, tsl], ps_sum[0:1, :]))
            W("v", ve)
            VE(lambda v, tsl=tsl: v.tensor_copy(s_sq[0:1, tsl], ps_sq[0:1, :]))
            W("v", ve)
            stats_evac = ve.v
        VE(lambda v: v.tensor_scalar_mul(mu_row[:, :], s_sum[:, :], 1.0 / D))
        W("v", ve)
        VE(lambda v: v.tensor_scalar_mul(rs_row[:, :], s_sq[:, :], 1.0 / D))
        W("v", ve)
        VE(lambda v: v.tensor_mul(n_scr[0:1, :], mu_row[:, :], mu_row[:, :]))
        W("v", ve)
        VE(lambda v: v.tensor_sub(rs_row[:, :], rs_row[:, :], n_scr[0:1, :]))
        W("c", ve)
        SC(lambda s: s.activation(rs_row[:, :], rs_row[:, :], AF.Ln,
                                  bias=eps_c[0:1, 0:1]))
        W("c", es)
        SC(lambda s: s.activation(rs_row[:, :], rs_row[:, :], AF.Exp, scale=-0.5))
        rows_done = es.v
        P.add("v", lambda e, h=es.h, v=rows_done: e.wait_ge(h, v))
        ve.v += 0
        for tc in range(4):
            tsl = slice(tc * 512, (tc + 1) * 512)
            for (rowt, dstt) in ((mu_row, mu_b), (rs_row, rs_b)):
                idx, ptile, gu = pp.next()
                WH("t", gu)
                P.add("t", lambda e, h=es.h, v=rows_done: e.wait_ge(h, v))
                mv = TE(lambda t, pt=ptile, rowt=rowt, tsl=tsl:
                        t.matmul(pt[:, :], onesf[0:1, :], rowt[0:1, tsl],
                                 start=True, stop=True))
                P.add("v", lambda e, h=mm.h, v=mv: e.wait_ge(h, v))
                evv = VE(lambda v, pt=ptile, dstt=dstt, tsl=tsl:
                         v.tensor_copy(dstt[:, tsl], pt[:, :]))
                pp.guard(idx, ve, evv)
        bcast_done = ve.v
        P.add("g", lambda e, h=mm.h, v=mm.v: e.wait_ge(h, v))
        wol = dma(Lwo, "g", wo_b[:, :, :], wo.ap().rearrange("(t p) d -> p t d", p=HD))

        ug2 = [list(qg[0]), list(qg[1])]
        nscb = [carve("nscb0", [HD, D], BF16, A0, 114688),
                carve("nscb1", [HD, D], BF16, A0, 118784)]
        nsc = nscb
        nsc_guard = [[], []]
        prev_fv = []
        for ht in range(NH):
            ubi = ht % 2
            WH("s", ug2[ubi])
            lv = dma(Lq[ubi], "s", qu_t[:, ubi, :], uin[ht * HD:(ht + 1) * HD, :])
            if ht == 0:
                P.add("g", lambda e, h=ve.h, v=bcast_done: e.wait_ge(h, v))
            WH("g", nsc_guard[ubi])
            GP(lambda g, ht=ht, ubi=ubi:
               g.tensor_sub(nsc[ubi][:, :], ocm[:, ht, :], mu_b[:, :]))
            P.add("g", lambda e, h=gp.h, v=gp.v: e.wait_ge(h, v))
            gsub = GP(lambda g, ubi=ubi:
                      g.tensor_mul(nsc[ubi][:, :], nsc[ubi][:, :], rs_b[:, :]))
            P.add("v", lambda e, h=gp.h, v=gsub: e.wait_ge(h, v))
            P.add("v", lambda e, h=Lq[ubi].h, v=lv: e.wait_ge(h, v))
            W("v", ve)
            VE(lambda v, ht=ht, ubi=ubi:
               v.tensor_scalar(nsc[ubi][:, :], nsc[ubi][:, :], lng_s[:, ht:ht + 1],
                               lnb_s[:, ht:ht + 1], ALU.mult, ALU.add))
            W("v", ve)
            fv = VE(lambda v, ht=ht, ubi=ubi:
                    v.tensor_mul(ocm[:, ht, :], qu_t[:, ubi, :], nsc[ubi][:, :]))
            nsc_guard[ubi] = [(ve.h, fv)]
            ug2[ubi] = [(ve.h, fv)]
        scm_done = ve.v

        x2g = []
        prev_ln = None
        for tt in range(NT):
            WH("s", x2g)
            xlv = dma(Lx2, "s", x2_t[:, :], x2i[tt * HD:(tt + 1) * HD, :])
            dn = None
            for dc in range(4):
                dsl = slice(dc * 512, (dc + 1) * 512)
                idx, ptile, gu = pp.next()
                WH("t", gu)
                if tt == 0 and dc == 0:
                    W("t", ve, scm_done)
                    W("t", Lwo, wol)
                for ht in range(NH):
                    TN(lambda t, pt=ptile, ht=ht, tt=tt, dsl=dsl, first=(ht == 0):
                       t.matmul(pt[:, :], ocm[:, ht, tt * HD:(tt + 1) * HD],
                                wo_b[:, ht, dsl], start=first, stop=False))
                mv = TE(lambda t, pt=ptile, dsl=dsl:
                        t.matmul(pt[:, :], onesb[0:1, :], bo_b[0:1, dsl],
                                 start=False, stop=True))
                dn = mv
                P.add("v", lambda e, h=mm.h, v=mv: e.wait_ge(h, v))
                P.add("v", lambda e, h=Lx2.h, v=xlv: e.wait_ge(h, v))
                if prev_ln is not None and dc == 0:
                    P.add("v", lambda e, h=Lsp.h, v=prev_ln: e.wait_ge(h, v))
                evv = VE(lambda v, pt=ptile, dsl=dsl:
                         v.tensor_add(x3_st[:, dsl], pt[:, :], x2_t[:, dsl]))
                pp.guard(idx, ve, evv)
            x2g = [(ve.h, ve.v)]
            W("v", ve)
            VE(lambda v: v.reduce_sum(pst[:, 0:1], x3_st[:, :], axis=AX.X))
            W("c", ve)
            W("c", es)
            SC(lambda s: s.activation(n_scr[:, :], x3_st[:, :], AF.Square,
                                      accum_out=pst[:, 1:2]))
            W("v", es)
            VE(lambda v: v.tensor_scalar_mul(pst[:, 0:1], pst[:, 0:1], 1.0 / D))
            W("v", ve)
            VE(lambda v: v.tensor_scalar_mul(pst[:, 1:2], pst[:, 1:2], 1.0 / D))
            W("v", ve)
            VE(lambda v: v.tensor_mul(pst[:, 2:3], pst[:, 0:1], pst[:, 0:1]))
            W("v", ve)
            VE(lambda v: v.tensor_sub(pst[:, 1:2], pst[:, 1:2], pst[:, 2:3]))
            W("c", ve)
            SC(lambda s: s.activation(pst[:, 1:2], pst[:, 1:2], AF.Sqrt,
                                      bias=eps_c[:, 0:1]))
            W("v", es)
            VE(lambda v: v.reciprocal(pst[:, 1:2], pst[:, 1:2]))
            W("v", ve)
            lnv = VE(lambda v: v.tensor_scalar(x3_st[:, :], x3_st[:, :], pst[:, 0:1],
                                               pst[:, 1:2], ALU.subtract, ALU.mult))
            P.add("s", lambda e, h=ve.h, v=lnv: e.wait_ge(h, v))
            prev_ln = dma(Lsp, "s", x3f[tt * HD:(tt + 1) * HD, :], x3_st[:, :])

        x3_all = Lsp.v
        x3g = []
        for tt in range(NT):
            WH("s", x3g)
            if tt == 0:
                P.add("s", lambda e, h=Lsp.h, v=x3_all: e.wait_ge(h, v))
            rlv = dma(Lx3r, "s", x3_st[:, :], x3f[tt * HD:(tt + 1) * HD, :])
            trl = None
            for dtk in range(NH):
                idx, ptile, gu = ptf.next()
                P.add("t", lambda e, h=Lx3r.h, v=rlv: e.wait_ge(h, v))
                if tt == 0 and dtk == 0:
                    W("t", ve, scm_done)
                WH("t", gu)
                tv = TE(lambda t, pt=ptile, dtk=dtk:
                        t.transpose(pt[:, :], x3_st[:, dtk * HD:(dtk + 1) * HD],
                                    idf[:, :]))
                trl = tv
                P.add("v", lambda e, h=mm.h, v=tv: e.wait_ge(h, v))
                evv = VE(lambda v, pt=ptile, tt=tt, dtk=dtk:
                         v.tensor_copy(x3T[:, dtk, tt * HD:(tt + 1) * HD], pt[:, :]))
                ptf.guard(idx, ve, evv)
            x3g = [(mm.h, trl)]
        x3T_done = ve.v

        P.add("g", lambda e, h=ve.h, v=ve.v: e.wait_ge(h, v))
        P.add("g", lambda e, h=es.h, v=es.v: e.wait_ge(h, v))
        P.add("g", lambda e, h=mm.h, v=mm.v: e.wait_ge(h, v))
        w13g = [[], []]
        prev_hv6 = None
        ffa_guard = [[], []]
        w2hr_guard = []
        prev_hb_acc = None
        first6 = True
        def w13_load(hidx, wbi):
            WH("g", w13g[wbi])
            dma(Lw13[wbi], "g", w13_t[:, wbi, 0, :, :],
                w1t.ap().rearrange("(t p) h -> p t h", p=HD)
                [:, :, hidx * HD:(hidx + 1) * HD])
            return dma(Lw13[wbi], "g", w13_t[:, wbi, 1, :, :],
                       w3t.ap().rearrange("(t p) h -> p t h", p=HD)
                       [:, :, hidx * HD:(hidx + 1) * HD])

        for tc2 in range(2):
            tbase = tc2 * 1024
            for hb in range(4):
                w13_lv = {0: w13_load(hb * NH + 0, 0), 1: w13_load(hb * NH + 1, 1)}
                WH("g", w2hr_guard)
                w2l = None
                for ht8 in range(NH):
                    hidx = hb * NH + ht8
                    w2l = dma(Lw2hr, "g", w2h_hb[:, ht8, :],
                              w2h.ap()[hidx * HD:(hidx + 1) * HD, :])
                up_last = None
                for ht8 in range(NH):
                    hidx = hb * NH + ht8
                    wbi = ht8 % 2
                    lv = w13_lv[ht8]
                    for th in range(2):
                        tsl = slice(tbase + th * 512, tbase + (th + 1) * 512)
                        i1, p1, g1 = pp.next()
                        i2, p2, g2 = pp.next()
                        WH("t", g1)
                        WH("t", g2)
                        W("t", Lw13[wbi], lv)
                        if first6:
                            W("t", ve, x3T_done)
                            first6 = False
                        for dtk in range(NH):
                            TN(lambda t, p1=p1, wbi=wbi, dtk=dtk, tsl=tsl:
                               t.matmul(p1[:, :], w13_t[:, wbi, 0, dtk, :],
                                        x3T[:, dtk, tsl], start=(dtk == 0),
                                        stop=(dtk == NH - 1)))
                        for dtk in range(NH):
                            if dtk < NH - 1:
                                TN(lambda t, p2=p2, wbi=wbi, dtk=dtk, tsl=tsl:
                                   t.matmul(p2[:, :], w13_t[:, wbi, 1, dtk, :],
                                            x3T[:, dtk, tsl], start=(dtk == 0), stop=False))
                            else:
                                mv = TE(lambda t, p2=p2, wbi=wbi, dtk=dtk, tsl=tsl:
                                        t.matmul(p2[:, :], w13_t[:, wbi, 1, dtk, :],
                                                 x3T[:, dtk, tsl], start=False, stop=True))
                        up_last = mv
                        P.add("c", lambda e, h=mm.h, v=mv: e.wait_ge(h, v))
                        W("c", es)
                        if prev_hv6 is not None:
                            P.add("c", lambda e, h=ve.h, v=prev_hv6: e.wait_ge(h, v))
                        s1v = SC(lambda s, p1=p1: s.activation(scr512[:, :], p1[:, :], AF.Silu))
                        pp.guard(i1, es, s1v)
                        P.add("v", lambda e, h=es.h, v=s1v: e.wait_ge(h, v))
                        hv = VE(lambda v, p2=p2, ht8=ht8, th=th:
                                v.tensor_mul(hidden[:, ht8, th * 512:(th + 1) * 512],
                                             scr512[:, :], p2[:, :]))
                        pp.guard(i2, ve, hv)
                        prev_hv6 = hv
                    w13g[wbi] = [(mm.h, up_last)]
                    if ht8 + 2 <= NH - 1:
                        w13_lv[ht8 + 2] = w13_load(hb * NH + ht8 + 2, (ht8 + 2) % 2)
                hid_done = ve.v
                dn_last = None
                for tt2 in range(8):
                    sbi = tt2 % 2
                    for dc in range(4):
                        dsl = slice(dc * 512, (dc + 1) * 512)
                        idx, ptile, gu = pp.next()
                        WH("t", gu)
                        if tt2 == 0 and dc == 0:
                            W("t", ve, hid_done)
                            W("t", Lw2hr, w2l)
                        for ht8 in range(NH):
                            if ht8 < NH - 1:
                                TN(lambda t, pt=ptile, ht8=ht8, tt2=tt2, dsl=dsl:
                                   t.matmul(pt[:, :],
                                            hidden[:, ht8, tt2 * HD:(tt2 + 1) * HD],
                                            w2h_hb[:, ht8, dsl], start=(ht8 == 0),
                                            stop=False))
                            else:
                                mv = TE(lambda t, pt=ptile, ht8=ht8, tt2=tt2, dsl=dsl:
                                        t.matmul(pt[:, :],
                                                 hidden[:, ht8, tt2 * HD:(tt2 + 1) * HD],
                                                 w2h_hb[:, ht8, dsl], start=False,
                                                 stop=True))
                        dn_last = mv
                        P.add("v", lambda e, h=mm.h, v=mv: e.wait_ge(h, v))
                        WH("v", ffa_guard[sbi] if dc == 0 else [])
                        evv = VE(lambda v, pt=ptile, sbi=sbi, dsl=dsl:
                                 v.tensor_copy(fstage[:, sbi, dsl], pt[:, :]))
                        pp.guard(idx, ve, evv)
                    P.add("g", lambda e, h=ve.h, v=ve.v: e.wait_ge(h, v))
                    if prev_hb_acc is not None and hb > 0:
                        WH("g", prev_hb_acc)
                    row = tc2 * 1024 + tt2 * HD
                    av = dma(Lffa[sbi], "g", ffd[row:row + HD, :], fstage[:, sbi, :],
                             accum=(hb > 0))
                    ffa_guard[sbi] = [(Lffa[sbi].h, av)]
                prev_hb_acc = [(Lffa[0].h, Lffa[0].v), (Lffa[1].h, Lffa[1].v)]
                w2hr_guard = [(mm.h, dn_last)]
            prev_hb_acc = None

        fst2 = carve("fst2", [HD, 2, D], F32, A0, 0)
        x3s2 = carve("x3s2", [HD, D], F32, A0, 32768)
        pst2 = sb("pst2", [HD, 4], F32)
        ffd_done = [(Lffa[0].h, Lffa[0].v), (Lffa[1].h, Lffa[1].v)]
        Lr2 = [Sem(ctx, nc, "Lr2_0"), Sem(ctx, nc, "Lr2_1")]
        Lo2 = [Lout, Sem(ctx, nc, "LoutB")]
        tile_guard = [[], []]
        p7io = {}

        last_add = {0: None, 1: None}
        last_out = {0: None, 1: None}

        def p7_load(tt2):
            if tt2 >= NT:
                return
            pi2 = tt2 % 2
            fs2 = fstage if pi2 == 0 else fst2
            if last_add[pi2] is not None:
                P.add("g", lambda e, h=gp.h, v=last_add[pi2]: e.wait_ge(h, v))
            a1 = dma(Lrl[pi2], "g", fs2[:, 0, :], ffd[tt2 * HD:(tt2 + 1) * HD, :])
            a2 = dma(Lr2[pi2], "g", fs2[:, 1, :], x3f[tt2 * HD:(tt2 + 1) * HD, :])
            p7io[tt2] = (a1, a2)

        for tt in range(NT):
            pi = tt % 2
            fs = fstage if pi == 0 else fst2
            xs = x3_st if pi == 0 else x3s2
            ps_ = pst if pi == 0 else pst2
            if tt == 0:
                P.add("g", lambda e, h=mm.h, v=mm.v: e.wait_ge(h, v))
                WH("g", ffd_done)
                p7_load(0)
                p7_load(1)
            f1, f2 = p7io[tt]
            P.add("g", lambda e, h=Lrl[pi].h, v=f1: e.wait_ge(h, v))
            P.add("g", lambda e, h=Lr2[pi].h, v=f2: e.wait_ge(h, v))
            if last_out[pi] is not None:
                P.add("g", lambda e, h=Lo2[pi].h, v=last_out[pi]: e.wait_ge(h, v))
            addv = GP(lambda g, fs=fs, xs=xs:
                      g.tensor_add(xs[:, :], fs[:, 0, :], fs[:, 1, :]))
            last_add[pi] = addv
            p7_load(tt + 2)
            P.add("v", lambda e, h=gp.h, v=addv: e.wait_ge(h, v))
            W("v", ve)
            VE(lambda v, xs=xs, ps_=ps_:
               v.reduce_sum(ps_[:, 0:1], xs[:, :], axis=AX.X))
            P.add("c", lambda e, h=gp.h, v=addv: e.wait_ge(h, v))
            W("c", es)
            sqv = SC(lambda s, xs=xs, ps_=ps_:
                     s.activation(n_scr[:, :], xs[:, :], AF.Square,
                                  accum_out=ps_[:, 1:2]))
            P.add("v", lambda e, h=es.h, v=sqv: e.wait_ge(h, v))
            W("v", ve)
            VE(lambda v, ps_=ps_:
               v.tensor_scalar(ps_[:, 0:2], ps_[:, 0:2], 1.0 / D, None, ALU.mult))
            W("v", ve)
            VE(lambda v, ps_=ps_:
               v.tensor_mul(ps_[:, 2:3], ps_[:, 0:1], ps_[:, 0:1]))
            W("v", ve)
            VE(lambda v, ps_=ps_:
               v.tensor_sub(ps_[:, 1:2], ps_[:, 1:2], ps_[:, 2:3]))
            W("c", ve)
            SC(lambda s, ps_=ps_:
               s.activation(ps_[:, 1:2], ps_[:, 1:2], AF.Sqrt, bias=eps_c[:, 0:1]))
            W("v", es)
            VE(lambda v, ps_=ps_: v.reciprocal(ps_[:, 1:2], ps_[:, 1:2]))
            W("v", ve)
            apv = VE(lambda v, xs=xs, ps_=ps_:
                     v.tensor_scalar(xs[:, :], xs[:, :], ps_[:, 0:1], ps_[:, 1:2],
                                     ALU.subtract, ALU.mult))
            P.add("s", lambda e, h=ve.h, v=apv: e.wait_ge(h, v))
            ov = dma(Lo2[pi], "s", out_o[tt * HD:(tt + 1) * HD, :], xs[:, :])
            last_out[pi] = ov
        for lane in (Lout, Lo2[1], Lrl[0], Lrl[1], Lr2[0], Lr2[1], Lsp, Lx3r,
                     Lx2, Lq[0], Lq[1], Lkv):
            W("s", lane)
        for lane in (Lffa[0], Lffa[1], Lw13[0], Lw13[1], Lw2hr, Lwo, Lc):
            W("g", lane)

        with nc.Block() as blk:
            @blk.gpsimd
            def _(g):
                for f in P.ops["g"]:
                    f(g)

            @blk.sync
            def _(s):
                for f in P.ops["s"]:
                    f(s)

            @blk.tensor
            def _(t):
                for f in P.ops["t"]:
                    f(t)

            @blk.vector
            def _(v):
                for f in P.ops["v"]:
                    f(v)

            @blk.scalar
            def _(c):
                for f in P.ops["c"]:
                    f(c)

    return nc


def _sylvester(k):
    H = np.array([[1.0]], np.float32)
    while H.shape[0] < k:
        H = np.block([[H, H], [H, -H]]).astype(np.float32)
    return H


def prep_launch2_inputs(inp, res1):
    f32 = np.float32
    wo_ = np.ascontiguousarray(np.asarray(inp["L1_attn_Wo"], f32))
    bo_ = np.asarray(inp["L1_attn_bo"], f32).reshape(1, D)
    lng_ = np.ascontiguousarray(np.asarray(inp["L1_attn_ln_g"], f32).reshape(NH, HD).T)
    lnb_ = np.ascontiguousarray(np.asarray(inp["L1_attn_ln_b"], f32).reshape(NH, HD).T)
    w1t_ = np.ascontiguousarray(np.asarray(inp["L1_ffn_W1"], f32).T)
    w3t_ = np.ascontiguousarray(np.asarray(inp["L1_ffn_W3"], f32).T)
    ident = np.eye(HD, dtype=f32)
    ones = np.ones((1, HD), f32)
    onesc = np.ones((HD, 1), f32)
    maps = []
    for c in range(NC):
        g0 = (c // 4) * 4
        kva_ = np.stack([np.asarray(res1[g0 + j]["kv_out"]) for j in range(4)])
        maps.append({
            "x2i": np.asarray(res1[c]["x2_out"]),
            "qin": np.asarray(res1[c]["q_out"]),
            "uin": np.asarray(res1[c]["u_out"]),
            "kva": kva_, "wo": wo_, "bo_row": bo_, "lng": lng_, "lnb": lnb_,
            "w1t": w1t_, "w3t": w3t_, "w2h": np.asarray(res1[c]["w2h_out"]),
            "ident": ident, "ones_in": ones, "ones_col": onesc,
        })
    return maps


_BUILD_CACHE = {}


def kernel(**inputs):
    if "l1" not in _BUILD_CACHE:
        _BUILD_CACHE["l1"] = build_launch1()
        _BUILD_CACHE["l2"] = build_launch2()
    nc1, nc2 = _BUILD_CACHE["l1"], _BUILD_CACHE["l2"]
    maps1 = prep_launch1_inputs(inputs)
    res1 = run_bass_kernel_spmd(nc1, maps1, core_ids=list(range(NC))).results
    maps2 = prep_launch2_inputs(inputs, res1)
    res2 = run_bass_kernel_spmd(nc2, maps2, core_ids=list(range(NC))).results
    halves = []
    for b in range(B):
        halves.append(np.concatenate(
            [np.asarray(res2[b * 4 + q]["out_o"]) for q in range(4)], axis=0))
    return np.stack(halves, axis=0)
